# version 02
# baseline (speedup 1.0000x reference)
import math
import contextlib
import numpy as np
import concourse.bass as bass
import concourse.mybir as mybir
from concourse.alu_op_type import AluOpType as OP
from concourse.bass_utils import run_bass_kernel_spmd

F32 = mybir.dt.float32
BF16 = mybir.dt.bfloat16
AF = mybir.ActivationFunctionType
AX = mybir.AxisListType

NCORES = 8
D = 4096
DC = 32
TPC = 1024
TT = 8
KV = 1024
H = 8
GF = 4
HD = 128
SL = 4096
NR = 4
SCALE = 1.0 / math.sqrt(HD)
EPS = 1e-5
NE = 8
RK = 16
LSCALE = 2.0

ENG = ("tensor", "vector", "scalar", "sync", "gpsimd")

_DEBUG_TAPS = []


class Rec:

    def __init__(self):
        self.ops = []
        self.wl = {}
        self.r = {}
        self.spans = {}
        self.touched = set()

    def span(self, res, lo, hi):
        self.spans[res] = (lo, hi)

    def retouch(self, res):
        self.touched.discard(res)

    def op(self, eng, meth, *args, reads=(), writes=(), accum=False, dma=None,
           cc=False, **kwargs):
        i = len(self.ops)
        deps = set()
        for res in writes:
            if res not in self.touched and res in self.spans:
                lo, hi = self.spans[res]
                for other, (lo2, hi2) in self.spans.items():
                    if other == res or hi2 <= lo or hi <= lo2:
                        continue
                    deps.update(self.wl.get(other, ()))
                    deps.update(self.r.get(other, ()))
                self.touched.add(res)
        for res in reads:
            deps.update(self.wl.get(res, ()))
        if not accum:
            for res in writes:
                deps.update(self.wl.get(res, ()))
                deps.update(self.r.get(res, ()))
        for res in writes:
            if accum:
                self.wl.setdefault(res, []).append(i)
            else:
                self.wl[res] = [i]
                self.r[res] = []
        for res in reads:
            self.r.setdefault(res, []).append(i)
        self.ops.append(dict(i=i, eng=eng, meth=meth, args=args, kwargs=kwargs,
                             deps=deps, dma=dma, cc=cc, need=False, sem=None,
                             val=None))
        return i

    def final_wait(self, dma_name):
        self.ops.append(dict(i=len(self.ops), eng="sync", meth="_final_wait",
                             args=(dma_name,), kwargs={}, deps=set(), dma=None,
                             cc=False, need=False, sem=None, val=None))

    def materialize(self):
        for o in self.ops:
            best = {}
            for d in o["deps"]:
                p = self.ops[d]
                key = ("dma", p["dma"]) if p["dma"] is not None else ("eng", p["eng"])
                if key not in best or d > best[key]:
                    best[key] = d
            o["deps"] = set(best.values())
        for o in self.ops:
            for d in o["deps"]:
                self.ops[d]["need"] = True
        engct = {e: 0 for e in ENG}
        dmact = {}
        self.dma_sems = set()
        for o in self.ops:
            if o["dma"] is not None:
                step = 1 if o["cc"] else 16
                dmact[o["dma"]] = dmact.get(o["dma"], 0) + step
                o["sem"], o["val"] = ("dma", o["dma"]), dmact[o["dma"]]
                self.dma_sems.add(o["dma"])
            elif o["need"]:
                engct[o["eng"]] += 1
                o["sem"], o["val"] = ("eng", o["eng"]), engct[o["eng"]]
        hw = {e: {} for e in ENG}
        for o in self.ops:
            if o["meth"] == "_final_wait":
                o["waits"] = [(("dma", o["args"][0]), dmact[o["args"][0]])]
                continue
            waits = []
            for d in sorted(o["deps"]):
                p = self.ops[d]
                if p["sem"] is None:
                    continue
                key = p["sem"]
                cur = hw[o["eng"]].get(key, 0)
                if p["val"] > cur:
                    waits.append((key, p["val"]))
                    hw[o["eng"]][key] = p["val"]
            o["waits"] = waits

    def replay(self, engobj, eng, esems, dsems):
        for o in self.ops:
            if o["eng"] != eng:
                continue
            for (kind, name), v in o.get("waits", ()):
                engobj.wait_ge(dsems[name] if kind == "dma" else esems[name], v)
            if o["meth"] == "_final_wait":
                continue
            inst = getattr(engobj, o["meth"])(*o["args"], **o["kwargs"])
            if o["sem"] is not None:
                if o["sem"][0] == "dma":
                    inst.then_inc(dsems[o["sem"][1]], 1 if o["cc"] else 16)
                else:
                    inst.then_inc(esems[eng], 1)


def build_nc(taps=()):
    nc = bass.Bass("TRN2", target_bir_lowering=False, debug=False)

    xd = nc.dram_tensor("x", [TPC, D], F32, kind="ExternalInput").ap()
    wq = nc.dram_tensor("wq", [D, D], F32, kind="ExternalInput").ap()
    wk = nc.dram_tensor("wk", [D, KV], F32, kind="ExternalInput").ap()
    wv = nc.dram_tensor("wv", [D, KV], F32, kind="ExternalInput").ap()
    wo = nc.dram_tensor("wo", [D, D], F32, kind="ExternalInput").ap()
    wqb = nc.dram_tensor("wqb", [D], F32, kind="ExternalInput").ap()
    wkb = nc.dram_tensor("wkb", [KV], F32, kind="ExternalInput").ap()
    wvb = nc.dram_tensor("wvb", [KV], F32, kind="ExternalInput").ap()
    wob = nc.dram_tensor("wob", [D], F32, kind="ExternalInput").ap()
    wbt = nc.dram_tensor("wbt", [D, D], F32, kind="ExternalInput").ap()
    bb = nc.dram_tensor("bb", [D], F32, kind="ExternalInput").ap()
    wr = nc.dram_tensor("wr", [NE, D], F32, kind="ExternalInput").ap()
    at_ = nc.dram_tensor("at", [D, NE * RK], F32, kind="ExternalInput").ap()
    br_ = nc.dram_tensor("br", [NE * RK, D], F32, kind="ExternalInput").ap()
    outd = nc.dram_tensor("out", [TPC, D], F32, kind="ExternalOutput").ap()

    qsc = nc.dram_tensor("qscratch", [D, TPC], BF16).ap()
    gin = nc.dram_tensor("gin", [2048, KV], BF16).ap()
    gout = nc.dram_tensor("gout", [NR * 2048, KV], BF16).ap()
    lnasc = nc.dram_tensor("lnascratch", [TPC, D], BF16).ap()

    tapd = {}
    tap_shapes = {"xfm": [128, DC * TPC], "afm": [128, DC * TPC],
                  "x2fm": [128, DC * TPC], "attn": [128, TT * D],
                  "ffn": [128, TT * D], "pbuf": [128, 16384],
                  "qkv": [2048, KV]}
    for t in taps:
        tapd[t] = nc.dram_tensor("tap_" + t, tap_shapes[t], BF16,
                                 kind="ExternalOutput").ap()

    r = Rec()

    SB0 = 16512

    def sb(name, shape, dtype, off):
        t = nc.alloc_sbuf_tensor_at(name, shape, dtype, offset=SB0 + off)
        nbytes = 1
        for s in shape[1:]:
            nbytes *= s
        nbytes *= 2 if dtype == BF16 else 4
        r.span(name, off, off + nbytes)
        return t

    x_fm = sb("x_fm", [128, DC * TPC], BF16, 0)
    for _tb in range(TT):
        r.span(f"xfmt{_tb}", 0, 65536)
    a_fm = sb("a_fm", [128, DC * TPC], BF16, 0)
    x2_fm = sb("x2_fm", [128, DC * TPC], BF16, 0)
    ln1buf = sb("ln1buf", [128, D], BF16, 65536)
    wrtmp = sb("wrtmp", [NE, D], BF16, 65536 + 8192)
    pbuf = [sb(f"pbuf{i}", [128, 16384], BF16, 65536 + i * 32768) for i in range(2)]
    qsl = [sb(f"qsl{i}", [128, TPC], BF16, 164864 + i * 2048) for i in range(2)]
    sq8k = sb("sq8k", [128, D], BF16, 168960)
    knrow = sb("knrow", [1, 512], F32, 177152)
    attn_sb = sb("attn_sb", [128, TT * D], BF16, 65536)
    ffn_sb = sb("ffn_sb", [128, TT * D], BF16, 65536)
    xsl = [sb(f"xsl{i}", [128, D], F32, 131072 + i * 16384) for i in range(2)]
    qtg = [sb(f"qtg{i}", [128, 512], BF16, 131072 + i * 1024) for i in range(2)]
    ktg = [sb(f"ktg{i}", [128, 512], BF16, 133120 + i * 1024) for i in range(2)]
    vtg = [sb(f"vtg{i}", [128, 256], BF16, 135168 + i * 512) for i in range(2)]
    khb = [sb(f"khb{i}", [128, NR, 1024], BF16, 131072 + i * 8192) for i in range(2)]
    vhb = [sb(f"vhb{i}", [128, DC, 130], BF16, 147456 + i * 8320) for i in range(2)]
    wobrow = sb("wobrow", [1, D], BF16, 131072)
    x5 = sb("x5", [128, D], F32, 131072)
    ln2buf = sb("ln2buf", [128, D], BF16, 147456)
    atsb = sb("atsb", [128, DC, NE * RK], BF16, 131072)
    ballt = sb("ballt", [128, D], BF16, 139264)
    bbrow = sb("bbrow", [1, D], BF16, 147456)
    lna7 = sb("lna7", [128, D], BF16, 131072)
    x7 = sb("x7", [128, D], F32, 147456)
    wst = [sb(f"wst{i}", [128, DC, 256], BF16, 164864 + i * 16384) for i in range(2)]
    M = 197632

    def msb(name, shape, dtype):
        nonlocal M
        t = sb(name, shape, dtype, M)
        nbytes = 1
        for s in shape[1:]:
            nbytes *= s
        nbytes *= 2 if dtype == BF16 else 4
        M += (nbytes + 31) // 32 * 32
        return t

    identf = msb("identf", [128, 128], F32)
    identb = msb("identb", [128, 128], BF16)
    iot0 = msb("iot0", [128, 128], F32)
    iot1 = msb("iot1", [128, 128], F32)
    wqb_sb = msb("wqb_sb", [128, 32], F32)
    wkb_sb = msb("wkb_sb", [128, 8], F32)
    braw = msb("braw", [32, 128], F32)
    onescol = msb("onescol", [128, 1], BF16)
    onesrow = msb("onesrow", [1, 128], BF16)
    wvbrow = msb("wvbrow", [1, KV], BF16)
    wrt = msb("wrt", [128, DC * NE], BF16)
    biasq = msb("biasq", [128, 1], F32)
    bqbf = msb("bqbf", [1, 1], BF16)
    mk2 = msb("mk2", [1, 1], F32)
    mq2 = msb("mq2", [1, 1], F32)
    nsc = msb("nsc", [1, 1], F32)
    bnb = msb("bnb", [128, 8, 6], F32)
    bn2 = msb("bn2", [128, 2], F32)
    sdv = msb("sdv", [128, 1], F32)
    rstd = msb("rstd", [128, 1], F32)
    recip = msb("recip", [128, 1], F32)
    astg = msb("astg", [128, 128], BF16)
    hsb = msb("hsb", [128, 136], F32)
    gtm = msb("gtm", [128, 128], BF16)
    gfm = msb("gfm", [128, TT * 128], BF16)
    tm1 = msb("tm1", [128, 1], F32)
    tm2 = msb("tm2", [128, 1], F32)
    teq1 = msb("teq1", [128, NE], F32)
    teq2 = msb("teq2", [128, NE], F32)
    tlm = msb("tlm", [128, NE], F32)
    trw2 = msb("trw2", [128, 1], F32)
    trw1 = msb("trw1", [128, 1], F32)
    tt1 = msb("tt1", [128, NE], F32)
    twts = msb("twts", [128, NE], F32)
    assert SB0 + M <= 229344, M

    stack = contextlib.ExitStack()
    ps_a = stack.enter_context(nc.psum_tensor("ps_a", [128, 2048], F32))
    ps_b = stack.enter_context(nc.psum_tensor("ps_b", [128, 1024], F32))
    ps_tr = stack.enter_context(nc.psum_tensor("ps_tr", [128, 1024], BF16))
    ps_d = stack.enter_context(nc.psum_tensor("ps_d", [128, 512], F32))

    r.op("gpsimd", "iota", iot0[:, :], [[1, 128]], channel_multiplier=0,
         allow_small_or_imprecise_dtypes=True, writes=["iot0"])
    r.op("gpsimd", "iota", iot1[:, :], [[0, 128]], channel_multiplier=1,
         allow_small_or_imprecise_dtypes=True, writes=["iot1"])
    r.op("vector", "tensor_tensor", identf[:, :], iot0[:, :], iot1[:, :], OP.is_equal,
         reads=["iot0", "iot1"], writes=["identf"])
    r.op("vector", "tensor_copy", identb[:, :], identf[:, :],
         reads=["identf"], writes=["identb"])
    r.op("gpsimd", "memset", onescol[:, :], 1.0, writes=["onescol"])
    r.op("gpsimd", "memset", onesrow[:, :], 1.0, writes=["onesrow"])
    for i in range(2):
        r.span(f"vones{i}", *r.spans[f"vhb{i}"])
    r.op("sync", "dma_start", out=braw[:, :], in_=wqb.rearrange("(a p) -> a p", p=128),
         dma="dbias", writes=["braw"])
    r.op("tensor", "transpose", ps_d[:, 0:32], braw[:, :], identf[0:32, 0:32],
         reads=["braw", "identf"], writes=["pd"])
    r.op("vector", "tensor_copy", wqb_sb[:, :], ps_d[:, 0:32], reads=["pd"],
         writes=["wqb_sb"])
    r.op("sync", "dma_start", out=braw[0:8, :], in_=wkb.rearrange("(a p) -> a p", p=128),
         dma="dbias", reads=["pd"], writes=["braw"])
    r.op("tensor", "transpose", ps_d[:, 32:40], braw[0:8, :], identf[0:8, 0:8],
         reads=["braw", "identf"], writes=["pd"])
    r.op("vector", "tensor_copy", wkb_sb[:, :], ps_d[:, 32:40], reads=["pd"],
         writes=["wkb_sb"])
    r.op("gpsimd", "dma_start", out=wvbrow[0:1, :],
         in_=wvb.rearrange("(a k) -> a k", a=1), dma="dwvb", writes=["wvbrow"])
    r.op("gpsimd", "dma_start", out=wrtmp[:, :], in_=wr, dma="dwrt", writes=["wrtmp"])
    for dc in range(DC):
        r.op("tensor", "transpose", ps_tr[:, 0:8], wrtmp[:, dc * 128:(dc + 1) * 128],
             identb[0:8, 0:8], reads=["wrtmp", "identb"], writes=["ptr"])
        r.op("vector", "tensor_copy", wrt[:, dc * 8:(dc + 1) * 8], ps_tr[:, 0:8],
             reads=["ptr"], writes=["wrt"], accum=(dc > 0))

    def layernorm_stats(src_ap, src_res):
        for i in range(8):
            r.op("vector", "bn_stats", bnb[:, i, :], src_ap[:, i * 512:(i + 1) * 512],
                 reads=[src_res], writes=["bnb"], accum=(i > 0))
        r.op("vector", "bn_aggr", bn2[:, :], bnb[:, :, :], reads=["bnb"], writes=["bn2"])
        r.op("vector", "tensor_scalar", sdv[:, :], bn2[:, 1:2], EPS, None, OP.add,
             reads=["bn2"], writes=["sdv"])
        r.op("scalar", "activation", sdv[:, :], sdv[:, :], AF.Sqrt,
             reads=["sdv"], writes=["sdv"])
        r.op("vector", "reciprocal", rstd[:, :], sdv[:, :], reads=["sdv"], writes=["rstd"])

    def emit_stripe_load(wsrc, s):
        sl = s % 2
        src = wsrc.rearrange("(dc p) n -> p dc n", p=128)[:, :, s * 256:(s + 1) * 256]
        r.op("gpsimd", "dma_start", out=wst[sl][:, :, :], in_=src,
             dma=f"dw{sl}", writes=[f"wst{sl}"])
        return sl

    pctr = [0]

    def next_bank():
        b = pctr[0] % 4
        pctr[0] += 1
        return b

    for tb in range(TT):
        sl = tb % 2
        r.op("sync", "dma_start", out=xsl[sl][:, :], in_=xd[tb * 128:(tb + 1) * 128, :],
             dma=f"dx{sl}", writes=[f"xsl{sl}"])
        layernorm_stats(xsl[sl][:, :], f"xsl{sl}")
        r.op("vector", "tensor_scalar", ln1buf[:, :], xsl[sl][:, :], bn2[:, 0:1],
             rstd[:, 0:1], OP.subtract, OP.mult,
             reads=[f"xsl{sl}", "bn2", "rstd"], writes=["ln1buf"])
        for dc in range(DC):
            r.op("tensor", "transpose", ps_tr[:, 0:128],
                 ln1buf[:, dc * 128:(dc + 1) * 128], identb[:, :],
                 reads=["ln1buf", "identb"], writes=["ptr"])
            r.op("vector", "tensor_copy",
                 x_fm[:, dc * TPC + tb * 128:dc * TPC + (tb + 1) * 128],
                 ps_tr[:, 0:128], reads=["ptr"], writes=[f"xfmt{tb}"],
                 accum=(dc > 0))
    if "xfm" in tapd:
        r.op("sync", "dma_start", out=tapd["xfm"], in_=x_fm[:, :], dma="dtap",
             reads=[f"xfmt{t}" for t in range(TT)])

    emit_stripe_load(wk, 0)
    for s in range(4):
        if s + 1 < 4:
            emit_stripe_load(wk, s + 1)
        sl = s % 2
        for sub in range(2):
            dkc = s * 2 + sub
            for tb in range(2):
                b = next_bank()
                pres = f"pa{b}"
                po = ps_a[:, b * 512:(b + 1) * 512]
                for dc in range(DC):
                    r.op("tensor", "matmul", po,
                         wst[sl][:, dc, sub * 128:(sub + 1) * 128],
                         x_fm[:, dc * TPC + tb * 512:dc * TPC + (tb + 1) * 512],
                         start=(dc == 0), stop=(dc == DC - 1),
                         reads=[f"wst{sl}", f"xfmt{tb * 4}", f"xfmt{tb * 4 + 1}",
                                f"xfmt{tb * 4 + 2}", f"xfmt{tb * 4 + 3}"],
                         writes=[pres], accum=(dc > 0))
                st = (dkc * 2 + tb) % 2
                r.op("vector", "tensor_scalar", ktg[st][:, :], po,
                     wkb_sb[:, dkc:dkc + 1], None, OP.add,
                     reads=[pres, "wkb_sb"], writes=[f"ktg{st}"])
                r.op("sync", "dma_start",
                     out=gin[dkc * 128:(dkc + 1) * 128, tb * 512:(tb + 1) * 512],
                     in_=ktg[st][:, :], dma=f"dkst{st}", reads=[f"ktg{st}"],
                     writes=["gin_k"], accum=(dkc + tb > 0))

    vgd = gin[1024:2048, :].rearrange("(b s16) t -> b s16 t", s16=16)
    vgd = vgd.rearrange("(h tc) s16 t -> h tc s16 t", tc=8)
    emit_stripe_load(wv, 0)
    for s in range(4):
        if s + 1 < 4:
            emit_stripe_load(wv, s + 1)
        sl = s % 2
        for tb in range(TT):
            b = next_bank()
            pres = f"pa{b}"
            po = ps_a[:, b * 512:b * 512 + 256]
            for dc in range(DC):
                r.op("tensor", "matmul", po,
                     x_fm[:, dc * TPC + tb * 128:dc * TPC + (tb + 1) * 128],
                     wst[sl][:, dc, :], start=(dc == 0), stop=False,
                     reads=[f"wst{sl}", f"xfmt{tb}"], writes=[pres], accum=(dc > 0))
            r.op("tensor", "matmul", po, onesrow[0:1, :],
                 wvbrow[0:1, s * 256:(s + 1) * 256], start=False, stop=True,
                 reads=["onesrow", "wvbrow"], writes=[pres], accum=True)
            st = tb % 2
            r.op("scalar", "activation", vtg[st][:, :], po, AF.Copy,
                 reads=[pres], writes=[f"vtg{st}"])
            dst = vgd[s * 2:(s + 1) * 2, tb, :, :].rearrange(
                "h s16 (t8 dv) -> (s16 t8) h dv", dv=128)
            r.op("sync", "dma_start", out=dst,
                 in_=vtg[st][:, :].rearrange("p (h dv) -> p h dv", dv=128),
                 dma=f"dvst{st}", reads=[f"vtg{st}"], writes=["gin_v"],
                 accum=(s + tb > 0))

    if "qkv" in tapd:
        r.op("sync", "dma_start", out=tapd["qkv"], in_=gin, dma="dtap",
             reads=["gin_k", "gin_v"])

    r.op("gpsimd", "collective_compute", "AllGather", OP.bypass,
         replica_groups=[[0, 1, 2, 3], [4, 5, 6, 7]],
         ins=[gin.opt()], outs=[gout.opt()],
         reads=["gin_k", "gin_v"], writes=["gout"], dma="dcc", cc=True)

    emit_stripe_load(wq, 0)
    for s in range(16):
        if s + 1 < 16:
            emit_stripe_load(wq, s + 1)
        sl = s % 2
        for sub in range(2):
            dqc = s * 2 + sub
            for tb in range(2):
                b = next_bank()
                pres = f"pa{b}"
                po = ps_a[:, b * 512:(b + 1) * 512]
                for dc in range(DC):
                    r.op("tensor", "matmul", po,
                         wst[sl][:, dc, sub * 128:(sub + 1) * 128],
                         x_fm[:, dc * TPC + tb * 512:dc * TPC + (tb + 1) * 512],
                         start=(dc == 0), stop=(dc == DC - 1),
                         reads=[f"wst{sl}", f"xfmt{tb * 4}", f"xfmt{tb * 4 + 1}",
                                f"xfmt{tb * 4 + 2}", f"xfmt{tb * 4 + 3}"],
                         writes=[pres], accum=(dc > 0))
                st = (dqc * 2 + tb) % 2
                r.op("vector", "tensor_scalar", qtg[st][:, :], po,
                     wqb_sb[:, dqc:dqc + 1], None, OP.add,
                     reads=[pres, "wqb_sb"], writes=[f"qtg{st}"])
                r.op("sync", "dma_start",
                     out=qsc[dqc * 128:(dqc + 1) * 128, tb * 512:(tb + 1) * 512],
                     in_=qtg[st][:, :], dma=f"dqst{st}", reads=[f"qtg{st}"],
                     writes=[f"qsc{dqc}"], accum=(tb > 0))

    gor = gout.rearrange("(rr blk) t -> rr blk t", blk=2048)
    govr = gor[:, 1024:2048, :].rearrange("rr (b s16) t -> rr b s16 t", s16=16)
    govr = govr.rearrange("rr (h tc) s16 t -> rr h tc s16 t", tc=8)

    for i in range(2):
        r.op("gpsimd", "memset", vhb[i][:, :, 128:129], 1.0, writes=[f"vones{i}"])

    def load_kv(h, sl):
        both = [f"khb{sl}", f"vhb{sl}"]
        r.op("sync", "dma_start", out=khb[sl][:, :, :],
             in_=gor[:, h * 128:(h + 1) * 128, :].rearrange("rr p t -> p rr t"),
             dma=f"dkh{sl}", reads=["gout"], writes=both)
        for rr in range(NR):
            src_v = govr[rr, h, :, :, :].rearrange("tc s (t8 dv) -> (s t8) tc dv",
                                                   dv=128)
            r.op("sync", "dma_start",
                 out=vhb[sl][:, rr * 8:(rr + 1) * 8, 0:128],
                 in_=src_v, dma=f"dkh{sl}", reads=["gout"],
                 writes=both, accum=True)

    iters = [(h, f, qb) for h in range(H) for f in range(GF) for qb in range(2)]

    def emit_h_prep(h):
        sl = h % 2
        khflat = khb[sl][:, :, :].rearrange("p rr t -> p (rr t)")
        r.op("scalar", "activation", sq8k[:, :], khflat, AF.Square,
             reads=[f"khb{sl}"], writes=["sq8k"])
        for j in range(8):
            r.op("tensor", "matmul", ps_d[0:1, 0:512], onescol[:, 0:1],
                 sq8k[:, j * 512:(j + 1) * 512], start=True, stop=True,
                 reads=["sq8k", "onescol"], writes=["pd"])
            r.op("vector", "tensor_copy", knrow[0:1, :], ps_d[0:1, 0:512],
                 reads=["pd"], writes=["knrow"])
            r.op("vector", "tensor_reduce", nsc[0:1, 0:1], knrow[0:1, :], AX.X,
                 OP.max, reads=["knrow"], writes=["nsc"])
            if j == 0:
                r.op("vector", "tensor_copy", mk2[0:1, 0:1], nsc[0:1, 0:1],
                     reads=["nsc"], writes=["mk2"])
            else:
                r.op("vector", "tensor_tensor", mk2[0:1, 0:1], mk2[0:1, 0:1],
                     nsc[0:1, 0:1], OP.max, reads=["nsc", "mk2"], writes=["mk2"])

    def emit_f_prep(h, f):
        qh = h * GF + f
        qs = qh % 2
        r.op("sync", "dma_start", out=qsl[qs][:, :],
             in_=qsc[qh * 128:(qh + 1) * 128, :], dma=f"dql{qs}",
             reads=[f"qsc{qh}"], writes=[f"qsl{qs}"])
        r.op("scalar", "activation", sq8k[:, 0:TPC], qsl[qs][:, :], AF.Square,
             reads=[f"qsl{qs}"], writes=["sq8k"])
        for j in range(2):
            r.op("tensor", "matmul", ps_d[0:1, 0:512], onescol[:, 0:1],
                 sq8k[:, j * 512:(j + 1) * 512], start=True, stop=True,
                 reads=["sq8k", "onescol"], writes=["pd"])
            r.op("vector", "tensor_copy", knrow[0:1, :], ps_d[0:1, 0:512],
                 reads=["pd"], writes=["knrow"])
            r.op("vector", "tensor_reduce", nsc[0:1, 0:1], knrow[0:1, :], AX.X,
                 OP.max, reads=["knrow"], writes=["nsc"])
            if j == 0:
                r.op("vector", "tensor_copy", mq2[0:1, 0:1], nsc[0:1, 0:1],
                     reads=["nsc"], writes=["mq2"])
            else:
                r.op("vector", "tensor_tensor", mq2[0:1, 0:1], mq2[0:1, 0:1],
                     nsc[0:1, 0:1], OP.max, reads=["nsc", "mq2"], writes=["mq2"])
        r.op("vector", "tensor_tensor", nsc[0:1, 0:1], mq2[0:1, 0:1],
             mk2[0:1, 0:1], OP.add, reads=["mq2", "mk2"], writes=["nsc"])
        r.op("vector", "tensor_scalar", nsc[0:1, 0:1], nsc[0:1, 0:1],
             -0.5 * SCALE, None, OP.mult, reads=["nsc"], writes=["nsc"],
             accum=True)
        r.op("vector", "tensor_copy", bqbf[0:1, 0:1], nsc[0:1, 0:1],
             reads=["nsc"], writes=["bqbf"])
        r.op("tensor", "matmul", ps_d[:, 256:257], onesrow[0:1, :],
             bqbf[0:1, 0:1], start=True, stop=True,
             reads=["onesrow", "bqbf"], writes=["pd"])
        r.op("vector", "tensor_copy", biasq[:, :], ps_d[:, 256:257],
             reads=["pd"], writes=["biasq"])

    def emit_scores(idx):
        h, f, qb = iters[idx]
        sl = h % 2
        qh = h * GF + f
        qs = qh % 2
        par = idx % 2
        khflat = khb[sl][:, :, :].rearrange("p rr t -> p (rr t)")
        for kc in range(DC):
            bk = kc % 4
            r.op("tensor", "matmul", ps_a[:, bk * 512:(bk + 1) * 512],
                 khflat[:, kc * 128:(kc + 1) * 128],
                 qsl[qs][:, qb * 512:(qb + 1) * 512],
                 start=True, stop=True,
                 reads=[f"khb{sl}", f"qsl{qs}"], writes=[f"pa{bk}"])
            if kc % 2 == 1:
                g0 = kc - 1
                bk0 = g0 % 4
                r.op("scalar", "activation",
                     pbuf[par][:, g0 * 512:(g0 + 2) * 512],
                     ps_a[:, bk0 * 512:(bk0 + 2) * 512], AF.Exp,
                     bias=biasq[:, 0:1], scale=SCALE,
                     reads=[f"pa{bk0}", f"pa{bk0 + 1}", "biasq"],
                     writes=[f"p_buf{par}"], accum=(g0 > 0))

    def emit_av(idx):
        h, f, qb = iters[idx]
        sl = h % 2
        qh = h * GF + f
        par = idx % 2
        for qt in range(4):
            bb_ = qt % 2
            pres = f"pb{bb_}"
            for kc in range(DC):
                r.op("tensor", "matmul", ps_b[:, bb_ * 512:bb_ * 512 + 129],
                     pbuf[par][:, kc * 512 + qt * 128:kc * 512 + (qt + 1) * 128],
                     vhb[sl][:, kc, 0:129],
                     start=(kc == 0), stop=(kc == DC - 1),
                     reads=[f"p_buf{par}", f"vhb{sl}", f"vones{sl}"],
                     writes=[pres], accum=(kc > 0))
            r.op("vector", "reciprocal", recip[:, :],
                 ps_b[:, bb_ * 512 + 128:bb_ * 512 + 129],
                 reads=[pres], writes=["recip"])
            r.op("vector", "tensor_scalar", astg[:, :],
                 ps_b[:, bb_ * 512:bb_ * 512 + 128], recip[:, 0:1], None,
                 OP.mult, reads=[pres, "recip"], writes=["astg"])
            r.op("tensor", "transpose", ps_tr[:, 0:128], astg[:, :],
                 identb[:, :], reads=["astg", "identb"], writes=["ptr"])
            qoff = qh * TPC + qb * 512 + qt * 128
            r.op("vector", "tensor_copy", a_fm[:, qoff:qoff + 128],
                 ps_tr[:, 0:128], reads=["ptr"], writes=["a_fm"],
                 accum=(qh + qb + qt > 0))

    load_kv(0, 0)
    load_kv(1, 1)
    emit_h_prep(0)
    emit_f_prep(0, 0)
    emit_scores(0)
    for idx in range(len(iters)):
        nxt = idx + 1
        if nxt < len(iters):
            hn, fn, qbn = iters[nxt]
            if qbn == 0:
                if fn == 0:
                    emit_h_prep(hn)
                emit_f_prep(hn, fn)
            emit_scores(nxt)
        emit_av(idx)
        hh, ff, qq = iters[idx]
        if ff == GF - 1 and qq == 1 and hh + 2 < H:
            load_kv(hh + 2, (hh + 2) % 2)

    if "afm" in tapd:
        r.op("sync", "dma_start", out=tapd["afm"], in_=a_fm[:, :], dma="dtap",
             reads=["a_fm"])

    r.op("gpsimd", "dma_start", out=wobrow[0:1, :],
         in_=wob.rearrange("(a k) -> a k", a=1), dma="dwob", writes=["wobrow"])
    r.retouch("wst0")
    r.retouch("wst1")
    emit_stripe_load(wo, 0)
    for s in range(16):
        if s + 1 < 16:
            emit_stripe_load(wo, s + 1)
        sl = s % 2
        for tb in range(TT):
            b = next_bank()
            pres = f"pa{b}"
            po = ps_a[:, b * 512:b * 512 + 256]
            for dc in range(DC):
                r.op("tensor", "matmul", po,
                     a_fm[:, dc * TPC + tb * 128:dc * TPC + (tb + 1) * 128],
                     wst[sl][:, dc, :], start=(dc == 0), stop=False,
                     reads=[f"wst{sl}", "a_fm"], writes=[pres], accum=(dc > 0))
            r.op("tensor", "matmul", po, onesrow[0:1, :],
                 wobrow[0:1, s * 256:(s + 1) * 256], start=False, stop=True,
                 reads=["onesrow", "wobrow"], writes=[pres], accum=True)
            r.op("scalar", "activation",
                 attn_sb[:, tb * D + s * 256:tb * D + (s + 1) * 256], po, AF.Copy,
                 reads=[pres], writes=["attn_sb"], accum=(s + tb > 0))
    for tb in range(TT):
        row = attn_sb[:, tb * D:(tb + 1) * D]
        layernorm_stats(row, "attn_sb")
        r.op("vector", "tensor_scalar", row, row, bn2[:, 0:1], rstd[:, 0:1],
             OP.subtract, OP.mult, reads=["attn_sb", "bn2", "rstd"],
             writes=["attn_sb"], accum=True)
        r.op("sync", "dma_start", out=lnasc[tb * 128:(tb + 1) * 128, :], in_=row,
             dma="dlna", reads=["attn_sb"], writes=["lnasc"], accum=(tb > 0))
    if "attn" in tapd:
        r.op("sync", "dma_start", out=tapd["attn"], in_=attn_sb[:, :], dma="dtap",
             reads=["attn_sb"])

    for tb in range(TT):
        r.op("sync", "dma_start", out=x5[:, :], in_=xd[tb * 128:(tb + 1) * 128, :],
             dma="dx5", writes=["x5"])
        r.op("vector", "tensor_tensor", x5[:, :], x5[:, :],
             attn_sb[:, tb * D:(tb + 1) * D], OP.add,
             reads=["x5", "attn_sb"], writes=["x5"], accum=True)
        layernorm_stats(x5[:, :], "x5")
        r.op("vector", "tensor_scalar", ln2buf[:, :], x5[:, :], bn2[:, 0:1],
             rstd[:, 0:1], OP.subtract, OP.mult,
             reads=["x5", "bn2", "rstd"], writes=["ln2buf"])
        for dc in range(DC):
            r.op("tensor", "transpose", ps_tr[:, 0:128],
                 ln2buf[:, dc * 128:(dc + 1) * 128], identb[:, :],
                 reads=["ln2buf", "identb"], writes=["ptr"])
            r.op("vector", "tensor_copy",
                 x2_fm[:, dc * TPC + tb * 128:dc * TPC + (tb + 1) * 128],
                 ps_tr[:, 0:128], reads=["ptr"], writes=["x2_fm"],
                 accum=(tb + dc > 0))
    if "x2fm" in tapd:
        r.op("sync", "dma_start", out=tapd["x2fm"], in_=x2_fm[:, :], dma="dtap",
             reads=["x2_fm"])

    r.op("gpsimd", "dma_start", out=atsb[:, :, :],
         in_=at_.rearrange("(dc p) e -> p dc e", p=128), dma="dat", writes=["atsb"])
    r.op("gpsimd", "dma_start", out=ballt[:, :], in_=br_, dma="dbr", writes=["ballt"])
    r.op("gpsimd", "dma_start", out=bbrow[0:1, :],
         in_=bb.rearrange("(a k) -> a k", a=1), dma="dbb", writes=["bbrow"])
    for tb in range(TT):
        for dc in range(DC):
            r.op("tensor", "matmul", ps_b[:, 0:128],
                 x2_fm[:, dc * TPC + tb * 128:dc * TPC + (tb + 1) * 128],
                 atsb[:, dc, :], start=(dc == 0), stop=False,
                 reads=["x2_fm", "atsb"], writes=["pb0"], accum=(dc > 0))
            r.op("tensor", "matmul", ps_b[:, 128:136],
                 x2_fm[:, dc * TPC + tb * 128:dc * TPC + (tb + 1) * 128],
                 wrt[:, dc * 8:(dc + 1) * 8], start=False, stop=(dc == DC - 1),
                 reads=["x2_fm", "wrt"], writes=["pb0"], accum=True)
        r.op("scalar", "activation", hsb[:, :], ps_b[:, 0:136], AF.Copy,
             reads=["pb0"], writes=["hsb"])
        lg = hsb[:, 128:136]
        r.op("vector", "tensor_reduce", tm1[:, :], lg, AX.X, OP.max,
             reads=["hsb"], writes=["tm1"])
        r.op("vector", "tensor_scalar", teq1[:, :], lg, tm1[:, 0:1], None,
             OP.is_equal, reads=["hsb", "tm1"], writes=["teq1"])
        r.op("vector", "scalar_tensor_tensor", tlm[:, :], teq1[:, :], -1e30, lg,
             OP.mult, OP.add, reads=["teq1", "hsb"], writes=["tlm"])
        r.op("vector", "tensor_reduce", tm2[:, :], tlm[:, :], AX.X, OP.max,
             reads=["tlm"], writes=["tm2"])
        r.op("vector", "tensor_scalar", teq2[:, :], tlm[:, :], tm2[:, 0:1], None,
             OP.is_equal, reads=["tlm", "tm2"], writes=["teq2"])
        r.op("vector", "tensor_scalar", tm2[:, :], tm2[:, :], tm1[:, 0:1], None,
             OP.subtract, reads=["tm2", "tm1"], writes=["tm2"])
        r.op("scalar", "activation", trw2[:, :], tm2[:, :], AF.Sigmoid,
             reads=["tm2"], writes=["trw2"])
        r.op("vector", "tensor_scalar", trw1[:, :], trw2[:, :], -1.0, 1.0,
             OP.mult, OP.add, reads=["trw2"], writes=["trw1"])
        r.op("vector", "tensor_scalar", tt1[:, :], teq1[:, :], trw1[:, 0:1], None,
             OP.mult, reads=["teq1", "trw1"], writes=["tt1"])
        r.op("vector", "scalar_tensor_tensor", twts[:, :], teq2[:, :], trw2[:, 0:1],
             tt1[:, :], OP.mult, OP.add, reads=["teq2", "trw2", "tt1"],
             writes=["twts"])
        for e in range(NE):
            r.op("vector", "tensor_scalar", gtm[:, e * RK:(e + 1) * RK],
                 hsb[:, e * RK:(e + 1) * RK], twts[:, e:e + 1], LSCALE,
                 OP.mult, OP.mult, reads=["hsb", "twts"], writes=["gtm"],
                 accum=(e > 0))
        r.op("tensor", "transpose", ps_tr[:, 0:128], gtm[:, :], identb[:, :],
             reads=["gtm", "identb"], writes=["ptr"])
        r.op("vector", "tensor_copy", gfm[:, tb * 128:(tb + 1) * 128],
             ps_tr[:, 0:128], reads=["ptr"], writes=["gfm"], accum=(tb > 0))
    r.retouch("wst0")
    r.retouch("wst1")
    emit_stripe_load(wbt, 0)
    for s in range(16):
        if s + 1 < 16:
            emit_stripe_load(wbt, s + 1)
        sl = s % 2
        for tb in range(TT):
            b = next_bank()
            pres = f"pa{b}"
            po = ps_a[:, b * 512:b * 512 + 256]
            for dc in range(DC):
                r.op("tensor", "matmul", po,
                     x2_fm[:, dc * TPC + tb * 128:dc * TPC + (tb + 1) * 128],
                     wst[sl][:, dc, :], start=(dc == 0), stop=False,
                     reads=[f"wst{sl}", "x2_fm"], writes=[pres], accum=(dc > 0))
            r.op("tensor", "matmul", po, gfm[:, tb * 128:(tb + 1) * 128],
                 ballt[:, s * 256:(s + 1) * 256], start=False, stop=False,
                 reads=["gfm", "ballt"], writes=[pres], accum=True)
            r.op("tensor", "matmul", po, onesrow[0:1, :],
                 bbrow[0:1, s * 256:(s + 1) * 256], start=False, stop=True,
                 reads=["onesrow", "bbrow"], writes=[pres], accum=True)
            r.op("scalar", "activation",
                 ffn_sb[:, tb * D + s * 256:tb * D + (s + 1) * 256], po, AF.Copy,
                 reads=[pres], writes=["ffn_sb"], accum=(s + tb > 0))
    if "ffn" in tapd:
        r.op("sync", "dma_start", out=tapd["ffn"], in_=ffn_sb[:, :], dma="dtap",
             reads=["ffn_sb"])

    for tb in range(TT):
        row = ffn_sb[:, tb * D:(tb + 1) * D]
        layernorm_stats(row, "ffn_sb")
        r.op("vector", "tensor_scalar", row, row, bn2[:, 0:1], rstd[:, 0:1],
             OP.subtract, OP.mult, reads=["ffn_sb", "bn2", "rstd"],
             writes=["ffn_sb"], accum=True)
        r.op("sync", "dma_start", out=x7[:, :], in_=xd[tb * 128:(tb + 1) * 128, :],
             dma="dx7", writes=["x7"])
        r.op("sync", "dma_start", out=lna7[:, :],
             in_=lnasc[tb * 128:(tb + 1) * 128, :], dma="dl7",
             reads=["lnasc"], writes=["lna7"])
        r.op("vector", "tensor_tensor", x7[:, :], x7[:, :], lna7[:, :], OP.add,
             reads=["x7", "lna7"], writes=["x7"], accum=True)
        r.op("vector", "tensor_tensor", x7[:, :], x7[:, :], row, OP.add,
             reads=["x7", "ffn_sb"], writes=["x7"], accum=True)
        r.op("sync", "dma_start", out=outd[tb * 128:(tb + 1) * 128, :], in_=x7[:, :],
             dma="dout", reads=["x7"], writes=["outd"], accum=(tb > 0))

    r.final_wait("dout")
    if tapd:
        r.final_wait("dtap")

    r.materialize()

    with contextlib.ExitStack() as st2:
        esems = {e: st2.enter_context(nc.semaphore(f"se_{e}")) for e in ENG}
        dsems = {n: st2.enter_context(nc.semaphore(f"sd_{n}"))
                 for n in sorted(r.dma_sems)}
        block = st2.enter_context(nc.Block())

        @block.tensor
        def _(t):
            r.replay(t, "tensor", esems, dsems)

        @block.vector
        def _(v):
            r.replay(v, "vector", esems, dsems)

        @block.scalar
        def _(sc):
            r.replay(sc, "scalar", esems, dsems)

        @block.sync
        def _(sy):
            r.replay(sy, "sync", esems, dsems)

        @block.gpsimd
        def _(gp):
            r.replay(gp, "gpsimd", esems, dsems)

    stack.close()
    return nc


def kernel(L0_attn_x, L0_attn_wq_w, L0_attn_wq_b, L0_attn_wk_w, L0_attn_wk_b,
           L0_attn_wv_w, L0_attn_wv_b, L0_attn_wo_w, L0_attn_wo_b,
           L0_ffn_W_base, L0_ffn_b_base, L0_ffn_W_router, L0_ffn_A, L0_ffn_B):
    xf = np.ascontiguousarray(
        np.asarray(L0_attn_x, dtype=np.float32).reshape(NCORES * TPC, D))
    base = {
        "wq": np.ascontiguousarray(L0_attn_wq_w, dtype=np.float32),
        "wk": np.ascontiguousarray(L0_attn_wk_w, dtype=np.float32),
        "wv": np.ascontiguousarray(L0_attn_wv_w, dtype=np.float32),
        "wo": np.ascontiguousarray(L0_attn_wo_w, dtype=np.float32),
        "wqb": np.ascontiguousarray(L0_attn_wq_b, dtype=np.float32),
        "wkb": np.ascontiguousarray(L0_attn_wk_b, dtype=np.float32),
        "wvb": np.ascontiguousarray(L0_attn_wv_b, dtype=np.float32),
        "wob": np.ascontiguousarray(L0_attn_wo_b, dtype=np.float32),
        "wbt": np.ascontiguousarray(np.asarray(L0_ffn_W_base, dtype=np.float32).T),
        "bb": np.ascontiguousarray(L0_ffn_b_base, dtype=np.float32),
        "wr": np.ascontiguousarray(L0_ffn_W_router, dtype=np.float32),
        "at": np.ascontiguousarray(
            np.asarray(L0_ffn_A, dtype=np.float32).reshape(NE * RK, D).T),
        "br": np.ascontiguousarray(
            np.asarray(L0_ffn_B, dtype=np.float32).transpose(0, 2, 1).reshape(NE * RK, D)),
    }
    in_maps = []
    for c in range(NCORES):
        m = dict(base)
        m["x"] = np.ascontiguousarray(xf[c * TPC:(c + 1) * TPC])
        in_maps.append(m)
    nc = build_nc(taps=_DEBUG_TAPS)
    res = run_bass_kernel_spmd(nc, in_maps, core_ids=list(range(NCORES)))
    out = np.concatenate([res.results[c]["out"] for c in range(NCORES)], axis=0)
    return np.ascontiguousarray(out.reshape(2, SL, D), dtype=np.float32)


# version 03
# speedup vs baseline: 1.0654x; 1.0654x over previous
import math
import contextlib
import numpy as np
import concourse.bass as bass
import concourse.mybir as mybir
from concourse.alu_op_type import AluOpType as OP
from concourse.bass_utils import run_bass_kernel_spmd

F32 = mybir.dt.float32
BF16 = mybir.dt.bfloat16
AF = mybir.ActivationFunctionType
AX = mybir.AxisListType

NCORES = 8
D = 4096
DC = 32
TPC = 1024
TT = 8
KV = 1024
H = 8
GF = 4
HD = 128
SL = 4096
NR = 4
SCALE = 1.0 / math.sqrt(HD)
EPS = 1e-5
NE = 8
RK = 16
LSCALE = 2.0

ENG = ("tensor", "vector", "scalar", "sync", "gpsimd")

_DEBUG_TAPS = []


class Rec:

    def __init__(self):
        self.ops = []
        self.wl = {}
        self.r = {}
        self.spans = {}
        self.touched = set()

    def span(self, res, lo, hi):
        self.spans[res] = (lo, hi)

    def retouch(self, res):
        self.touched.discard(res)

    def op(self, eng, meth, *args, reads=(), writes=(), accum=False, dma=None,
           cc=False, **kwargs):
        i = len(self.ops)
        deps = set()
        for res in writes:
            if res not in self.touched and res in self.spans:
                lo, hi = self.spans[res]
                for other, (lo2, hi2) in self.spans.items():
                    if other == res or hi2 <= lo or hi <= lo2:
                        continue
                    deps.update(self.wl.get(other, ()))
                    deps.update(self.r.get(other, ()))
                self.touched.add(res)
        for res in reads:
            deps.update(self.wl.get(res, ()))
        if not accum:
            for res in writes:
                deps.update(self.wl.get(res, ()))
                deps.update(self.r.get(res, ()))
        for res in writes:
            if accum:
                self.wl.setdefault(res, []).append(i)
            else:
                self.wl[res] = [i]
                self.r[res] = []
        for res in reads:
            self.r.setdefault(res, []).append(i)
        self.ops.append(dict(i=i, eng=eng, meth=meth, args=args, kwargs=kwargs,
                             deps=deps, dma=dma, cc=cc, need=False, sem=None,
                             val=None))
        return i

    def final_wait(self, dma_name):
        self.ops.append(dict(i=len(self.ops), eng="sync", meth="_final_wait",
                             args=(dma_name,), kwargs={}, deps=set(), dma=None,
                             cc=False, need=False, sem=None, val=None))

    def materialize(self):
        for o in self.ops:
            best = {}
            for d in o["deps"]:
                p = self.ops[d]
                key = ("dma", p["dma"]) if p["dma"] is not None else ("eng", p["eng"])
                if key not in best or d > best[key]:
                    best[key] = d
            o["deps"] = set(best.values())
        for o in self.ops:
            for d in o["deps"]:
                self.ops[d]["need"] = True
        engct = {e: 0 for e in ENG}
        dmact = {}
        self.dma_sems = set()
        for o in self.ops:
            if o["dma"] is not None:
                step = 1 if o["cc"] else 16
                dmact[o["dma"]] = dmact.get(o["dma"], 0) + step
                o["sem"], o["val"] = ("dma", o["dma"]), dmact[o["dma"]]
                self.dma_sems.add(o["dma"])
            elif o["need"]:
                engct[o["eng"]] += 1
                o["sem"], o["val"] = ("eng", o["eng"]), engct[o["eng"]]
        hw = {e: {} for e in ENG}
        for o in self.ops:
            if o["meth"] == "_final_wait":
                o["waits"] = [(("dma", o["args"][0]), dmact[o["args"][0]])]
                continue
            waits = []
            for d in sorted(o["deps"]):
                p = self.ops[d]
                if p["sem"] is None:
                    continue
                key = p["sem"]
                cur = hw[o["eng"]].get(key, 0)
                if p["val"] > cur:
                    waits.append((key, p["val"]))
                    hw[o["eng"]][key] = p["val"]
            o["waits"] = waits

    def replay(self, engobj, eng, esems, dsems):
        for o in self.ops:
            if o["eng"] != eng:
                continue
            for (kind, name), v in o.get("waits", ()):
                engobj.wait_ge(dsems[name] if kind == "dma" else esems[name], v)
            if o["meth"] == "_final_wait":
                continue
            inst = getattr(engobj, o["meth"])(*o["args"], **o["kwargs"])
            if o["sem"] is not None:
                if o["sem"][0] == "dma":
                    inst.then_inc(dsems[o["sem"][1]], 1 if o["cc"] else 16)
                else:
                    inst.then_inc(esems[eng], 1)


def build_nc(taps=()):
    nc = bass.Bass("TRN2", target_bir_lowering=False, debug=False)

    xd = nc.dram_tensor("x", [TPC, D], F32, kind="ExternalInput").ap()
    wq = nc.dram_tensor("wq", [D, D], F32, kind="ExternalInput").ap()
    wk = nc.dram_tensor("wk", [D, KV], F32, kind="ExternalInput").ap()
    wv = nc.dram_tensor("wv", [D, KV], F32, kind="ExternalInput").ap()
    wo = nc.dram_tensor("wo", [D, D], F32, kind="ExternalInput").ap()
    wqb = nc.dram_tensor("wqb", [D], F32, kind="ExternalInput").ap()
    wkb = nc.dram_tensor("wkb", [KV], F32, kind="ExternalInput").ap()
    wvb = nc.dram_tensor("wvb", [KV], F32, kind="ExternalInput").ap()
    wob = nc.dram_tensor("wob", [D], F32, kind="ExternalInput").ap()
    wbt = nc.dram_tensor("wbt", [D, D], F32, kind="ExternalInput").ap()
    bb = nc.dram_tensor("bb", [D], F32, kind="ExternalInput").ap()
    wr = nc.dram_tensor("wr", [NE, D], F32, kind="ExternalInput").ap()
    at_ = nc.dram_tensor("at", [D, NE * RK], F32, kind="ExternalInput").ap()
    br_ = nc.dram_tensor("br", [NE * RK, D], F32, kind="ExternalInput").ap()
    outd = nc.dram_tensor("out", [TPC, D], F32, kind="ExternalOutput").ap()

    qsc = nc.dram_tensor("qscratch", [D, TPC], BF16).ap()
    gin = nc.dram_tensor("gin", [2048, KV], BF16).ap()
    gout = nc.dram_tensor("gout", [NR * 2048, KV], BF16).ap()
    lnasc = nc.dram_tensor("lnascratch", [TPC, D], BF16).ap()

    tapd = {}
    tap_shapes = {"xfm": [128, DC * TPC], "afm": [128, DC * TPC],
                  "x2fm": [128, DC * TPC], "attn": [128, TT * D],
                  "ffn": [128, TT * D], "pbuf": [128, 16384],
                  "qkv": [2048, KV]}
    for t in taps:
        tapd[t] = nc.dram_tensor("tap_" + t, tap_shapes[t], BF16,
                                 kind="ExternalOutput").ap()

    r = Rec()

    SB0 = 16512

    def sb(name, shape, dtype, off):
        t = nc.alloc_sbuf_tensor_at(name, shape, dtype, offset=SB0 + off)
        nbytes = 1
        for s in shape[1:]:
            nbytes *= s
        nbytes *= 2 if dtype == BF16 else 4
        r.span(name, off, off + nbytes)
        return t

    x_fm = sb("x_fm", [128, DC * TPC], BF16, 0)
    for _tb in range(TT):
        r.span(f"xfmt{_tb}", 0, 65536)
    a_fm = sb("a_fm", [128, DC * TPC], BF16, 0)
    x2_fm = sb("x2_fm", [128, DC * TPC], BF16, 0)
    ln1buf = sb("ln1buf", [128, D], BF16, 65536)
    wrtmp = sb("wrtmp", [NE, D], BF16, 65536 + 8192)
    pbuf = [sb(f"pbuf{i}", [128, 16384], BF16, 65536 + i * 32768) for i in range(2)]
    for _p in range(2):
        for _g in range(16):
            r.span(f"pb{_p}_{_g}", 65536 + _p * 32768 + _g * 2048,
                   65536 + _p * 32768 + (_g + 1) * 2048)
    qsl = [sb(f"qsl{i}", [128, TPC], BF16, 164864 + i * 2048) for i in range(2)]
    sq8k = sb("sq8k", [128, D], BF16, 168960)
    knrow = sb("knrow", [1, 512], F32, 177152)
    attn_sb = sb("attn_sb", [128, TT * D], BF16, 65536)
    ffn_sb = sb("ffn_sb", [128, TT * D], BF16, 65536)
    xsl = [sb(f"xsl{i}", [128, D], F32, 131072 + i * 16384) for i in range(2)]
    qtg = [sb(f"qtg{i}", [128, 512], BF16, 131072 + i * 1024) for i in range(2)]
    ktg = [sb(f"ktg{i}", [128, 512], BF16, 133120 + i * 1024) for i in range(2)]
    vtg = [sb(f"vtg{i}", [128, 256], BF16, 135168 + i * 512) for i in range(2)]
    khb = [sb(f"khb{i}", [128, NR, 1024], BF16, 131072 + i * 8192) for i in range(2)]
    vhb = [sb(f"vhb{i}", [128, DC, 130], BF16, 147456 + i * 8320) for i in range(2)]
    wobrow = sb("wobrow", [1, D], BF16, 131072)
    x5 = sb("x5", [128, D], F32, 131072)
    ln2buf = sb("ln2buf", [128, D], BF16, 147456)
    atsb = sb("atsb", [128, DC, NE * RK], BF16, 131072)
    ballt = sb("ballt", [128, D], BF16, 139264)
    bbrow = sb("bbrow", [1, D], BF16, 147456)
    lna7 = sb("lna7", [128, D], BF16, 131072)
    x7 = sb("x7", [128, D], F32, 147456)
    wst = [sb(f"wst{i}", [128, DC, 256], BF16, 164864 + i * 16384) for i in range(2)]
    M = 197632

    def msb(name, shape, dtype):
        nonlocal M
        t = sb(name, shape, dtype, M)
        nbytes = 1
        for s in shape[1:]:
            nbytes *= s
        nbytes *= 2 if dtype == BF16 else 4
        M += (nbytes + 31) // 32 * 32
        return t

    identf = msb("identf", [128, 128], F32)
    identb = msb("identb", [128, 128], BF16)
    iot0 = msb("iot0", [128, 128], F32)
    iot1 = msb("iot1", [128, 128], F32)
    wqb_sb = msb("wqb_sb", [128, 32], F32)
    wkb_sb = msb("wkb_sb", [128, 8], F32)
    braw = msb("braw", [32, 128], F32)
    onescol = msb("onescol", [128, 1], BF16)
    onesrow = msb("onesrow", [1, 128], BF16)
    wvbrow = msb("wvbrow", [1, KV], BF16)
    wrt = msb("wrt", [128, DC * NE], BF16)
    biasq = msb("biasq", [128, 1], F32)
    bqbf = msb("bqbf", [1, 1], BF16)
    mk2 = msb("mk2", [1, 1], F32)
    mq2 = msb("mq2", [1, 1], F32)
    nsc = msb("nsc", [1, 1], F32)
    bnb = msb("bnb", [128, 8, 6], F32)
    bn2 = msb("bn2", [128, 2], F32)
    sdv = msb("sdv", [128, 1], F32)
    rstd = msb("rstd", [128, 1], F32)
    recip = msb("recip", [128, 1], F32)
    astg = msb("astg", [128, 128], BF16)
    hsb = msb("hsb", [128, 136], F32)
    gtm = msb("gtm", [128, 128], BF16)
    gfm = msb("gfm", [128, TT * 128], BF16)
    tm1 = msb("tm1", [128, 1], F32)
    tm2 = msb("tm2", [128, 1], F32)
    teq1 = msb("teq1", [128, NE], F32)
    teq2 = msb("teq2", [128, NE], F32)
    tlm = msb("tlm", [128, NE], F32)
    trw2 = msb("trw2", [128, 1], F32)
    trw1 = msb("trw1", [128, 1], F32)
    tt1 = msb("tt1", [128, NE], F32)
    twts = msb("twts", [128, NE], F32)
    assert SB0 + M <= 229344, M

    stack = contextlib.ExitStack()
    ps_a = stack.enter_context(nc.psum_tensor("ps_a", [128, 2048], F32))
    ps_b = stack.enter_context(nc.psum_tensor("ps_b", [128, 1024], F32))
    ps_tr = stack.enter_context(nc.psum_tensor("ps_tr", [128, 1024], BF16))
    ps_d = stack.enter_context(nc.psum_tensor("ps_d", [128, 512], F32))

    r.op("gpsimd", "iota", iot0[:, :], [[1, 128]], channel_multiplier=0,
         allow_small_or_imprecise_dtypes=True, writes=["iot0"])
    r.op("gpsimd", "iota", iot1[:, :], [[0, 128]], channel_multiplier=1,
         allow_small_or_imprecise_dtypes=True, writes=["iot1"])
    r.op("vector", "tensor_tensor", identf[:, :], iot0[:, :], iot1[:, :], OP.is_equal,
         reads=["iot0", "iot1"], writes=["identf"])
    r.op("vector", "tensor_copy", identb[:, :], identf[:, :],
         reads=["identf"], writes=["identb"])
    r.op("gpsimd", "memset", onescol[:, :], 1.0, writes=["onescol"])
    r.op("gpsimd", "memset", onesrow[:, :], 1.0, writes=["onesrow"])
    for i in range(2):
        r.span(f"vones{i}", *r.spans[f"vhb{i}"])
    r.op("sync", "dma_start", out=braw[:, :], in_=wqb.rearrange("(a p) -> a p", p=128),
         dma="dbias", writes=["braw"])
    r.op("tensor", "transpose", ps_d[:, 0:32], braw[:, :], identf[0:32, 0:32],
         reads=["braw", "identf"], writes=["pd"])
    r.op("vector", "tensor_copy", wqb_sb[:, :], ps_d[:, 0:32], reads=["pd"],
         writes=["wqb_sb"])
    r.op("sync", "dma_start", out=braw[0:8, :], in_=wkb.rearrange("(a p) -> a p", p=128),
         dma="dbias", reads=["pd"], writes=["braw"])
    r.op("tensor", "transpose", ps_d[:, 32:40], braw[0:8, :], identf[0:8, 0:8],
         reads=["braw", "identf"], writes=["pd"])
    r.op("vector", "tensor_copy", wkb_sb[:, :], ps_d[:, 32:40], reads=["pd"],
         writes=["wkb_sb"])
    r.op("gpsimd", "dma_start", out=wvbrow[0:1, :],
         in_=wvb.rearrange("(a k) -> a k", a=1), dma="dwvb", writes=["wvbrow"])
    r.op("gpsimd", "dma_start", out=wrtmp[:, :], in_=wr, dma="dwrt", writes=["wrtmp"])
    for dc in range(DC):
        r.op("tensor", "transpose", ps_tr[:, 0:8], wrtmp[:, dc * 128:(dc + 1) * 128],
             identb[0:8, 0:8], reads=["wrtmp", "identb"], writes=["ptr"])
        r.op("vector", "tensor_copy", wrt[:, dc * 8:(dc + 1) * 8], ps_tr[:, 0:8],
             reads=["ptr"], writes=["wrt"], accum=(dc > 0))

    def layernorm_stats(src_ap, src_res):
        for i in range(8):
            r.op("vector", "bn_stats", bnb[:, i, :], src_ap[:, i * 512:(i + 1) * 512],
                 reads=[src_res], writes=["bnb"], accum=(i > 0))
        r.op("vector", "bn_aggr", bn2[:, :], bnb[:, :, :], reads=["bnb"], writes=["bn2"])
        r.op("vector", "tensor_scalar", sdv[:, :], bn2[:, 1:2], EPS, None, OP.add,
             reads=["bn2"], writes=["sdv"])
        r.op("scalar", "activation", sdv[:, :], sdv[:, :], AF.Sqrt,
             reads=["sdv"], writes=["sdv"])
        r.op("vector", "reciprocal", rstd[:, :], sdv[:, :], reads=["sdv"], writes=["rstd"])

    def emit_stripe_load(wsrc, s):
        sl = s % 2
        src = wsrc.rearrange("(dc p) n -> p dc n", p=128)[:, :, s * 256:(s + 1) * 256]
        r.op("gpsimd", "dma_start", out=wst[sl][:, :, :], in_=src,
             dma=f"dw{sl}", writes=[f"wst{sl}"])
        return sl

    pctr = [0]

    def next_bank():
        b = pctr[0] % 4
        pctr[0] += 1
        return b

    for tb in range(TT):
        sl = tb % 2
        r.op("sync", "dma_start", out=xsl[sl][:, :], in_=xd[tb * 128:(tb + 1) * 128, :],
             dma=f"dx{sl}", writes=[f"xsl{sl}"])
        layernorm_stats(xsl[sl][:, :], f"xsl{sl}")
        r.op("vector", "tensor_scalar", ln1buf[:, :], xsl[sl][:, :], bn2[:, 0:1],
             rstd[:, 0:1], OP.subtract, OP.mult,
             reads=[f"xsl{sl}", "bn2", "rstd"], writes=["ln1buf"])
        for dc in range(DC):
            r.op("tensor", "transpose", ps_tr[:, 0:128],
                 ln1buf[:, dc * 128:(dc + 1) * 128], identb[:, :],
                 reads=["ln1buf", "identb"], writes=["ptr"])
            r.op("vector", "tensor_copy",
                 x_fm[:, dc * TPC + tb * 128:dc * TPC + (tb + 1) * 128],
                 ps_tr[:, 0:128], reads=["ptr"], writes=[f"xfmt{tb}"],
                 accum=(dc > 0))
    if "xfm" in tapd:
        r.op("sync", "dma_start", out=tapd["xfm"], in_=x_fm[:, :], dma="dtap",
             reads=[f"xfmt{t}" for t in range(TT)])

    emit_stripe_load(wk, 0)
    for s in range(4):
        if s + 1 < 4:
            emit_stripe_load(wk, s + 1)
        sl = s % 2
        for sub in range(2):
            dkc = s * 2 + sub
            for tb in range(2):
                b = next_bank()
                pres = f"pa{b}"
                po = ps_a[:, b * 512:(b + 1) * 512]
                for dc in range(DC):
                    r.op("tensor", "matmul", po,
                         wst[sl][:, dc, sub * 128:(sub + 1) * 128],
                         x_fm[:, dc * TPC + tb * 512:dc * TPC + (tb + 1) * 512],
                         start=(dc == 0), stop=(dc == DC - 1),
                         reads=[f"wst{sl}", f"xfmt{tb * 4}", f"xfmt{tb * 4 + 1}",
                                f"xfmt{tb * 4 + 2}", f"xfmt{tb * 4 + 3}"],
                         writes=[pres], accum=(dc > 0))
                st = (dkc * 2 + tb) % 2
                r.op("vector", "tensor_scalar", ktg[st][:, :], po,
                     wkb_sb[:, dkc:dkc + 1], None, OP.add,
                     reads=[pres, "wkb_sb"], writes=[f"ktg{st}"])
                r.op("sync", "dma_start",
                     out=gin[dkc * 128:(dkc + 1) * 128, tb * 512:(tb + 1) * 512],
                     in_=ktg[st][:, :], dma=f"dkst{st}", reads=[f"ktg{st}"],
                     writes=["gin_k"], accum=(dkc + tb > 0))

    vgd = gin[1024:2048, :].rearrange("(b s16) t -> b s16 t", s16=16)
    vgd = vgd.rearrange("(h tc) s16 t -> h tc s16 t", tc=8)
    emit_stripe_load(wv, 0)
    for s in range(4):
        if s + 1 < 4:
            emit_stripe_load(wv, s + 1)
        sl = s % 2
        for tb in range(TT):
            b = next_bank()
            pres = f"pa{b}"
            po = ps_a[:, b * 512:b * 512 + 256]
            for dc in range(DC):
                r.op("tensor", "matmul", po,
                     x_fm[:, dc * TPC + tb * 128:dc * TPC + (tb + 1) * 128],
                     wst[sl][:, dc, :], start=(dc == 0), stop=False,
                     reads=[f"wst{sl}", f"xfmt{tb}"], writes=[pres], accum=(dc > 0))
            r.op("tensor", "matmul", po, onesrow[0:1, :],
                 wvbrow[0:1, s * 256:(s + 1) * 256], start=False, stop=True,
                 reads=["onesrow", "wvbrow"], writes=[pres], accum=True)
            st = tb % 2
            r.op("scalar", "activation", vtg[st][:, :], po, AF.Copy,
                 reads=[pres], writes=[f"vtg{st}"])
            dst = vgd[s * 2:(s + 1) * 2, tb, :, :].rearrange(
                "h s16 (t8 dv) -> (s16 t8) h dv", dv=128)
            r.op("sync", "dma_start", out=dst,
                 in_=vtg[st][:, :].rearrange("p (h dv) -> p h dv", dv=128),
                 dma=f"dvst{st}", reads=[f"vtg{st}"], writes=["gin_v"],
                 accum=(s + tb > 0))

    if "qkv" in tapd:
        r.op("sync", "dma_start", out=tapd["qkv"], in_=gin, dma="dtap",
             reads=["gin_k", "gin_v"])

    r.op("gpsimd", "collective_compute", "AllGather", OP.bypass,
         replica_groups=[[0, 1, 2, 3], [4, 5, 6, 7]],
         ins=[gin.opt()], outs=[gout.opt()],
         reads=["gin_k", "gin_v"], writes=["gout"], dma="dcc", cc=True)

    emit_stripe_load(wq, 0)
    for s in range(16):
        if s + 1 < 16:
            emit_stripe_load(wq, s + 1)
        sl = s % 2
        for sub in range(2):
            dqc = s * 2 + sub
            for tb in range(2):
                b = next_bank()
                pres = f"pa{b}"
                po = ps_a[:, b * 512:(b + 1) * 512]
                for dc in range(DC):
                    r.op("tensor", "matmul", po,
                         wst[sl][:, dc, sub * 128:(sub + 1) * 128],
                         x_fm[:, dc * TPC + tb * 512:dc * TPC + (tb + 1) * 512],
                         start=(dc == 0), stop=(dc == DC - 1),
                         reads=[f"wst{sl}", f"xfmt{tb * 4}", f"xfmt{tb * 4 + 1}",
                                f"xfmt{tb * 4 + 2}", f"xfmt{tb * 4 + 3}"],
                         writes=[pres], accum=(dc > 0))
                st = (dqc * 2 + tb) % 2
                r.op("vector", "tensor_scalar", qtg[st][:, :], po,
                     wqb_sb[:, dqc:dqc + 1], None, OP.add,
                     reads=[pres, "wqb_sb"], writes=[f"qtg{st}"])
                r.op("sync", "dma_start",
                     out=qsc[dqc * 128:(dqc + 1) * 128, tb * 512:(tb + 1) * 512],
                     in_=qtg[st][:, :], dma=f"dqst{st}", reads=[f"qtg{st}"],
                     writes=[f"qsc{dqc}"], accum=(tb > 0))

    gor = gout.rearrange("(rr blk) t -> rr blk t", blk=2048)
    govr = gor[:, 1024:2048, :].rearrange("rr (b s16) t -> rr b s16 t", s16=16)
    govr = govr.rearrange("rr (h tc) s16 t -> rr h tc s16 t", tc=8)

    for i in range(2):
        r.op("gpsimd", "memset", vhb[i][:, :, 128:129], 1.0, writes=[f"vones{i}"])

    def load_kv(h, sl):
        both = [f"khb{sl}", f"vhb{sl}"]
        r.op("sync", "dma_start", out=khb[sl][:, :, :],
             in_=gor[:, h * 128:(h + 1) * 128, :].rearrange("rr p t -> p rr t"),
             dma=f"dkh{sl}", reads=["gout"], writes=both)
        for rr in range(NR):
            src_v = govr[rr, h, :, :, :].rearrange("tc s (t8 dv) -> (s t8) tc dv",
                                                   dv=128)
            r.op("sync", "dma_start",
                 out=vhb[sl][:, rr * 8:(rr + 1) * 8, 0:128],
                 in_=src_v, dma=f"dkh{sl}", reads=["gout"],
                 writes=both, accum=True)

    iters = [(h, f, qb) for h in range(H) for f in range(GF) for qb in range(2)]

    def emit_h_prep(h):
        sl = h % 2
        khflat = khb[sl][:, :, :].rearrange("p rr t -> p (rr t)")
        r.op("scalar", "activation", sq8k[:, :], khflat, AF.Square,
             reads=[f"khb{sl}"], writes=["sq8k"])
        for j in range(8):
            r.op("tensor", "matmul", ps_d[0:1, 0:512], onescol[:, 0:1],
                 sq8k[:, j * 512:(j + 1) * 512], start=True, stop=True,
                 reads=["sq8k", "onescol"], writes=["pd"])
            r.op("vector", "tensor_copy", knrow[0:1, :], ps_d[0:1, 0:512],
                 reads=["pd"], writes=["knrow"])
            r.op("vector", "tensor_reduce", nsc[0:1, 0:1], knrow[0:1, :], AX.X,
                 OP.max, reads=["knrow"], writes=["nsc"])
            if j == 0:
                r.op("vector", "tensor_copy", mk2[0:1, 0:1], nsc[0:1, 0:1],
                     reads=["nsc"], writes=["mk2"])
            else:
                r.op("vector", "tensor_tensor", mk2[0:1, 0:1], mk2[0:1, 0:1],
                     nsc[0:1, 0:1], OP.max, reads=["nsc", "mk2"], writes=["mk2"])

    def emit_f_prep(h, f):
        qh = h * GF + f
        qs = qh % 2
        r.op("sync", "dma_start", out=qsl[qs][:, :],
             in_=qsc[qh * 128:(qh + 1) * 128, :], dma=f"dql{qs}",
             reads=[f"qsc{qh}"], writes=[f"qsl{qs}"])
        r.op("scalar", "activation", sq8k[:, 0:TPC], qsl[qs][:, :], AF.Square,
             reads=[f"qsl{qs}"], writes=["sq8k"])
        for j in range(2):
            r.op("tensor", "matmul", ps_d[0:1, 0:512], onescol[:, 0:1],
                 sq8k[:, j * 512:(j + 1) * 512], start=True, stop=True,
                 reads=["sq8k", "onescol"], writes=["pd"])
            r.op("vector", "tensor_copy", knrow[0:1, :], ps_d[0:1, 0:512],
                 reads=["pd"], writes=["knrow"])
            r.op("vector", "tensor_reduce", nsc[0:1, 0:1], knrow[0:1, :], AX.X,
                 OP.max, reads=["knrow"], writes=["nsc"])
            if j == 0:
                r.op("vector", "tensor_copy", mq2[0:1, 0:1], nsc[0:1, 0:1],
                     reads=["nsc"], writes=["mq2"])
            else:
                r.op("vector", "tensor_tensor", mq2[0:1, 0:1], mq2[0:1, 0:1],
                     nsc[0:1, 0:1], OP.max, reads=["nsc", "mq2"], writes=["mq2"])
        r.op("vector", "tensor_tensor", nsc[0:1, 0:1], mq2[0:1, 0:1],
             mk2[0:1, 0:1], OP.add, reads=["mq2", "mk2"], writes=["nsc"])
        r.op("vector", "tensor_scalar", nsc[0:1, 0:1], nsc[0:1, 0:1],
             -0.5 * SCALE, None, OP.mult, reads=["nsc"], writes=["nsc"],
             accum=True)
        r.op("vector", "tensor_copy", bqbf[0:1, 0:1], nsc[0:1, 0:1],
             reads=["nsc"], writes=["bqbf"])
        r.op("tensor", "matmul", ps_d[:, 256:257], onesrow[0:1, :],
             bqbf[0:1, 0:1], start=True, stop=True,
             reads=["onesrow", "bqbf"], writes=["pd"])
        r.op("vector", "tensor_copy", biasq[:, :], ps_d[:, 256:257],
             reads=["pd"], writes=["biasq"])

    def emit_scores_step(idx, j):
        h, f, qb = iters[idx]
        sl = h % 2
        qh = h * GF + f
        qs = qh % 2
        par = idx % 2
        khflat = khb[sl][:, :, :].rearrange("p rr t -> p (rr t)")
        for kc in (2 * j, 2 * j + 1):
            bk = kc % 4
            r.op("tensor", "matmul", ps_a[:, bk * 512:(bk + 1) * 512],
                 khflat[:, kc * 128:(kc + 1) * 128],
                 qsl[qs][:, qb * 512:(qb + 1) * 512],
                 start=True, stop=True,
                 reads=[f"khb{sl}", f"qsl{qs}"], writes=[f"pa{bk}"])
        g0 = 2 * j
        bk0 = g0 % 4
        r.op("scalar", "activation",
             pbuf[par][:, g0 * 512:(g0 + 2) * 512],
             ps_a[:, bk0 * 512:(bk0 + 2) * 512], AF.Exp,
             bias=biasq[:, 0:1], scale=SCALE,
             reads=[f"pa{bk0}", f"pa{bk0 + 1}", "biasq"],
             writes=[f"pb{par}_{j}"])

    def emit_av_step(idx, j):
        h, f, qb = iters[idx]
        sl = h % 2
        qh = h * GF + f
        par = idx % 2
        for m in range(8 * j, 8 * j + 8):
            qt = m // DC
            kc = m % DC
            bb_ = qt % 2
            pres = f"pb{bb_}"
            r.op("tensor", "matmul", ps_b[:, bb_ * 512:bb_ * 512 + 129],
                 pbuf[par][:, kc * 512 + qt * 128:kc * 512 + (qt + 1) * 128],
                 vhb[sl][:, kc, 0:129],
                 start=(kc == 0), stop=(kc == DC - 1),
                 reads=[f"pb{par}_{kc // 2}", f"vhb{sl}", f"vones{sl}"],
                 writes=[pres], accum=(kc > 0))
            if kc == DC - 1:
                r.op("vector", "reciprocal", recip[:, :],
                     ps_b[:, bb_ * 512 + 128:bb_ * 512 + 129],
                     reads=[pres], writes=["recip"])
                r.op("vector", "tensor_scalar", astg[:, :],
                     ps_b[:, bb_ * 512:bb_ * 512 + 128], recip[:, 0:1], None,
                     OP.mult, reads=[pres, "recip"], writes=["astg"])
                r.op("tensor", "transpose", ps_tr[:, 0:128], astg[:, :],
                     identb[:, :], reads=["astg", "identb"], writes=["ptr"])
                qoff = qh * TPC + qb * 512 + qt * 128
                r.op("vector", "tensor_copy", a_fm[:, qoff:qoff + 128],
                     ps_tr[:, 0:128], reads=["ptr"], writes=["a_fm"],
                     accum=(qh + qb + qt > 0))

    def emit_interleaved(s_idx, a_idx):
        for j in range(16):
            if s_idx is not None:
                emit_scores_step(s_idx, j)
            if a_idx is not None:
                emit_av_step(a_idx, j)

    load_kv(0, 0)
    load_kv(1, 1)
    emit_h_prep(0)
    emit_f_prep(0, 0)
    emit_interleaved(0, None)
    for idx in range(len(iters)):
        nxt = idx + 1
        if nxt < len(iters):
            hn, fn, qbn = iters[nxt]
            if qbn == 0:
                if fn == 0:
                    emit_h_prep(hn)
                emit_f_prep(hn, fn)
        emit_interleaved(nxt if nxt < len(iters) else None, idx)
        hh, ff, qq = iters[idx]
        if ff == GF - 1 and qq == 1 and hh + 2 < H:
            load_kv(hh + 2, (hh + 2) % 2)

    if "afm" in tapd:
        r.op("sync", "dma_start", out=tapd["afm"], in_=a_fm[:, :], dma="dtap",
             reads=["a_fm"])

    r.op("gpsimd", "dma_start", out=wobrow[0:1, :],
         in_=wob.rearrange("(a k) -> a k", a=1), dma="dwob", writes=["wobrow"])
    r.retouch("wst0")
    r.retouch("wst1")
    emit_stripe_load(wo, 0)
    for s in range(16):
        if s + 1 < 16:
            emit_stripe_load(wo, s + 1)
        sl = s % 2
        for tb in range(TT):
            b = next_bank()
            pres = f"pa{b}"
            po = ps_a[:, b * 512:b * 512 + 256]
            for dc in range(DC):
                r.op("tensor", "matmul", po,
                     a_fm[:, dc * TPC + tb * 128:dc * TPC + (tb + 1) * 128],
                     wst[sl][:, dc, :], start=(dc == 0), stop=False,
                     reads=[f"wst{sl}", "a_fm"], writes=[pres], accum=(dc > 0))
            r.op("tensor", "matmul", po, onesrow[0:1, :],
                 wobrow[0:1, s * 256:(s + 1) * 256], start=False, stop=True,
                 reads=["onesrow", "wobrow"], writes=[pres], accum=True)
            r.op("scalar", "activation",
                 attn_sb[:, tb * D + s * 256:tb * D + (s + 1) * 256], po, AF.Copy,
                 reads=[pres], writes=["attn_sb"], accum=(s + tb > 0))
    for tb in range(TT):
        row = attn_sb[:, tb * D:(tb + 1) * D]
        layernorm_stats(row, "attn_sb")
        r.op("vector", "tensor_scalar", row, row, bn2[:, 0:1], rstd[:, 0:1],
             OP.subtract, OP.mult, reads=["attn_sb", "bn2", "rstd"],
             writes=["attn_sb"], accum=True)
        r.op("sync", "dma_start", out=lnasc[tb * 128:(tb + 1) * 128, :], in_=row,
             dma="dlna", reads=["attn_sb"], writes=["lnasc"], accum=(tb > 0))
    if "attn" in tapd:
        r.op("sync", "dma_start", out=tapd["attn"], in_=attn_sb[:, :], dma="dtap",
             reads=["attn_sb"])

    for tb in range(TT):
        r.op("sync", "dma_start", out=x5[:, :], in_=xd[tb * 128:(tb + 1) * 128, :],
             dma="dx5", writes=["x5"])
        r.op("vector", "tensor_tensor", x5[:, :], x5[:, :],
             attn_sb[:, tb * D:(tb + 1) * D], OP.add,
             reads=["x5", "attn_sb"], writes=["x5"], accum=True)
        layernorm_stats(x5[:, :], "x5")
        r.op("vector", "tensor_scalar", ln2buf[:, :], x5[:, :], bn2[:, 0:1],
             rstd[:, 0:1], OP.subtract, OP.mult,
             reads=["x5", "bn2", "rstd"], writes=["ln2buf"])
        for dc in range(DC):
            r.op("tensor", "transpose", ps_tr[:, 0:128],
                 ln2buf[:, dc * 128:(dc + 1) * 128], identb[:, :],
                 reads=["ln2buf", "identb"], writes=["ptr"])
            r.op("vector", "tensor_copy",
                 x2_fm[:, dc * TPC + tb * 128:dc * TPC + (tb + 1) * 128],
                 ps_tr[:, 0:128], reads=["ptr"], writes=["x2_fm"],
                 accum=(tb + dc > 0))
    if "x2fm" in tapd:
        r.op("sync", "dma_start", out=tapd["x2fm"], in_=x2_fm[:, :], dma="dtap",
             reads=["x2_fm"])

    r.op("gpsimd", "dma_start", out=atsb[:, :, :],
         in_=at_.rearrange("(dc p) e -> p dc e", p=128), dma="dat", writes=["atsb"])
    r.op("gpsimd", "dma_start", out=ballt[:, :], in_=br_, dma="dbr", writes=["ballt"])
    r.op("gpsimd", "dma_start", out=bbrow[0:1, :],
         in_=bb.rearrange("(a k) -> a k", a=1), dma="dbb", writes=["bbrow"])
    for tb in range(TT):
        for dc in range(DC):
            r.op("tensor", "matmul", ps_b[:, 0:128],
                 x2_fm[:, dc * TPC + tb * 128:dc * TPC + (tb + 1) * 128],
                 atsb[:, dc, :], start=(dc == 0), stop=False,
                 reads=["x2_fm", "atsb"], writes=["pb0"], accum=(dc > 0))
            r.op("tensor", "matmul", ps_b[:, 128:136],
                 x2_fm[:, dc * TPC + tb * 128:dc * TPC + (tb + 1) * 128],
                 wrt[:, dc * 8:(dc + 1) * 8], start=False, stop=(dc == DC - 1),
                 reads=["x2_fm", "wrt"], writes=["pb0"], accum=True)
        r.op("scalar", "activation", hsb[:, :], ps_b[:, 0:136], AF.Copy,
             reads=["pb0"], writes=["hsb"])
        lg = hsb[:, 128:136]
        r.op("vector", "tensor_reduce", tm1[:, :], lg, AX.X, OP.max,
             reads=["hsb"], writes=["tm1"])
        r.op("vector", "tensor_scalar", teq1[:, :], lg, tm1[:, 0:1], None,
             OP.is_equal, reads=["hsb", "tm1"], writes=["teq1"])
        r.op("vector", "scalar_tensor_tensor", tlm[:, :], teq1[:, :], -1e30, lg,
             OP.mult, OP.add, reads=["teq1", "hsb"], writes=["tlm"])
        r.op("vector", "tensor_reduce", tm2[:, :], tlm[:, :], AX.X, OP.max,
             reads=["tlm"], writes=["tm2"])
        r.op("vector", "tensor_scalar", teq2[:, :], tlm[:, :], tm2[:, 0:1], None,
             OP.is_equal, reads=["tlm", "tm2"], writes=["teq2"])
        r.op("vector", "tensor_scalar", tm2[:, :], tm2[:, :], tm1[:, 0:1], None,
             OP.subtract, reads=["tm2", "tm1"], writes=["tm2"])
        r.op("scalar", "activation", trw2[:, :], tm2[:, :], AF.Sigmoid,
             reads=["tm2"], writes=["trw2"])
        r.op("vector", "tensor_scalar", trw1[:, :], trw2[:, :], -1.0, 1.0,
             OP.mult, OP.add, reads=["trw2"], writes=["trw1"])
        r.op("vector", "tensor_scalar", tt1[:, :], teq1[:, :], trw1[:, 0:1], None,
             OP.mult, reads=["teq1", "trw1"], writes=["tt1"])
        r.op("vector", "scalar_tensor_tensor", twts[:, :], teq2[:, :], trw2[:, 0:1],
             tt1[:, :], OP.mult, OP.add, reads=["teq2", "trw2", "tt1"],
             writes=["twts"])
        for e in range(NE):
            r.op("vector", "tensor_scalar", gtm[:, e * RK:(e + 1) * RK],
                 hsb[:, e * RK:(e + 1) * RK], twts[:, e:e + 1], LSCALE,
                 OP.mult, OP.mult, reads=["hsb", "twts"], writes=["gtm"],
                 accum=(e > 0))
        r.op("tensor", "transpose", ps_tr[:, 0:128], gtm[:, :], identb[:, :],
             reads=["gtm", "identb"], writes=["ptr"])
        r.op("vector", "tensor_copy", gfm[:, tb * 128:(tb + 1) * 128],
             ps_tr[:, 0:128], reads=["ptr"], writes=["gfm"], accum=(tb > 0))
    r.retouch("wst0")
    r.retouch("wst1")
    emit_stripe_load(wbt, 0)
    for s in range(16):
        if s + 1 < 16:
            emit_stripe_load(wbt, s + 1)
        sl = s % 2
        for tb in range(TT):
            b = next_bank()
            pres = f"pa{b}"
            po = ps_a[:, b * 512:b * 512 + 256]
            for dc in range(DC):
                r.op("tensor", "matmul", po,
                     x2_fm[:, dc * TPC + tb * 128:dc * TPC + (tb + 1) * 128],
                     wst[sl][:, dc, :], start=(dc == 0), stop=False,
                     reads=[f"wst{sl}", "x2_fm"], writes=[pres], accum=(dc > 0))
            r.op("tensor", "matmul", po, gfm[:, tb * 128:(tb + 1) * 128],
                 ballt[:, s * 256:(s + 1) * 256], start=False, stop=False,
                 reads=["gfm", "ballt"], writes=[pres], accum=True)
            r.op("tensor", "matmul", po, onesrow[0:1, :],
                 bbrow[0:1, s * 256:(s + 1) * 256], start=False, stop=True,
                 reads=["onesrow", "bbrow"], writes=[pres], accum=True)
            r.op("scalar", "activation",
                 ffn_sb[:, tb * D + s * 256:tb * D + (s + 1) * 256], po, AF.Copy,
                 reads=[pres], writes=["ffn_sb"], accum=(s + tb > 0))
    if "ffn" in tapd:
        r.op("sync", "dma_start", out=tapd["ffn"], in_=ffn_sb[:, :], dma="dtap",
             reads=["ffn_sb"])

    for tb in range(TT):
        row = ffn_sb[:, tb * D:(tb + 1) * D]
        layernorm_stats(row, "ffn_sb")
        r.op("vector", "tensor_scalar", row, row, bn2[:, 0:1], rstd[:, 0:1],
             OP.subtract, OP.mult, reads=["ffn_sb", "bn2", "rstd"],
             writes=["ffn_sb"], accum=True)
        r.op("sync", "dma_start", out=x7[:, :], in_=xd[tb * 128:(tb + 1) * 128, :],
             dma="dx7", writes=["x7"])
        r.op("sync", "dma_start", out=lna7[:, :],
             in_=lnasc[tb * 128:(tb + 1) * 128, :], dma="dl7",
             reads=["lnasc"], writes=["lna7"])
        r.op("vector", "tensor_tensor", x7[:, :], x7[:, :], lna7[:, :], OP.add,
             reads=["x7", "lna7"], writes=["x7"], accum=True)
        r.op("vector", "tensor_tensor", x7[:, :], x7[:, :], row, OP.add,
             reads=["x7", "ffn_sb"], writes=["x7"], accum=True)
        r.op("sync", "dma_start", out=outd[tb * 128:(tb + 1) * 128, :], in_=x7[:, :],
             dma="dout", reads=["x7"], writes=["outd"], accum=(tb > 0))

    r.final_wait("dout")
    if tapd:
        r.final_wait("dtap")

    r.materialize()

    with contextlib.ExitStack() as st2:
        esems = {e: st2.enter_context(nc.semaphore(f"se_{e}")) for e in ENG}
        dsems = {n: st2.enter_context(nc.semaphore(f"sd_{n}"))
                 for n in sorted(r.dma_sems)}
        block = st2.enter_context(nc.Block())

        @block.tensor
        def _(t):
            r.replay(t, "tensor", esems, dsems)

        @block.vector
        def _(v):
            r.replay(v, "vector", esems, dsems)

        @block.scalar
        def _(sc):
            r.replay(sc, "scalar", esems, dsems)

        @block.sync
        def _(sy):
            r.replay(sy, "sync", esems, dsems)

        @block.gpsimd
        def _(gp):
            r.replay(gp, "gpsimd", esems, dsems)

    stack.close()
    return nc


def kernel(L0_attn_x, L0_attn_wq_w, L0_attn_wq_b, L0_attn_wk_w, L0_attn_wk_b,
           L0_attn_wv_w, L0_attn_wv_b, L0_attn_wo_w, L0_attn_wo_b,
           L0_ffn_W_base, L0_ffn_b_base, L0_ffn_W_router, L0_ffn_A, L0_ffn_B):
    xf = np.ascontiguousarray(
        np.asarray(L0_attn_x, dtype=np.float32).reshape(NCORES * TPC, D))
    base = {
        "wq": np.ascontiguousarray(L0_attn_wq_w, dtype=np.float32),
        "wk": np.ascontiguousarray(L0_attn_wk_w, dtype=np.float32),
        "wv": np.ascontiguousarray(L0_attn_wv_w, dtype=np.float32),
        "wo": np.ascontiguousarray(L0_attn_wo_w, dtype=np.float32),
        "wqb": np.ascontiguousarray(L0_attn_wq_b, dtype=np.float32),
        "wkb": np.ascontiguousarray(L0_attn_wk_b, dtype=np.float32),
        "wvb": np.ascontiguousarray(L0_attn_wv_b, dtype=np.float32),
        "wob": np.ascontiguousarray(L0_attn_wo_b, dtype=np.float32),
        "wbt": np.ascontiguousarray(np.asarray(L0_ffn_W_base, dtype=np.float32).T),
        "bb": np.ascontiguousarray(L0_ffn_b_base, dtype=np.float32),
        "wr": np.ascontiguousarray(L0_ffn_W_router, dtype=np.float32),
        "at": np.ascontiguousarray(
            np.asarray(L0_ffn_A, dtype=np.float32).reshape(NE * RK, D).T),
        "br": np.ascontiguousarray(
            np.asarray(L0_ffn_B, dtype=np.float32).transpose(0, 2, 1).reshape(NE * RK, D)),
    }
    in_maps = []
    for c in range(NCORES):
        m = dict(base)
        m["x"] = np.ascontiguousarray(xf[c * TPC:(c + 1) * TPC])
        in_maps.append(m)
    nc = build_nc(taps=_DEBUG_TAPS)
    res = run_bass_kernel_spmd(nc, in_maps, core_ids=list(range(NCORES)))
    out = np.concatenate([res.results[c]["out"] for c in range(NCORES)], axis=0)
    return np.ascontiguousarray(out.reshape(2, SL, D), dtype=np.float32)


# version 04
# speedup vs baseline: 1.0784x; 1.0123x over previous
import math
import contextlib
import numpy as np
import concourse.bass as bass
import concourse.mybir as mybir
from concourse.alu_op_type import AluOpType as OP
from concourse.bass_utils import run_bass_kernel_spmd

F32 = mybir.dt.float32
BF16 = mybir.dt.bfloat16
AF = mybir.ActivationFunctionType
AX = mybir.AxisListType

NCORES = 8
D = 4096
DC = 32
TPC = 1024
TT = 8
KV = 1024
H = 8
GF = 4
HD = 128
SL = 4096
NR = 4
SCALE = 1.0 / math.sqrt(HD)
EPS = 1e-5
NE = 8
RK = 16
LSCALE = 2.0

ENG = ("tensor", "vector", "scalar", "sync", "gpsimd")

_DEBUG_TAPS = []


class Rec:

    def __init__(self):
        self.ops = []
        self.wl = {}
        self.r = {}
        self.spans = {}
        self.touched = set()

    def span(self, res, lo, hi):
        self.spans[res] = (lo, hi)

    def retouch(self, res):
        self.touched.discard(res)

    def op(self, eng, meth, *args, reads=(), writes=(), accum=False, dma=None,
           cc=False, **kwargs):
        i = len(self.ops)
        deps = set()
        for res in writes:
            if res not in self.touched and res in self.spans:
                lo, hi = self.spans[res]
                for other, (lo2, hi2) in self.spans.items():
                    if other == res or hi2 <= lo or hi <= lo2:
                        continue
                    deps.update(self.wl.get(other, ()))
                    deps.update(self.r.get(other, ()))
                self.touched.add(res)
        for res in reads:
            deps.update(self.wl.get(res, ()))
        if not accum:
            for res in writes:
                deps.update(self.wl.get(res, ()))
                deps.update(self.r.get(res, ()))
        for res in writes:
            if accum:
                self.wl.setdefault(res, []).append(i)
            else:
                self.wl[res] = [i]
                self.r[res] = []
        for res in reads:
            self.r.setdefault(res, []).append(i)
        self.ops.append(dict(i=i, eng=eng, meth=meth, args=args, kwargs=kwargs,
                             deps=deps, dma=dma, cc=cc, need=False, sem=None,
                             val=None))
        return i

    def final_wait(self, dma_name):
        self.ops.append(dict(i=len(self.ops), eng="sync", meth="_final_wait",
                             args=(dma_name,), kwargs={}, deps=set(), dma=None,
                             cc=False, need=False, sem=None, val=None))

    def materialize(self):
        for o in self.ops:
            best = {}
            for d in o["deps"]:
                p = self.ops[d]
                key = ("dma", p["dma"]) if p["dma"] is not None else ("eng", p["eng"])
                if key not in best or d > best[key]:
                    best[key] = d
            o["deps"] = set(best.values())
        for o in self.ops:
            for d in o["deps"]:
                self.ops[d]["need"] = True
        engct = {e: 0 for e in ENG}
        dmact = {}
        self.dma_sems = set()
        for o in self.ops:
            if o["dma"] is not None:
                step = 1 if o["cc"] else 16
                dmact[o["dma"]] = dmact.get(o["dma"], 0) + step
                o["sem"], o["val"] = ("dma", o["dma"]), dmact[o["dma"]]
                self.dma_sems.add(o["dma"])
            elif o["need"]:
                engct[o["eng"]] += 1
                o["sem"], o["val"] = ("eng", o["eng"]), engct[o["eng"]]
        hw = {e: {} for e in ENG}
        for o in self.ops:
            if o["meth"] == "_final_wait":
                o["waits"] = [(("dma", o["args"][0]), dmact[o["args"][0]])]
                continue
            waits = []
            for d in sorted(o["deps"]):
                p = self.ops[d]
                if p["sem"] is None:
                    continue
                key = p["sem"]
                cur = hw[o["eng"]].get(key, 0)
                if p["val"] > cur:
                    waits.append((key, p["val"]))
                    hw[o["eng"]][key] = p["val"]
            o["waits"] = waits

    def replay(self, engobj, eng, esems, dsems):
        for o in self.ops:
            if o["eng"] != eng:
                continue
            for (kind, name), v in o.get("waits", ()):
                engobj.wait_ge(dsems[name] if kind == "dma" else esems[name], v)
            if o["meth"] == "_final_wait":
                continue
            inst = getattr(engobj, o["meth"])(*o["args"], **o["kwargs"])
            if o["sem"] is not None:
                if o["sem"][0] == "dma":
                    inst.then_inc(dsems[o["sem"][1]], 1 if o["cc"] else 16)
                else:
                    inst.then_inc(esems[eng], 1)


def build_nc(taps=()):
    nc = bass.Bass("TRN2", target_bir_lowering=False, debug=False)

    xd = nc.dram_tensor("x", [TPC, D], F32, kind="ExternalInput").ap()
    wq = nc.dram_tensor("wq", [D, D], F32, kind="ExternalInput").ap()
    wk = nc.dram_tensor("wk", [D, KV], F32, kind="ExternalInput").ap()
    wv = nc.dram_tensor("wv", [D, KV], F32, kind="ExternalInput").ap()
    wo = nc.dram_tensor("wo", [D, D], F32, kind="ExternalInput").ap()
    wqb = nc.dram_tensor("wqb", [D], F32, kind="ExternalInput").ap()
    wkb = nc.dram_tensor("wkb", [KV], F32, kind="ExternalInput").ap()
    wvb = nc.dram_tensor("wvb", [KV], F32, kind="ExternalInput").ap()
    wob = nc.dram_tensor("wob", [D], F32, kind="ExternalInput").ap()
    wbt = nc.dram_tensor("wbt", [D, D], F32, kind="ExternalInput").ap()
    bb = nc.dram_tensor("bb", [D], F32, kind="ExternalInput").ap()
    wr = nc.dram_tensor("wr", [NE, D], F32, kind="ExternalInput").ap()
    at_ = nc.dram_tensor("at", [D, NE * RK], F32, kind="ExternalInput").ap()
    br_ = nc.dram_tensor("br", [NE * RK, D], F32, kind="ExternalInput").ap()
    outd = nc.dram_tensor("out", [TPC, D], F32, kind="ExternalOutput").ap()

    qsc = nc.dram_tensor("qscratch", [D, TPC], BF16).ap()
    gin = nc.dram_tensor("gin", [2064, KV], BF16).ap()
    gout = nc.dram_tensor("gout", [NR * 2064, KV], BF16).ap()
    lnasc = nc.dram_tensor("lnascratch", [TPC, D], BF16).ap()

    tapd = {}
    tap_shapes = {"xfm": [128, DC * TPC], "afm": [128, DC * TPC],
                  "x2fm": [128, DC * TPC], "attn": [128, TT * D],
                  "ffn": [128, TT * D], "pbuf": [128, 16384],
                  "qkv": [2048, KV]}
    for t in taps:
        tapd[t] = nc.dram_tensor("tap_" + t, tap_shapes[t], BF16,
                                 kind="ExternalOutput").ap()

    r = Rec()

    SB0 = 16512

    def sb(name, shape, dtype, off):
        t = nc.alloc_sbuf_tensor_at(name, shape, dtype, offset=SB0 + off)
        nbytes = 1
        for s in shape[1:]:
            nbytes *= s
        nbytes *= 2 if dtype == BF16 else 4
        r.span(name, off, off + nbytes)
        return t

    x_fm = sb("x_fm", [128, DC * TPC], BF16, 0)
    for _tb in range(TT):
        r.span(f"xfmt{_tb}", 0, 65536)
    a_fm = sb("a_fm", [128, DC * TPC], BF16, 0)
    x2_fm = sb("x2_fm", [128, DC * TPC], BF16, 0)
    ln1buf = sb("ln1buf", [128, D], BF16, 65536)
    wrtmp = sb("wrtmp", [NE, D], BF16, 65536 + 8192)
    pbuf = [sb(f"pbuf{i}", [128, 16384], BF16, 65536 + i * 32768) for i in range(2)]
    for _p in range(2):
        for _g in range(16):
            r.span(f"pb{_p}_{_g}", 65536 + _p * 32768 + _g * 2048,
                   65536 + _p * 32768 + (_g + 1) * 2048)
    qsl = [sb(f"qsl{i}", [128, TPC], BF16, 164864 + i * 2048) for i in range(2)]
    sq8k = sb("sq8k", [128, D], BF16, 168960)
    knrow = sb("knrow", [1, 512], F32, 177152)
    attn_sb = sb("attn_sb", [128, TT * D], BF16, 65536)
    ffn_sb = sb("ffn_sb", [128, TT * D], BF16, 65536)
    xsl = [sb(f"xsl{i}", [128, D], F32, 131072 + i * 16384) for i in range(2)]
    qtg = [sb(f"qtg{i}", [128, 512], BF16, 131072 + i * 1024) for i in range(2)]
    ktg = [sb(f"ktg{i}", [128, 512], BF16, 133120 + i * 1024) for i in range(2)]
    vtg = [sb(f"vtg{i}", [128, 256], BF16, 135168 + i * 512) for i in range(2)]
    khb = [sb(f"khb{i}", [128, NR, 1024], BF16, 131072 + i * 8192) for i in range(2)]
    vhb = [sb(f"vhb{i}", [128, DC, 130], BF16, 147456 + i * 8320) for i in range(2)]
    wobrow = sb("wobrow", [1, D], BF16, 155648)
    x5 = sb("x5", [128, D], F32, 131072)
    ln2buf = sb("ln2buf", [128, D], BF16, 147456)
    atsb = sb("atsb", [128, DC, NE * RK], BF16, 131072)
    ballt = sb("ballt", [128, D], BF16, 139264)
    bbrow = sb("bbrow", [1, D], BF16, 147456)
    x7 = sb("x7", [128, D], F32, 131072)
    lna7 = sb("lna7", [128, D], BF16, 147456)
    wst = [sb(f"wst{i}", [128, DC, 256], BF16, 164864 + i * 16384) for i in range(2)]
    M = 197632

    def msb(name, shape, dtype):
        nonlocal M
        t = sb(name, shape, dtype, M)
        nbytes = 1
        for s in shape[1:]:
            nbytes *= s
        nbytes *= 2 if dtype == BF16 else 4
        M += (nbytes + 31) // 32 * 32
        return t

    identf = msb("identf", [128, 128], F32)
    identb = msb("identb", [128, 128], BF16)
    iot0 = msb("iot0", [128, 128], F32)
    iot1 = msb("iot1", [128, 128], F32)
    wqb_sb = msb("wqb_sb", [128, 32], F32)
    wkb_sb = msb("wkb_sb", [128, 8], F32)
    braw = msb("braw", [32, 128], F32)
    onescol = msb("onescol", [128, 1], BF16)
    onesrow = msb("onesrow", [1, 128], BF16)
    wvbrow = msb("wvbrow", [1, KV], BF16)
    wrt = msb("wrt", [128, DC * NE], BF16)
    biasq = msb("biasq", [128, 1], F32)
    bqbf = msb("bqbf", [1, 1], BF16)
    mk2 = msb("mk2", [1, 1], F32)
    mq2 = msb("mq2", [1, 1], F32)
    nsc = msb("nsc", [1, 1], F32)
    bnb = msb("bnb", [128, 8, 6], F32)
    bn2 = msb("bn2", [128, 2], F32)
    sdv = msb("sdv", [128, 1], F32)
    rstd = msb("rstd", [128, 1], F32)
    recip = msb("recip", [128, 1], F32)
    astg = msb("astg", [128, 128], BF16)
    hsb = msb("hsb", [128, 136], F32)
    gtm = msb("gtm", [128, 128], BF16)
    gfm = msb("gfm", [128, TT * 128], BF16)
    tm1 = msb("tm1", [128, 1], F32)
    tm2 = msb("tm2", [128, 1], F32)
    teq1 = msb("teq1", [128, NE], F32)
    teq2 = msb("teq2", [128, NE], F32)
    tlm = msb("tlm", [128, NE], F32)
    trw2 = msb("trw2", [128, 1], F32)
    trw1 = msb("trw1", [128, 1], F32)
    tt1 = msb("tt1", [128, NE], F32)
    twts = msb("twts", [128, NE], F32)
    ksq5 = msb("ksq5", [128, 512], BF16)
    nrow = msb("nrow", [1, 512], F32)
    mklr = msb("mklr", [1, NE], F32)
    mqlr = msb("mqlr", [1, 32], F32)
    mk2f = msb("mk2f", [1, NE], F32)
    nrows4 = msb("nrows4", [1, 32], BF16)
    bqrow = msb("bqrow", [1, 32], F32)
    bqrowb = msb("bqrowb", [1, 32], BF16)
    biasqs = msb("biasqs", [128, 32], F32)
    assert SB0 + M <= 229344, M

    stack = contextlib.ExitStack()
    ps_a = stack.enter_context(nc.psum_tensor("ps_a", [128, 2048], F32))
    ps_b = stack.enter_context(nc.psum_tensor("ps_b", [128, 1024], F32))
    ps_tr = stack.enter_context(nc.psum_tensor("ps_tr", [128, 1024], BF16))
    ps_d = stack.enter_context(nc.psum_tensor("ps_d", [128, 512], F32))

    r.op("gpsimd", "iota", iot0[:, :], [[1, 128]], channel_multiplier=0,
         allow_small_or_imprecise_dtypes=True, writes=["iot0"])
    r.op("gpsimd", "iota", iot1[:, :], [[0, 128]], channel_multiplier=1,
         allow_small_or_imprecise_dtypes=True, writes=["iot1"])
    r.op("vector", "tensor_tensor", identf[:, :], iot0[:, :], iot1[:, :], OP.is_equal,
         reads=["iot0", "iot1"], writes=["identf"])
    r.op("vector", "tensor_copy", identb[:, :], identf[:, :],
         reads=["identf"], writes=["identb"])
    r.op("gpsimd", "memset", onescol[:, :], 1.0, writes=["onescol"])
    r.op("gpsimd", "memset", onesrow[:, :], 1.0, writes=["onesrow"])
    for i in range(2):
        r.span(f"vones{i}", *r.spans[f"vhb{i}"])
    r.op("sync", "dma_start", out=braw[:, :], in_=wqb.rearrange("(a p) -> a p", p=128),
         dma="dbias", writes=["braw"])
    r.op("tensor", "transpose", ps_d[:, 0:32], braw[:, :], identf[0:32, 0:32],
         reads=["braw", "identf"], writes=["pd"])
    r.op("vector", "tensor_copy", wqb_sb[:, :], ps_d[:, 0:32], reads=["pd"],
         writes=["wqb_sb"])
    r.op("sync", "dma_start", out=braw[0:8, :], in_=wkb.rearrange("(a p) -> a p", p=128),
         dma="dbias", reads=["pd"], writes=["braw"])
    r.op("tensor", "transpose", ps_d[:, 32:40], braw[0:8, :], identf[0:8, 0:8],
         reads=["braw", "identf"], writes=["pd"])
    r.op("vector", "tensor_copy", wkb_sb[:, :], ps_d[:, 32:40], reads=["pd"],
         writes=["wkb_sb"])
    r.op("gpsimd", "dma_start", out=wvbrow[0:1, :],
         in_=wvb.rearrange("(a k) -> a k", a=1), dma="dwvb", writes=["wvbrow"])
    r.op("gpsimd", "dma_start", out=wrtmp[:, :], in_=wr, dma="dwrt", writes=["wrtmp"])
    for dc in range(DC):
        r.op("tensor", "transpose", ps_tr[:, 0:8], wrtmp[:, dc * 128:(dc + 1) * 128],
             identb[0:8, 0:8], reads=["wrtmp", "identb"], writes=["ptr"])
        r.op("vector", "tensor_copy", wrt[:, dc * 8:(dc + 1) * 8], ps_tr[:, 0:8],
             reads=["ptr"], writes=["wrt"], accum=(dc > 0))

    def layernorm_stats(src_ap, src_res):
        for i in range(8):
            r.op("vector", "bn_stats", bnb[:, i, :], src_ap[:, i * 512:(i + 1) * 512],
                 reads=[src_res], writes=["bnb"], accum=(i > 0))
        r.op("vector", "bn_aggr", bn2[:, :], bnb[:, :, :], reads=["bnb"], writes=["bn2"])
        r.op("vector", "tensor_scalar", sdv[:, :], bn2[:, 1:2], EPS, None, OP.add,
             reads=["bn2"], writes=["sdv"])
        r.op("scalar", "activation", sdv[:, :], sdv[:, :], AF.Sqrt,
             reads=["sdv"], writes=["sdv"])
        r.op("vector", "reciprocal", rstd[:, :], sdv[:, :], reads=["sdv"], writes=["rstd"])

    def emit_stripe_load(wsrc, s):
        sl = s % 2
        src = wsrc.rearrange("(dc p) n -> p dc n", p=128)[:, :, s * 256:(s + 1) * 256]
        r.op("gpsimd", "dma_start", out=wst[sl][:, :, :], in_=src,
             dma=f"dw{sl}", writes=[f"wst{sl}"])
        return sl

    pctr = [0]

    def next_bank():
        b = pctr[0] % 4
        pctr[0] += 1
        return b

    for tb in range(TT):
        sl = tb % 2
        r.op("sync", "dma_start", out=xsl[sl][:, :], in_=xd[tb * 128:(tb + 1) * 128, :],
             dma=f"dx{sl}", writes=[f"xsl{sl}"])
        layernorm_stats(xsl[sl][:, :], f"xsl{sl}")
        r.op("vector", "tensor_scalar", ln1buf[:, :], xsl[sl][:, :], bn2[:, 0:1],
             rstd[:, 0:1], OP.subtract, OP.mult,
             reads=[f"xsl{sl}", "bn2", "rstd"], writes=["ln1buf"])
        for dc in range(DC):
            r.op("tensor", "transpose", ps_tr[:, 0:128],
                 ln1buf[:, dc * 128:(dc + 1) * 128], identb[:, :],
                 reads=["ln1buf", "identb"], writes=["ptr"])
            r.op("vector", "tensor_copy",
                 x_fm[:, dc * TPC + tb * 128:dc * TPC + (tb + 1) * 128],
                 ps_tr[:, 0:128], reads=["ptr"], writes=[f"xfmt{tb}"],
                 accum=(dc > 0))
    if "xfm" in tapd:
        r.op("sync", "dma_start", out=tapd["xfm"], in_=x_fm[:, :], dma="dtap",
             reads=[f"xfmt{t}" for t in range(TT)])

    emit_stripe_load(wk, 0)
    for s in range(4):
        if s + 1 < 4:
            emit_stripe_load(wk, s + 1)
        sl = s % 2
        for sub in range(2):
            dkc = s * 2 + sub
            for tb in range(2):
                b = next_bank()
                pres = f"pa{b}"
                po = ps_a[:, b * 512:(b + 1) * 512]
                for dc in range(DC):
                    r.op("tensor", "matmul", po,
                         wst[sl][:, dc, sub * 128:(sub + 1) * 128],
                         x_fm[:, dc * TPC + tb * 512:dc * TPC + (tb + 1) * 512],
                         start=(dc == 0), stop=(dc == DC - 1),
                         reads=[f"wst{sl}", f"xfmt{tb * 4}", f"xfmt{tb * 4 + 1}",
                                f"xfmt{tb * 4 + 2}", f"xfmt{tb * 4 + 3}"],
                         writes=[pres], accum=(dc > 0))
                st = (dkc * 2 + tb) % 2
                r.op("vector", "tensor_scalar", ktg[st][:, :], po,
                     wkb_sb[:, dkc:dkc + 1], None, OP.add,
                     reads=[pres, "wkb_sb"], writes=[f"ktg{st}"])
                r.op("sync", "dma_start",
                     out=gin[dkc * 128:(dkc + 1) * 128, tb * 512:(tb + 1) * 512],
                     in_=ktg[st][:, :], dma=f"dkst{st}", reads=[f"ktg{st}"],
                     writes=["gin_k"], accum=(dkc + tb > 0))
                r.op("scalar", "activation", ksq5[:, :], ktg[st][:, :], AF.Square,
                     reads=[f"ktg{st}"], writes=["ksq5"])
                r.op("tensor", "matmul", ps_d[0:1, 0:512], onescol[:, 0:1],
                     ksq5[:, :], start=True, stop=True,
                     reads=["ksq5", "onescol"], writes=["pd"])
                r.op("vector", "tensor_copy", nrow[0:1, :], ps_d[0:1, 0:512],
                     reads=["pd"], writes=["nrow"])
                r.op("vector", "tensor_reduce", nsc[0:1, 0:1], nrow[0:1, :], AX.X,
                     OP.max, reads=["nrow"], writes=["nsc"])
                if tb == 0:
                    r.op("vector", "tensor_copy", mklr[0:1, dkc:dkc + 1],
                         nsc[0:1, 0:1], reads=["nsc"], writes=["mklr"],
                         accum=(dkc > 0))
                else:
                    r.op("vector", "tensor_tensor", mklr[0:1, dkc:dkc + 1],
                         mklr[0:1, dkc:dkc + 1], nsc[0:1, 0:1], OP.max,
                         reads=["nsc", "mklr"], writes=["mklr"], accum=True)

    r.op("gpsimd", "memset", ksq5[0:16, :], 0.0, writes=["ksq5"])
    r.op("gpsimd", "dma_start", out=gin[2048:2064, 0:512], in_=ksq5[0:16, :],
         dma="dnrm", reads=["ksq5"], writes=["gin_n"])
    r.op("gpsimd", "dma_start", out=gin[2048:2064, 512:1024], in_=ksq5[0:16, :],
         dma="dnrm", reads=["ksq5"], writes=["gin_n"], accum=True)
    r.op("gpsimd", "dma_start", out=gin[2048:2049, 0:NE], in_=mklr[0:1, :],
         dma="dnrm", reads=["mklr"], writes=["gin_n"])

    vgd = gin[1024:2048, :].rearrange("(b s16) t -> b s16 t", s16=16)
    vgd = vgd.rearrange("(h tc) s16 t -> h tc s16 t", tc=8)
    emit_stripe_load(wv, 0)
    for s in range(4):
        if s + 1 < 4:
            emit_stripe_load(wv, s + 1)
        sl = s % 2
        for tb in range(TT):
            b = next_bank()
            pres = f"pa{b}"
            po = ps_a[:, b * 512:b * 512 + 256]
            for dc in range(DC):
                r.op("tensor", "matmul", po,
                     x_fm[:, dc * TPC + tb * 128:dc * TPC + (tb + 1) * 128],
                     wst[sl][:, dc, :], start=(dc == 0), stop=False,
                     reads=[f"wst{sl}", f"xfmt{tb}"], writes=[pres], accum=(dc > 0))
            r.op("tensor", "matmul", po, onesrow[0:1, :],
                 wvbrow[0:1, s * 256:(s + 1) * 256], start=False, stop=True,
                 reads=["onesrow", "wvbrow"], writes=[pres], accum=True)
            st = tb % 2
            r.op("scalar", "activation", vtg[st][:, :], po, AF.Copy,
                 reads=[pres], writes=[f"vtg{st}"])
            dst = vgd[s * 2:(s + 1) * 2, tb, :, :].rearrange(
                "h s16 (t8 dv) -> (s16 t8) h dv", dv=128)
            r.op("sync", "dma_start", out=dst,
                 in_=vtg[st][:, :].rearrange("p (h dv) -> p h dv", dv=128),
                 dma=f"dvst{st}", reads=[f"vtg{st}"], writes=["gin_v"],
                 accum=(s + tb > 0))

    if "qkv" in tapd:
        r.op("sync", "dma_start", out=tapd["qkv"], in_=gin, dma="dtap",
             reads=["gin_k", "gin_v"])

    emit_stripe_load(wq, 0)
    r.op("gpsimd", "collective_compute", "AllGather", OP.bypass,
         replica_groups=[[0, 1, 2, 3], [4, 5, 6, 7]],
         ins=[gin.opt()], outs=[gout.opt()],
         reads=["gin_k", "gin_v", "gin_n"], writes=["gout"], dma="dcc", cc=True)
    for s in range(16):
        if s + 1 < 16:
            emit_stripe_load(wq, s + 1)
        sl = s % 2
        for sub in range(2):
            dqc = s * 2 + sub
            for tb in range(2):
                b = next_bank()
                pres = f"pa{b}"
                po = ps_a[:, b * 512:(b + 1) * 512]
                for dc in range(DC):
                    r.op("tensor", "matmul", po,
                         wst[sl][:, dc, sub * 128:(sub + 1) * 128],
                         x_fm[:, dc * TPC + tb * 512:dc * TPC + (tb + 1) * 512],
                         start=(dc == 0), stop=(dc == DC - 1),
                         reads=[f"wst{sl}", f"xfmt{tb * 4}", f"xfmt{tb * 4 + 1}",
                                f"xfmt{tb * 4 + 2}", f"xfmt{tb * 4 + 3}"],
                         writes=[pres], accum=(dc > 0))
                st = (dqc * 2 + tb) % 2
                r.op("vector", "tensor_scalar", qtg[st][:, :], po,
                     wqb_sb[:, dqc:dqc + 1], None, OP.add,
                     reads=[pres, "wqb_sb"], writes=[f"qtg{st}"])
                r.op("sync", "dma_start",
                     out=qsc[dqc * 128:(dqc + 1) * 128, tb * 512:(tb + 1) * 512],
                     in_=qtg[st][:, :], dma=f"dqst{st}", reads=[f"qtg{st}"],
                     writes=[f"qsc{dqc}"], accum=(tb > 0))
                r.op("scalar", "activation", ksq5[:, :], qtg[st][:, :], AF.Square,
                     reads=[f"qtg{st}"], writes=["ksq5"])
                r.op("tensor", "matmul", ps_d[0:1, 0:512], onescol[:, 0:1],
                     ksq5[:, :], start=True, stop=True,
                     reads=["ksq5", "onescol"], writes=["pd"])
                r.op("vector", "tensor_copy", nrow[0:1, :], ps_d[0:1, 0:512],
                     reads=["pd"], writes=["nrow"])
                r.op("vector", "tensor_reduce", nsc[0:1, 0:1], nrow[0:1, :], AX.X,
                     OP.max, reads=["nrow"], writes=["nsc"])
                if tb == 0:
                    r.op("vector", "tensor_copy", mqlr[0:1, dqc:dqc + 1],
                         nsc[0:1, 0:1], reads=["nsc"], writes=["mqlr"],
                         accum=(dqc > 0))
                else:
                    r.op("vector", "tensor_tensor", mqlr[0:1, dqc:dqc + 1],
                         mqlr[0:1, dqc:dqc + 1], nsc[0:1, 0:1], OP.max,
                         reads=["nsc", "mqlr"], writes=["mqlr"], accum=True)

    gor = gout.rearrange("(rr blk) t -> rr blk t", blk=2064)
    govr = gor[:, 1024:2048, :].rearrange("rr (b s16) t -> rr b s16 t", s16=16)
    govr = govr.rearrange("rr (h tc) s16 t -> rr h tc s16 t", tc=8)

    for i in range(2):
        r.op("gpsimd", "memset", vhb[i][:, :, 128:129], 1.0, writes=[f"vones{i}"])

    def load_kv(h, sl):
        both = [f"khb{sl}", f"vhb{sl}"]
        r.op("sync", "dma_start", out=khb[sl][:, :, :],
             in_=gor[:, h * 128:(h + 1) * 128, :].rearrange("rr p t -> p rr t"),
             dma=f"dkh{sl}", reads=["gout"], writes=both)
        for rr in range(NR):
            src_v = govr[rr, h, :, :, :].rearrange("tc s (t8 dv) -> (s t8) tc dv",
                                                   dv=128)
            r.op("sync", "dma_start",
                 out=vhb[sl][:, rr * 8:(rr + 1) * 8, 0:128],
                 in_=src_v, dma=f"dkh{sl}", reads=["gout"],
                 writes=both, accum=True)

    iters = [(h, f, qb) for h in range(H) for f in range(GF) for qb in range(2)]

    r.op("sync", "dma_start", out=nrows4[0:1, :].rearrange("a (rr e) -> a rr e", e=NE),
         in_=gout.rearrange("(rr blk) t -> rr blk t", blk=2064)[:, 2048:2049, 0:NE]
         .rearrange("rr a e -> a rr e"),
         dma="dnr4", reads=["gout"], writes=["nrows4"])
    r.op("vector", "tensor_copy", mk2f[0:1, :], nrows4[0:1, 0:NE],
         reads=["nrows4"], writes=["mk2f"])
    for rr in range(1, NR):
        r.op("vector", "tensor_tensor", mk2f[0:1, :], mk2f[0:1, :],
             nrows4[0:1, rr * NE:(rr + 1) * NE], OP.max,
             reads=["nrows4", "mk2f"], writes=["mk2f"], accum=True)
    for h in range(H):
        r.op("vector", "tensor_scalar", bqrow[0:1, h * GF:(h + 1) * GF],
             mqlr[0:1, h * GF:(h + 1) * GF], mk2f[0:1, h:h + 1], None, OP.add,
             reads=["mqlr", "mk2f"], writes=["bqrow"], accum=(h > 0))
    r.op("vector", "tensor_scalar", bqrow[0:1, :], bqrow[0:1, :],
         -0.5 * SCALE, None, OP.mult, reads=["bqrow"], writes=["bqrow"], accum=True)
    r.op("vector", "tensor_copy", bqrowb[0:1, :], bqrow[0:1, :],
         reads=["bqrow"], writes=["bqrowb"])
    r.op("tensor", "matmul", ps_d[:, 0:32], onesrow[0:1, :], bqrowb[0:1, :],
         start=True, stop=True, reads=["onesrow", "bqrowb"], writes=["pd"])
    r.op("vector", "tensor_copy", biasqs[:, :], ps_d[:, 0:32],
         reads=["pd"], writes=["biasqs"])

    def emit_f_prep(h, f):
        qh = h * GF + f
        qs = qh % 2
        r.op("sync", "dma_start", out=qsl[qs][:, :],
             in_=qsc[qh * 128:(qh + 1) * 128, :], dma=f"dql{qs}",
             reads=[f"qsc{qh}"], writes=[f"qsl{qs}"])

    def emit_scores_step(idx, j):
        h, f, qb = iters[idx]
        sl = h % 2
        qh = h * GF + f
        qs = qh % 2
        par = idx % 2
        khflat = khb[sl][:, :, :].rearrange("p rr t -> p (rr t)")
        for kc in (2 * j, 2 * j + 1):
            bk = kc % 4
            r.op("tensor", "matmul", ps_a[:, bk * 512:(bk + 1) * 512],
                 khflat[:, kc * 128:(kc + 1) * 128],
                 qsl[qs][:, qb * 512:(qb + 1) * 512],
                 start=True, stop=True,
                 reads=[f"khb{sl}", f"qsl{qs}"], writes=[f"pa{bk}"])
        g0 = 2 * j
        bk0 = g0 % 4
        r.op("scalar", "activation",
             pbuf[par][:, g0 * 512:(g0 + 2) * 512],
             ps_a[:, bk0 * 512:(bk0 + 2) * 512], AF.Exp,
             bias=biasqs[:, qh:qh + 1], scale=SCALE,
             reads=[f"pa{bk0}", f"pa{bk0 + 1}", "biasqs"],
             writes=[f"pb{par}_{j}"])

    def emit_av_step(idx, j):
        h, f, qb = iters[idx]
        sl = h % 2
        qh = h * GF + f
        par = idx % 2
        for m in range(8 * j, 8 * j + 8):
            qt = m // DC
            kc = m % DC
            bb_ = qt % 2
            pres = f"pb{bb_}"
            r.op("tensor", "matmul", ps_b[:, bb_ * 512:bb_ * 512 + 129],
                 pbuf[par][:, kc * 512 + qt * 128:kc * 512 + (qt + 1) * 128],
                 vhb[sl][:, kc, 0:129],
                 start=(kc == 0), stop=(kc == DC - 1),
                 reads=[f"pb{par}_{kc // 2}", f"vhb{sl}", f"vones{sl}"],
                 writes=[pres], accum=(kc > 0))
            if kc == DC - 1:
                r.op("vector", "reciprocal", recip[:, :],
                     ps_b[:, bb_ * 512 + 128:bb_ * 512 + 129],
                     reads=[pres], writes=["recip"])
                r.op("vector", "tensor_scalar", astg[:, :],
                     ps_b[:, bb_ * 512:bb_ * 512 + 128], recip[:, 0:1], None,
                     OP.mult, reads=[pres, "recip"], writes=["astg"])
                r.op("tensor", "transpose", ps_tr[:, 0:128], astg[:, :],
                     identb[:, :], reads=["astg", "identb"], writes=["ptr"])
                qoff = qh * TPC + qb * 512 + qt * 128
                r.op("vector", "tensor_copy", a_fm[:, qoff:qoff + 128],
                     ps_tr[:, 0:128], reads=["ptr"], writes=["a_fm"],
                     accum=(qh + qb + qt > 0))

    def emit_interleaved(s_idx, a_idx):
        for j in range(16):
            if s_idx is not None:
                emit_scores_step(s_idx, j)
            if a_idx is not None:
                emit_av_step(a_idx, j)

    load_kv(0, 0)
    load_kv(1, 1)
    emit_f_prep(0, 0)
    emit_interleaved(0, None)
    for idx in range(len(iters)):
        nxt = idx + 1
        if nxt < len(iters):
            hn, fn, qbn = iters[nxt]
            if qbn == 0:
                emit_f_prep(hn, fn)
        emit_interleaved(nxt if nxt < len(iters) else None, idx)
        hh, ff, qq = iters[idx]
        if ff == GF - 1 and qq == 1 and hh + 2 < H:
            load_kv(hh + 2, (hh + 2) % 2)

    if "afm" in tapd:
        r.op("sync", "dma_start", out=tapd["afm"], in_=a_fm[:, :], dma="dtap",
             reads=["a_fm"])

    for _tb in range(TT):
        r.span(f"attnt{_tb}", 65536 + _tb * 8192, 65536 + (_tb + 1) * 8192)
        r.span(f"x2t{_tb}", 0, 65536)
        r.span(f"ffnt{_tb}", 65536 + _tb * 8192, 65536 + (_tb + 1) * 8192)

    def emit_p5(tb):
        r.op("sync", "dma_start", out=x5[:, :], in_=xd[tb * 128:(tb + 1) * 128, :],
             dma="dx5", writes=["x5"])
        r.op("vector", "tensor_tensor", x5[:, :], x5[:, :],
             attn_sb[:, tb * D:(tb + 1) * D], OP.add,
             reads=["x5", f"attnt{tb}"], writes=["x5"], accum=True)
        layernorm_stats(x5[:, :], "x5")
        r.op("vector", "tensor_scalar", ln2buf[:, :], x5[:, :], bn2[:, 0:1],
             rstd[:, 0:1], OP.subtract, OP.mult,
             reads=["x5", "bn2", "rstd"], writes=["ln2buf"])
        for dc in range(DC):
            r.op("tensor", "transpose", ps_tr[:, 0:128],
                 ln2buf[:, dc * 128:(dc + 1) * 128], identb[:, :],
                 reads=["ln2buf", "identb"], writes=["ptr"])
            r.op("vector", "tensor_copy",
                 x2_fm[:, dc * TPC + tb * 128:dc * TPC + (tb + 1) * 128],
                 ps_tr[:, 0:128], reads=["ptr"], writes=[f"x2t{tb}"],
                 accum=(dc > 0))

    r.op("gpsimd", "dma_start", out=wobrow[0:1, :],
         in_=wob.rearrange("(a k) -> a k", a=1), dma="dwob", writes=["wobrow"])
    r.retouch("wst0")
    r.retouch("wst1")
    emit_stripe_load(wo, 0)
    for s in range(16):
        if s + 1 < 16:
            emit_stripe_load(wo, s + 1)
        sl = s % 2
        for tb in range(TT):
            if s == 15 and tb >= 2:
                emit_p5(tb - 2)
            b = next_bank()
            pres = f"pa{b}"
            po = ps_a[:, b * 512:b * 512 + 256]
            for dc in range(DC):
                r.op("tensor", "matmul", po,
                     a_fm[:, dc * TPC + tb * 128:dc * TPC + (tb + 1) * 128],
                     wst[sl][:, dc, :], start=(dc == 0), stop=False,
                     reads=[f"wst{sl}", "a_fm"], writes=[pres], accum=(dc > 0))
            r.op("tensor", "matmul", po, onesrow[0:1, :],
                 wobrow[0:1, s * 256:(s + 1) * 256], start=False, stop=True,
                 reads=["onesrow", "wobrow"], writes=[pres], accum=True)
            r.op("scalar", "activation",
                 attn_sb[:, tb * D + s * 256:tb * D + (s + 1) * 256], po, AF.Copy,
                 reads=[pres], writes=[f"attnt{tb}"], accum=(s > 0))
            if s == 15:
                row = attn_sb[:, tb * D:(tb + 1) * D]
                layernorm_stats(row, f"attnt{tb}")
                r.op("vector", "tensor_scalar", row, row, bn2[:, 0:1], rstd[:, 0:1],
                     OP.subtract, OP.mult, reads=[f"attnt{tb}", "bn2", "rstd"],
                     writes=[f"attnt{tb}"], accum=True)
                r.op("sync", "dma_start", out=lnasc[tb * 128:(tb + 1) * 128, :],
                     in_=row, dma=f"dlna{tb}", reads=[f"attnt{tb}"],
                     writes=["lnasc"], accum=(tb > 0))
    emit_p5(6)
    emit_p5(7)
    if "attn" in tapd:
        r.op("sync", "dma_start", out=tapd["attn"], in_=attn_sb[:, :], dma="dtap",
             reads=[f"attnt{t}" for t in range(TT)])
    if "x2fm" in tapd:
        r.op("sync", "dma_start", out=tapd["x2fm"], in_=x2_fm[:, :], dma="dtap",
             reads=[f"x2t{t}" for t in range(TT)])

    r.op("gpsimd", "dma_start", out=atsb[:, :, :],
         in_=at_.rearrange("(dc p) e -> p dc e", p=128), dma="dat", writes=["atsb"])
    r.op("gpsimd", "dma_start", out=ballt[:, :], in_=br_, dma="dbr", writes=["ballt"])
    r.op("gpsimd", "dma_start", out=bbrow[0:1, :],
         in_=bb.rearrange("(a k) -> a k", a=1), dma="dbb", writes=["bbrow"])
    for tb in range(TT):
        for dc in range(DC):
            r.op("tensor", "matmul", ps_b[:, 0:128],
                 x2_fm[:, dc * TPC + tb * 128:dc * TPC + (tb + 1) * 128],
                 atsb[:, dc, :], start=(dc == 0), stop=False,
                 reads=[f"x2t{tb}", "atsb"], writes=["pb0"], accum=(dc > 0))
            r.op("tensor", "matmul", ps_b[:, 128:136],
                 x2_fm[:, dc * TPC + tb * 128:dc * TPC + (tb + 1) * 128],
                 wrt[:, dc * 8:(dc + 1) * 8], start=False, stop=(dc == DC - 1),
                 reads=[f"x2t{tb}", "wrt"], writes=["pb0"], accum=True)
        r.op("scalar", "activation", hsb[:, :], ps_b[:, 0:136], AF.Copy,
             reads=["pb0"], writes=["hsb"])
        lg = hsb[:, 128:136]
        r.op("vector", "tensor_reduce", tm1[:, :], lg, AX.X, OP.max,
             reads=["hsb"], writes=["tm1"])
        r.op("vector", "tensor_scalar", teq1[:, :], lg, tm1[:, 0:1], None,
             OP.is_equal, reads=["hsb", "tm1"], writes=["teq1"])
        r.op("vector", "scalar_tensor_tensor", tlm[:, :], teq1[:, :], -1e30, lg,
             OP.mult, OP.add, reads=["teq1", "hsb"], writes=["tlm"])
        r.op("vector", "tensor_reduce", tm2[:, :], tlm[:, :], AX.X, OP.max,
             reads=["tlm"], writes=["tm2"])
        r.op("vector", "tensor_scalar", teq2[:, :], tlm[:, :], tm2[:, 0:1], None,
             OP.is_equal, reads=["tlm", "tm2"], writes=["teq2"])
        r.op("vector", "tensor_scalar", tm2[:, :], tm2[:, :], tm1[:, 0:1], None,
             OP.subtract, reads=["tm2", "tm1"], writes=["tm2"])
        r.op("scalar", "activation", trw2[:, :], tm2[:, :], AF.Sigmoid,
             reads=["tm2"], writes=["trw2"])
        r.op("vector", "tensor_scalar", trw1[:, :], trw2[:, :], -1.0, 1.0,
             OP.mult, OP.add, reads=["trw2"], writes=["trw1"])
        r.op("vector", "tensor_scalar", tt1[:, :], teq1[:, :], trw1[:, 0:1], None,
             OP.mult, reads=["teq1", "trw1"], writes=["tt1"])
        r.op("vector", "scalar_tensor_tensor", twts[:, :], teq2[:, :], trw2[:, 0:1],
             tt1[:, :], OP.mult, OP.add, reads=["teq2", "trw2", "tt1"],
             writes=["twts"])
        for e in range(NE):
            r.op("vector", "tensor_scalar", gtm[:, e * RK:(e + 1) * RK],
                 hsb[:, e * RK:(e + 1) * RK], twts[:, e:e + 1], LSCALE,
                 OP.mult, OP.mult, reads=["hsb", "twts"], writes=["gtm"],
                 accum=(e > 0))
        r.op("tensor", "transpose", ps_tr[:, 0:128], gtm[:, :], identb[:, :],
             reads=["gtm", "identb"], writes=["ptr"])
        r.op("vector", "tensor_copy", gfm[:, tb * 128:(tb + 1) * 128],
             ps_tr[:, 0:128], reads=["ptr"], writes=["gfm"], accum=(tb > 0))

    def emit_p7(tb):
        row = ffn_sb[:, tb * D:(tb + 1) * D]
        layernorm_stats(row, f"ffnt{tb}")
        r.op("vector", "tensor_scalar", row, row, bn2[:, 0:1], rstd[:, 0:1],
             OP.subtract, OP.mult, reads=[f"ffnt{tb}", "bn2", "rstd"],
             writes=[f"ffnt{tb}"], accum=True)
        r.op("scalar", "dma_start", out=x7[:, :], in_=xd[tb * 128:(tb + 1) * 128, :],
             dma="dx7", writes=["x7"])
        r.op("gpsimd", "dma_start", out=lna7[:, :],
             in_=lnasc[tb * 128:(tb + 1) * 128, :], dma="dl7",
             reads=["lnasc"], writes=["lna7"])
        r.op("vector", "tensor_tensor", x7[:, :], x7[:, :], lna7[:, :], OP.add,
             reads=["x7", "lna7"], writes=["x7"], accum=True)
        r.op("vector", "tensor_tensor", x7[:, :], x7[:, :], row, OP.add,
             reads=["x7", f"ffnt{tb}"], writes=["x7"], accum=True)
        r.op("sync", "dma_start", out=outd[tb * 128:(tb + 1) * 128, :], in_=x7[:, :],
             dma="dout", reads=["x7"], writes=["outd"], accum=(tb > 0))

    r.retouch("wst0")
    r.retouch("wst1")
    emit_stripe_load(wbt, 0)
    for s in range(16):
        if s + 1 < 16:
            emit_stripe_load(wbt, s + 1)
        sl = s % 2
        for tb in range(TT):
            b = next_bank()
            pres = f"pa{b}"
            po = ps_a[:, b * 512:b * 512 + 256]
            for dc in range(DC):
                r.op("tensor", "matmul", po,
                     x2_fm[:, dc * TPC + tb * 128:dc * TPC + (tb + 1) * 128],
                     wst[sl][:, dc, :], start=(dc == 0), stop=False,
                     reads=[f"wst{sl}", f"x2t{tb}"], writes=[pres], accum=(dc > 0))
            r.op("tensor", "matmul", po, gfm[:, tb * 128:(tb + 1) * 128],
                 ballt[:, s * 256:(s + 1) * 256], start=False, stop=False,
                 reads=["gfm", "ballt"], writes=[pres], accum=True)
            r.op("tensor", "matmul", po, onesrow[0:1, :],
                 bbrow[0:1, s * 256:(s + 1) * 256], start=False, stop=True,
                 reads=["onesrow", "bbrow"], writes=[pres], accum=True)
            r.op("scalar", "activation",
                 ffn_sb[:, tb * D + s * 256:tb * D + (s + 1) * 256], po, AF.Copy,
                 reads=[pres], writes=[f"ffnt{tb}"], accum=(s > 0))
    for tb in range(TT):
        emit_p7(tb)
    if "ffn" in tapd:
        r.op("sync", "dma_start", out=tapd["ffn"], in_=ffn_sb[:, :], dma="dtap",
             reads=[f"ffnt{t}" for t in range(TT)])

    r.final_wait("dout")
    if tapd:
        r.final_wait("dtap")

    r.materialize()

    with contextlib.ExitStack() as st2:
        esems = {e: st2.enter_context(nc.semaphore(f"se_{e}")) for e in ENG}
        dsems = {n: st2.enter_context(nc.semaphore(f"sd_{n}"))
                 for n in sorted(r.dma_sems)}
        block = st2.enter_context(nc.Block())

        @block.tensor
        def _(t):
            r.replay(t, "tensor", esems, dsems)

        @block.vector
        def _(v):
            r.replay(v, "vector", esems, dsems)

        @block.scalar
        def _(sc):
            r.replay(sc, "scalar", esems, dsems)

        @block.sync
        def _(sy):
            r.replay(sy, "sync", esems, dsems)

        @block.gpsimd
        def _(gp):
            r.replay(gp, "gpsimd", esems, dsems)

    stack.close()
    return nc


def kernel(L0_attn_x, L0_attn_wq_w, L0_attn_wq_b, L0_attn_wk_w, L0_attn_wk_b,
           L0_attn_wv_w, L0_attn_wv_b, L0_attn_wo_w, L0_attn_wo_b,
           L0_ffn_W_base, L0_ffn_b_base, L0_ffn_W_router, L0_ffn_A, L0_ffn_B):
    xf = np.ascontiguousarray(
        np.asarray(L0_attn_x, dtype=np.float32).reshape(NCORES * TPC, D))
    base = {
        "wq": np.ascontiguousarray(L0_attn_wq_w, dtype=np.float32),
        "wk": np.ascontiguousarray(L0_attn_wk_w, dtype=np.float32),
        "wv": np.ascontiguousarray(L0_attn_wv_w, dtype=np.float32),
        "wo": np.ascontiguousarray(L0_attn_wo_w, dtype=np.float32),
        "wqb": np.ascontiguousarray(L0_attn_wq_b, dtype=np.float32),
        "wkb": np.ascontiguousarray(L0_attn_wk_b, dtype=np.float32),
        "wvb": np.ascontiguousarray(L0_attn_wv_b, dtype=np.float32),
        "wob": np.ascontiguousarray(L0_attn_wo_b, dtype=np.float32),
        "wbt": np.ascontiguousarray(np.asarray(L0_ffn_W_base, dtype=np.float32).T),
        "bb": np.ascontiguousarray(L0_ffn_b_base, dtype=np.float32),
        "wr": np.ascontiguousarray(L0_ffn_W_router, dtype=np.float32),
        "at": np.ascontiguousarray(
            np.asarray(L0_ffn_A, dtype=np.float32).reshape(NE * RK, D).T),
        "br": np.ascontiguousarray(
            np.asarray(L0_ffn_B, dtype=np.float32).transpose(0, 2, 1).reshape(NE * RK, D)),
    }
    in_maps = []
    for c in range(NCORES):
        m = dict(base)
        m["x"] = np.ascontiguousarray(xf[c * TPC:(c + 1) * TPC])
        in_maps.append(m)
    nc = build_nc(taps=_DEBUG_TAPS)
    res = run_bass_kernel_spmd(nc, in_maps, core_ids=list(range(NCORES)))
    out = np.concatenate([res.results[c]["out"] for c in range(NCORES)], axis=0)
    return np.ascontiguousarray(out.reshape(2, SL, D), dtype=np.float32)


# version 05
# speedup vs baseline: 1.1634x; 1.0788x over previous
import math
import contextlib
import numpy as np
import concourse.bass as bass
import concourse.mybir as mybir
from concourse.alu_op_type import AluOpType as OP
from concourse.bass_utils import run_bass_kernel_spmd

F32 = mybir.dt.float32
BF16 = mybir.dt.bfloat16
AF = mybir.ActivationFunctionType
AX = mybir.AxisListType

NCORES = 8
D = 4096
DC = 32
TPC = 1024
TT = 8
KV = 1024
H = 8
GF = 4
HD = 128
SL = 4096
NR = 4
SCALE = 1.0 / math.sqrt(HD)
EPS = 1e-5
NE = 8
RK = 16
LSCALE = 2.0

ENG = ("tensor", "vector", "scalar", "sync", "gpsimd")

_DEBUG_TAPS = []


class Rec:

    def __init__(self):
        self.ops = []
        self.wl = {}
        self.r = {}
        self.spans = {}
        self.touched = set()

    def span(self, res, lo, hi):
        self.spans[res] = (lo, hi)

    def retouch(self, res):
        self.touched.discard(res)

    def op(self, eng, meth, *args, reads=(), writes=(), accum=False, dma=None,
           cc=False, **kwargs):
        i = len(self.ops)
        deps = set()
        for res in writes:
            if res not in self.touched and res in self.spans:
                lo, hi = self.spans[res]
                for other, (lo2, hi2) in self.spans.items():
                    if other == res or hi2 <= lo or hi <= lo2:
                        continue
                    deps.update(self.wl.get(other, ()))
                    deps.update(self.r.get(other, ()))
                self.touched.add(res)
        for res in reads:
            deps.update(self.wl.get(res, ()))
        if not accum:
            for res in writes:
                deps.update(self.wl.get(res, ()))
                deps.update(self.r.get(res, ()))
        for res in writes:
            if accum:
                self.wl.setdefault(res, []).append(i)
            else:
                self.wl[res] = [i]
                self.r[res] = []
        for res in reads:
            self.r.setdefault(res, []).append(i)
        self.ops.append(dict(i=i, eng=eng, meth=meth, args=args, kwargs=kwargs,
                             deps=deps, dma=dma, cc=cc, need=False, sem=None,
                             val=None))
        return i

    def final_wait(self, dma_name):
        self.ops.append(dict(i=len(self.ops), eng="sync", meth="_final_wait",
                             args=(dma_name,), kwargs={}, deps=set(), dma=None,
                             cc=False, need=False, sem=None, val=None))

    def materialize(self):
        for o in self.ops:
            best = {}
            for d in o["deps"]:
                p = self.ops[d]
                key = ("dma", p["dma"]) if p["dma"] is not None else ("eng", p["eng"])
                if key not in best or d > best[key]:
                    best[key] = d
            o["deps"] = set(best.values())
        for o in self.ops:
            for d in o["deps"]:
                self.ops[d]["need"] = True
        engct = {e: 0 for e in ENG}
        dmact = {}
        self.dma_sems = set()
        for o in self.ops:
            if o["dma"] is not None:
                step = 1 if o["cc"] else 16
                dmact[o["dma"]] = dmact.get(o["dma"], 0) + step
                o["sem"], o["val"] = ("dma", o["dma"]), dmact[o["dma"]]
                self.dma_sems.add(o["dma"])
            elif o["need"]:
                engct[o["eng"]] += 1
                o["sem"], o["val"] = ("eng", o["eng"]), engct[o["eng"]]
        hw = {e: {} for e in ENG}
        for o in self.ops:
            if o["meth"] == "_final_wait":
                o["waits"] = [(("dma", o["args"][0]), dmact[o["args"][0]])]
                continue
            waits = []
            for d in sorted(o["deps"]):
                p = self.ops[d]
                if p["sem"] is None:
                    continue
                key = p["sem"]
                cur = hw[o["eng"]].get(key, 0)
                if p["val"] > cur:
                    waits.append((key, p["val"]))
                    hw[o["eng"]][key] = p["val"]
            o["waits"] = waits

    def replay(self, engobj, eng, esems, dsems):
        for o in self.ops:
            if o["eng"] != eng:
                continue
            for (kind, name), v in o.get("waits", ()):
                engobj.wait_ge(dsems[name] if kind == "dma" else esems[name], v)
            if o["meth"] == "_final_wait":
                continue
            inst = getattr(engobj, o["meth"])(*o["args"], **o["kwargs"])
            if o["sem"] is not None:
                if o["sem"][0] == "dma":
                    inst.then_inc(dsems[o["sem"][1]], 1 if o["cc"] else 16)
                else:
                    inst.then_inc(esems[eng], 1)


def build_nc(taps=()):
    nc = bass.Bass("TRN2", target_bir_lowering=False, debug=False)

    xd = nc.dram_tensor("x", [TPC, D], F32, kind="ExternalInput").ap()
    wq = nc.dram_tensor("wq", [D, D], F32, kind="ExternalInput").ap()
    wk = nc.dram_tensor("wk", [D, KV], F32, kind="ExternalInput").ap()
    wv = nc.dram_tensor("wv", [D, KV], F32, kind="ExternalInput").ap()
    wo = nc.dram_tensor("wo", [D, D], F32, kind="ExternalInput").ap()
    wqb = nc.dram_tensor("wqb", [D], F32, kind="ExternalInput").ap()
    wkb = nc.dram_tensor("wkb", [KV], F32, kind="ExternalInput").ap()
    wvb = nc.dram_tensor("wvb", [KV], F32, kind="ExternalInput").ap()
    wob = nc.dram_tensor("wob", [D], F32, kind="ExternalInput").ap()
    wbt = nc.dram_tensor("wbt", [D, D], F32, kind="ExternalInput").ap()
    bb = nc.dram_tensor("bb", [D], F32, kind="ExternalInput").ap()
    wr = nc.dram_tensor("wr", [NE, D], F32, kind="ExternalInput").ap()
    at_ = nc.dram_tensor("at", [D, NE * RK], F32, kind="ExternalInput").ap()
    br_ = nc.dram_tensor("br", [NE * RK, D], F32, kind="ExternalInput").ap()
    outd = nc.dram_tensor("out", [TPC, D], F32, kind="ExternalOutput").ap()

    qsc = nc.dram_tensor("qscratch", [D, TPC], BF16).ap()
    gin = nc.dram_tensor("gin", [2064, KV], BF16).ap()
    gout = nc.dram_tensor("gout", [NR * 2064, KV], BF16).ap()
    lnasc = nc.dram_tensor("lnascratch", [TPC, D], BF16).ap()

    tapd = {}
    tap_shapes = {"xfm": [128, DC * TPC], "afm": [128, DC * TPC],
                  "x2fm": [128, DC * TPC], "attn": [128, TT * D],
                  "ffn": [128, TT * D], "pbuf": [128, 16384],
                  "qkv": [2048, KV]}
    for t in taps:
        tapd[t] = nc.dram_tensor("tap_" + t, tap_shapes[t], BF16,
                                 kind="ExternalOutput").ap()

    r = Rec()

    SB0 = 16512

    def sb(name, shape, dtype, off):
        t = nc.alloc_sbuf_tensor_at(name, shape, dtype, offset=SB0 + off)
        nbytes = 1
        for s in shape[1:]:
            nbytes *= s
        nbytes *= 2 if dtype == BF16 else 4
        r.span(name, off, off + nbytes)
        return t

    x_fm = sb("x_fm", [128, DC * TPC], BF16, 0)
    for _tb in range(TT):
        r.span(f"xfmt{_tb}", 0, 65536)
    a_fm = sb("a_fm", [128, DC * TPC], BF16, 0)
    x2_fm = sb("x2_fm", [128, DC * TPC], BF16, 0)
    ln1bufs = [sb(f"ln1buf{i}", [128, D], BF16, 65536 + i * 8192) for i in range(2)]
    wrtmp = sb("wrtmp", [NE, D], BF16, 65536 + 8192)
    pbuf = [sb(f"pbuf{i}", [128, 16384], BF16, 65536 + i * 32768) for i in range(2)]
    for _p in range(2):
        for _g in range(16):
            r.span(f"pb{_p}_{_g}", 65536 + _p * 32768 + _g * 2048,
                   65536 + _p * 32768 + (_g + 1) * 2048)
    qsl = [sb(f"qsl{i}", [128, TPC], BF16, 164864 + i * 2048) for i in range(2)]
    sq8k = sb("sq8k", [128, D], BF16, 168960)
    knrow = sb("knrow", [1, 512], F32, 177152)
    attn_sb = sb("attn_sb", [128, TT * D], BF16, 65536)
    ffn_sb = sb("ffn_sb", [128, TT * D], BF16, 65536)
    xsl = [sb(f"xsl{i}", [128, D], F32, 131072 + i * 16384) for i in range(2)]
    qtg = [sb(f"qtg{i}", [128, 512], BF16, 131072 + i * 1024) for i in range(2)]
    ktg = [sb(f"ktg{i}", [128, 512], BF16, 133120 + i * 1024) for i in range(2)]
    vtg = [sb(f"vtg{i}", [128, 256], BF16, 135168 + i * 512) for i in range(2)]
    khb = [sb(f"khb{i}", [128, NR, 1024], BF16, 131072 + i * 8192) for i in range(2)]
    vhb = [sb(f"vhb{i}", [128, DC, 130], BF16, 147456 + i * 8320) for i in range(2)]
    wobrow = sb("wobrow", [1, D], BF16, 155648)
    x5 = sb("x5", [128, D], F32, 131072)
    ln2buf = sb("ln2buf", [128, D], BF16, 147456)
    atsb = sb("atsb", [128, DC, NE * RK], BF16, 131072)
    ballt = sb("ballt", [128, D], BF16, 139264)
    bbrow = sb("bbrow", [1, D], BF16, 147456)
    x7 = sb("x7", [128, D], F32, 131072)
    lna7 = sb("lna7", [128, D], BF16, 147456)
    wst = [sb(f"wst{i}", [128, DC, 256], BF16, 164864 + i * 16384) for i in range(2)]
    M = 197632

    def msb(name, shape, dtype):
        nonlocal M
        t = sb(name, shape, dtype, M)
        nbytes = 1
        for s in shape[1:]:
            nbytes *= s
        nbytes *= 2 if dtype == BF16 else 4
        M += (nbytes + 31) // 32 * 32
        return t

    identf = msb("identf", [128, 128], F32)
    identb = msb("identb", [128, 128], BF16)
    iot0 = msb("iot0", [128, 128], F32)
    iot1 = msb("iot1", [128, 128], F32)
    wqb_sb = msb("wqb_sb", [128, 32], F32)
    wkb_sb = msb("wkb_sb", [128, 8], F32)
    braw = msb("braw", [32, 128], F32)
    onescol = msb("onescol", [128, 1], BF16)
    onesrow = msb("onesrow", [1, 128], BF16)
    wvbrow = msb("wvbrow", [1, KV], BF16)
    wrt = msb("wrt", [128, DC * NE], BF16)
    biasq = msb("biasq", [128, 1], F32)
    bqbf = msb("bqbf", [1, 1], BF16)
    mk2 = msb("mk2", [1, 1], F32)
    mq2 = msb("mq2", [1, 1], F32)
    nsc = msb("nsc", [1, 1], F32)
    bnb = msb("bnb", [128, 8, 6], F32)
    bn2 = msb("bn2", [128, 2], F32)
    sdv = msb("sdv", [128, 1], F32)
    rstd = msb("rstd", [128, 1], F32)
    recip = msb("recip", [128, 1], F32)
    astg = msb("astg", [128, 128], BF16)
    hsb = msb("hsb", [128, 136], F32)
    gtm = msb("gtm", [128, 128], BF16)
    gfm = msb("gfm", [128, TT * 128], BF16)
    tm1 = msb("tm1", [128, 1], F32)
    tm2 = msb("tm2", [128, 1], F32)
    teq1 = msb("teq1", [128, NE], F32)
    teq2 = msb("teq2", [128, NE], F32)
    tlm = msb("tlm", [128, NE], F32)
    trw2 = msb("trw2", [128, 1], F32)
    trw1 = msb("trw1", [128, 1], F32)
    tt1 = msb("tt1", [128, NE], F32)
    twts = msb("twts", [128, NE], F32)
    ksq5 = msb("ksq5", [128, 512], BF16)
    nrow = msb("nrow", [1, 512], F32)
    mklr = msb("mklr", [1, NE], F32)
    mqlr = msb("mqlr", [1, 32], F32)
    mk2f = msb("mk2f", [1, NE], F32)
    nrows4 = msb("nrows4", [1, 32], BF16)
    bqrow = msb("bqrow", [1, 32], F32)
    bqrowb = msb("bqrowb", [1, 32], BF16)
    biasqs = msb("biasqs", [128, 32], F32)
    asum = msb("asum", [128, TT, 16], F32)
    asq = msb("asq", [128, TT, 16], F32)
    sqscrap = msb("sqscrap", [128, 256], BF16)
    exv = msb("exv", [128, 1], F32)
    ex2 = msb("ex2", [128, 1], F32)
    assert SB0 + M <= 229344, M

    stack = contextlib.ExitStack()
    ps_a = stack.enter_context(nc.psum_tensor("ps_a", [128, 2048], F32))
    ps_b = stack.enter_context(nc.psum_tensor("ps_b", [128, 1024], F32))
    ps_tr = stack.enter_context(nc.psum_tensor("ps_tr", [128, 1024], BF16))
    ps_d = stack.enter_context(nc.psum_tensor("ps_d", [128, 512], F32))

    r.op("gpsimd", "iota", iot0[:, :], [[1, 128]], channel_multiplier=0,
         allow_small_or_imprecise_dtypes=True, writes=["iot0"])
    r.op("gpsimd", "iota", iot1[:, :], [[0, 128]], channel_multiplier=1,
         allow_small_or_imprecise_dtypes=True, writes=["iot1"])
    r.op("vector", "tensor_tensor", identf[:, :], iot0[:, :], iot1[:, :], OP.is_equal,
         reads=["iot0", "iot1"], writes=["identf"])
    r.op("vector", "tensor_copy", identb[:, :], identf[:, :],
         reads=["identf"], writes=["identb"])
    r.op("gpsimd", "memset", onescol[:, :], 1.0, writes=["onescol"])
    r.op("gpsimd", "memset", onesrow[:, :], 1.0, writes=["onesrow"])
    for i in range(2):
        r.span(f"vones{i}", *r.spans[f"vhb{i}"])
    r.op("sync", "dma_start", out=braw[:, :], in_=wqb.rearrange("(a p) -> a p", p=128),
         dma="dbias", writes=["braw"])
    r.op("tensor", "transpose", ps_d[:, 0:32], braw[:, :], identf[0:32, 0:32],
         reads=["braw", "identf"], writes=["pd"])
    r.op("vector", "tensor_copy", wqb_sb[:, :], ps_d[:, 0:32], reads=["pd"],
         writes=["wqb_sb"])
    r.op("sync", "dma_start", out=braw[0:8, :], in_=wkb.rearrange("(a p) -> a p", p=128),
         dma="dbias", reads=["pd"], writes=["braw"])
    r.op("tensor", "transpose", ps_d[:, 32:40], braw[0:8, :], identf[0:8, 0:8],
         reads=["braw", "identf"], writes=["pd"])
    r.op("vector", "tensor_copy", wkb_sb[:, :], ps_d[:, 32:40], reads=["pd"],
         writes=["wkb_sb"])
    r.op("gpsimd", "dma_start", out=wvbrow[0:1, :],
         in_=wvb.rearrange("(a k) -> a k", a=1), dma="dwvb", writes=["wvbrow"])
    r.op("gpsimd", "dma_start", out=wrtmp[:, :], in_=wr, dma="dwrt", writes=["wrtmp"])
    for dc in range(DC):
        r.op("tensor", "transpose", ps_tr[:, 0:8], wrtmp[:, dc * 128:(dc + 1) * 128],
             identb[0:8, 0:8], reads=["wrtmp", "identb"], writes=["ptr"])
        r.op("vector", "tensor_copy", wrt[:, dc * 8:(dc + 1) * 8], ps_tr[:, 0:8],
             reads=["ptr"], writes=["wrt"], accum=(dc > 0))

    def layernorm_stats(src_ap, src_res):
        for i in range(8):
            r.op("vector", "bn_stats", bnb[:, i, :], src_ap[:, i * 512:(i + 1) * 512],
                 reads=[src_res], writes=["bnb"], accum=(i > 0))
        r.op("vector", "bn_aggr", bn2[:, :], bnb[:, :, :], reads=["bnb"], writes=["bn2"])
        r.op("vector", "tensor_scalar", sdv[:, :], bn2[:, 1:2], EPS, None, OP.add,
             reads=["bn2"], writes=["sdv"])
        r.op("scalar", "activation", sdv[:, :], sdv[:, :], AF.Sqrt,
             reads=["sdv"], writes=["sdv"])
        r.op("vector", "reciprocal", rstd[:, :], sdv[:, :], reads=["sdv"], writes=["rstd"])

    def emit_stripe_load(wsrc, s):
        sl = s % 2
        src = wsrc.rearrange("(dc p) n -> p dc n", p=128)[:, :, s * 256:(s + 1) * 256]
        r.op("gpsimd", "dma_start", out=wst[sl][:, :, :], in_=src,
             dma=f"dw{sl}", writes=[f"wst{sl}"])
        return sl

    pctr = [0]

    def next_bank():
        b = pctr[0] % 4
        pctr[0] += 1
        return b

    for tb in range(TT):
        sl = tb % 2
        r.op("sync", "dma_start", out=xsl[sl][:, :], in_=xd[tb * 128:(tb + 1) * 128, :],
             dma=f"dx{sl}", writes=[f"xsl{sl}"])
        layernorm_stats(xsl[sl][:, :], f"xsl{sl}")
        r.op("vector", "tensor_scalar", ln1bufs[sl][:, :], xsl[sl][:, :], bn2[:, 0:1],
             rstd[:, 0:1], OP.subtract, OP.mult,
             reads=[f"xsl{sl}", "bn2", "rstd"], writes=[f"ln1buf{sl}"])
        for dc in range(DC):
            r.op("tensor", "transpose", ps_tr[:, 0:128],
                 ln1bufs[sl][:, dc * 128:(dc + 1) * 128], identb[:, :],
                 reads=[f"ln1buf{sl}", "identb"], writes=["ptr"])
            r.op("vector", "tensor_copy",
                 x_fm[:, dc * TPC + tb * 128:dc * TPC + (tb + 1) * 128],
                 ps_tr[:, 0:128], reads=["ptr"], writes=[f"xfmt{tb}"],
                 accum=(dc > 0))
    if "xfm" in tapd:
        r.op("sync", "dma_start", out=tapd["xfm"], in_=x_fm[:, :], dma="dtap",
             reads=[f"xfmt{t}" for t in range(TT)])

    emit_stripe_load(wk, 0)
    for s in range(4):
        if s + 1 < 4:
            emit_stripe_load(wk, s + 1)
        sl = s % 2
        for sub in range(2):
            dkc = s * 2 + sub
            for tb in range(2):
                b = next_bank()
                pres = f"pa{b}"
                po = ps_a[:, b * 512:(b + 1) * 512]
                for dc in range(DC):
                    r.op("tensor", "matmul", po,
                         wst[sl][:, dc, sub * 128:(sub + 1) * 128],
                         x_fm[:, dc * TPC + tb * 512:dc * TPC + (tb + 1) * 512],
                         start=(dc == 0), stop=(dc == DC - 1),
                         reads=[f"wst{sl}", f"xfmt{tb * 4}", f"xfmt{tb * 4 + 1}",
                                f"xfmt{tb * 4 + 2}", f"xfmt{tb * 4 + 3}"],
                         writes=[pres], accum=(dc > 0))
                st = (dkc * 2 + tb) % 2
                r.op("vector", "tensor_scalar", ktg[st][:, :], po,
                     wkb_sb[:, dkc:dkc + 1], None, OP.add,
                     reads=[pres, "wkb_sb"], writes=[f"ktg{st}"])
                r.op("sync", "dma_start",
                     out=gin[dkc * 128:(dkc + 1) * 128, tb * 512:(tb + 1) * 512],
                     in_=ktg[st][:, :], dma=f"dkst{st}", reads=[f"ktg{st}"],
                     writes=["gin_k"], accum=(dkc + tb > 0))
                r.op("scalar", "activation", ksq5[:, :], ktg[st][:, :], AF.Square,
                     reads=[f"ktg{st}"], writes=["ksq5"])
                r.op("tensor", "matmul", ps_d[0:1, 0:512], onescol[:, 0:1],
                     ksq5[:, :], start=True, stop=True,
                     reads=["ksq5", "onescol"], writes=["pd"])
                r.op("vector", "tensor_copy", nrow[0:1, :], ps_d[0:1, 0:512],
                     reads=["pd"], writes=["nrow"])
                r.op("vector", "tensor_reduce", nsc[0:1, 0:1], nrow[0:1, :], AX.X,
                     OP.max, reads=["nrow"], writes=["nsc"])
                if tb == 0:
                    r.op("vector", "tensor_copy", mklr[0:1, dkc:dkc + 1],
                         nsc[0:1, 0:1], reads=["nsc"], writes=["mklr"],
                         accum=(dkc > 0))
                else:
                    r.op("vector", "tensor_tensor", mklr[0:1, dkc:dkc + 1],
                         mklr[0:1, dkc:dkc + 1], nsc[0:1, 0:1], OP.max,
                         reads=["nsc", "mklr"], writes=["mklr"], accum=True)

    vgd = gin[1024:2048, :].rearrange("(b s16) t -> b s16 t", s16=16)
    vgd = vgd.rearrange("(h tc) s16 t -> h tc s16 t", tc=8)
    emit_stripe_load(wv, 0)
    for s in range(4):
        if s + 1 < 4:
            emit_stripe_load(wv, s + 1)
        sl = s % 2
        for tb in range(TT):
            b = next_bank()
            pres = f"pa{b}"
            po = ps_a[:, b * 512:b * 512 + 256]
            for dc in range(DC):
                r.op("tensor", "matmul", po,
                     x_fm[:, dc * TPC + tb * 128:dc * TPC + (tb + 1) * 128],
                     wst[sl][:, dc, :], start=(dc == 0), stop=False,
                     reads=[f"wst{sl}", f"xfmt{tb}"], writes=[pres], accum=(dc > 0))
            r.op("tensor", "matmul", po, onesrow[0:1, :],
                 wvbrow[0:1, s * 256:(s + 1) * 256], start=False, stop=True,
                 reads=["onesrow", "wvbrow"], writes=[pres], accum=True)
            st = tb % 2
            r.op("scalar", "activation", vtg[st][:, :], po, AF.Copy,
                 reads=[pres], writes=[f"vtg{st}"])
            dst = vgd[s * 2:(s + 1) * 2, tb, :, :].rearrange(
                "h s16 (t8 dv) -> (s16 t8) h dv", dv=128)
            r.op("sync", "dma_start", out=dst,
                 in_=vtg[st][:, :].rearrange("p (h dv) -> p h dv", dv=128),
                 dma=f"dvst{st}", reads=[f"vtg{st}"], writes=["gin_v"],
                 accum=(s + tb > 0))

    if "qkv" in tapd:
        r.op("sync", "dma_start", out=tapd["qkv"], in_=gin, dma="dtap",
             reads=["gin_k", "gin_v"])

    emit_stripe_load(wq, 0)
    r.op("gpsimd", "memset", ksq5[0:16, :], 0.0, writes=["ksq5"])
    r.op("gpsimd", "dma_start", out=gin[2048:2064, 0:512], in_=ksq5[0:16, :],
         dma="dnrm", reads=["ksq5"], writes=["gin_n"])
    r.op("gpsimd", "dma_start", out=gin[2048:2064, 512:1024], in_=ksq5[0:16, :],
         dma="dnrm", reads=["ksq5"], writes=["gin_n"], accum=True)
    r.op("gpsimd", "dma_start", out=gin[2048:2049, 0:NE], in_=mklr[0:1, :],
         dma="dnrm", reads=["mklr"], writes=["gin_n"])

    r.op("gpsimd", "collective_compute", "AllGather", OP.bypass,
         replica_groups=[[0, 1, 2, 3], [4, 5, 6, 7]],
         ins=[gin.opt()], outs=[gout.opt()],
         reads=["gin_k", "gin_v", "gin_n"], writes=["gout"], dma="dcc", cc=True)
    for s in range(16):
        if s + 1 < 16:
            emit_stripe_load(wq, s + 1)
        sl = s % 2
        for sub in range(2):
            dqc = s * 2 + sub
            for tb in range(2):
                b = next_bank()
                pres = f"pa{b}"
                po = ps_a[:, b * 512:(b + 1) * 512]
                for dc in range(DC):
                    r.op("tensor", "matmul", po,
                         wst[sl][:, dc, sub * 128:(sub + 1) * 128],
                         x_fm[:, dc * TPC + tb * 512:dc * TPC + (tb + 1) * 512],
                         start=(dc == 0), stop=(dc == DC - 1),
                         reads=[f"wst{sl}", f"xfmt{tb * 4}", f"xfmt{tb * 4 + 1}",
                                f"xfmt{tb * 4 + 2}", f"xfmt{tb * 4 + 3}"],
                         writes=[pres], accum=(dc > 0))
                st = (dqc * 2 + tb) % 2
                r.op("vector", "tensor_scalar", qtg[st][:, :], po,
                     wqb_sb[:, dqc:dqc + 1], None, OP.add,
                     reads=[pres, "wqb_sb"], writes=[f"qtg{st}"])
                r.op("sync", "dma_start",
                     out=qsc[dqc * 128:(dqc + 1) * 128, tb * 512:(tb + 1) * 512],
                     in_=qtg[st][:, :], dma=f"dqst{st}", reads=[f"qtg{st}"],
                     writes=[f"qsc{dqc}"], accum=(tb > 0))
                r.op("scalar", "activation", ksq5[:, :], qtg[st][:, :], AF.Square,
                     reads=[f"qtg{st}"], writes=["ksq5"])
                r.op("tensor", "matmul", ps_d[0:1, 0:512], onescol[:, 0:1],
                     ksq5[:, :], start=True, stop=True,
                     reads=["ksq5", "onescol"], writes=["pd"])
                r.op("vector", "tensor_copy", nrow[0:1, :], ps_d[0:1, 0:512],
                     reads=["pd"], writes=["nrow"])
                r.op("vector", "tensor_reduce", nsc[0:1, 0:1], nrow[0:1, :], AX.X,
                     OP.max, reads=["nrow"], writes=["nsc"])
                if tb == 0:
                    r.op("vector", "tensor_copy", mqlr[0:1, dqc:dqc + 1],
                         nsc[0:1, 0:1], reads=["nsc"], writes=["mqlr"],
                         accum=(dqc > 0))
                else:
                    r.op("vector", "tensor_tensor", mqlr[0:1, dqc:dqc + 1],
                         mqlr[0:1, dqc:dqc + 1], nsc[0:1, 0:1], OP.max,
                         reads=["nsc", "mqlr"], writes=["mqlr"], accum=True)

    gor = gout.rearrange("(rr blk) t -> rr blk t", blk=2064)
    govr = gor[:, 1024:2048, :].rearrange("rr (b s16) t -> rr b s16 t", s16=16)
    govr = govr.rearrange("rr (h tc) s16 t -> rr h tc s16 t", tc=8)

    for i in range(2):
        r.op("gpsimd", "memset", vhb[i][:, :, 128:129], 1.0, writes=[f"vones{i}"])

    def load_kv(h, sl):
        both = [f"khb{sl}", f"vhb{sl}"]
        r.op("sync", "dma_start", out=khb[sl][:, :, :],
             in_=gor[:, h * 128:(h + 1) * 128, :].rearrange("rr p t -> p rr t"),
             dma=f"dkh{sl}", reads=["gout"], writes=both)
        for rr in range(NR):
            src_v = govr[rr, h, :, :, :].rearrange("tc s (t8 dv) -> (s t8) tc dv",
                                                   dv=128)
            r.op("sync", "dma_start",
                 out=vhb[sl][:, rr * 8:(rr + 1) * 8, 0:128],
                 in_=src_v, dma=f"dkh{sl}", reads=["gout"],
                 writes=both, accum=True)

    iters = [(h, f, qb) for h in range(H) for f in range(GF) for qb in range(2)]

    r.op("sync", "dma_start", out=nrows4[0:1, :].rearrange("a (rr e) -> a rr e", e=NE),
         in_=gout.rearrange("(rr blk) t -> rr blk t", blk=2064)[:, 2048:2049, 0:NE]
         .rearrange("rr a e -> a rr e"),
         dma="dnr4", reads=["gout"], writes=["nrows4"])
    r.op("vector", "tensor_copy", mk2f[0:1, :], nrows4[0:1, 0:NE],
         reads=["nrows4"], writes=["mk2f"])
    for rr in range(1, NR):
        r.op("vector", "tensor_tensor", mk2f[0:1, :], mk2f[0:1, :],
             nrows4[0:1, rr * NE:(rr + 1) * NE], OP.max,
             reads=["nrows4", "mk2f"], writes=["mk2f"], accum=True)
    for h in range(H):
        r.op("vector", "tensor_scalar", bqrow[0:1, h * GF:(h + 1) * GF],
             mqlr[0:1, h * GF:(h + 1) * GF], mk2f[0:1, h:h + 1], None, OP.add,
             reads=["mqlr", "mk2f"], writes=["bqrow"], accum=(h > 0))
    r.op("vector", "tensor_scalar", bqrow[0:1, :], bqrow[0:1, :],
         -0.5 * SCALE, None, OP.mult, reads=["bqrow"], writes=["bqrow"], accum=True)
    r.op("vector", "tensor_copy", bqrowb[0:1, :], bqrow[0:1, :],
         reads=["bqrow"], writes=["bqrowb"])
    r.op("tensor", "matmul", ps_d[:, 0:32], onesrow[0:1, :], bqrowb[0:1, :],
         start=True, stop=True, reads=["onesrow", "bqrowb"], writes=["pd"])
    r.op("vector", "tensor_copy", biasqs[:, :], ps_d[:, 0:32],
         reads=["pd"], writes=["biasqs"])

    def emit_f_prep(h, f):
        qh = h * GF + f
        qs = qh % 2
        r.op("sync", "dma_start", out=qsl[qs][:, :],
             in_=qsc[qh * 128:(qh + 1) * 128, :], dma=f"dql{qs}",
             reads=[f"qsc{qh}"], writes=[f"qsl{qs}"])

    def emit_scores_step(idx, j):
        h, f, qb = iters[idx]
        sl = h % 2
        qh = h * GF + f
        qs = qh % 2
        par = idx % 2
        khflat = khb[sl][:, :, :].rearrange("p rr t -> p (rr t)")
        for kc in (2 * j, 2 * j + 1):
            bk = kc % 4
            r.op("tensor", "matmul", ps_a[:, bk * 512:(bk + 1) * 512],
                 khflat[:, kc * 128:(kc + 1) * 128],
                 qsl[qs][:, qb * 512:(qb + 1) * 512],
                 start=True, stop=True,
                 reads=[f"khb{sl}", f"qsl{qs}"], writes=[f"pa{bk}"])
        g0 = 2 * j
        bk0 = g0 % 4
        r.op("scalar", "activation",
             pbuf[par][:, g0 * 512:(g0 + 2) * 512],
             ps_a[:, bk0 * 512:(bk0 + 2) * 512], AF.Exp,
             bias=biasqs[:, qh:qh + 1], scale=SCALE,
             reads=[f"pa{bk0}", f"pa{bk0 + 1}", "biasqs"],
             writes=[f"pb{par}_{j}"])

    pend_ep = []
    step_serial = [0]

    def flush_eps(force=False):
        while pend_ep and (force or pend_ep[0][0] <= step_serial[0]):
            pend_ep.pop(0)[1]()

    def emit_av_step(idx, j):
        h, f, qb = iters[idx]
        sl = h % 2
        qh = h * GF + f
        par = idx % 2
        for m in range(8 * j, 8 * j + 8):
            qt = m // DC
            kc = m % DC
            bb_ = qt % 2
            pres = f"pb{bb_}"
            r.op("tensor", "matmul", ps_b[:, bb_ * 512:bb_ * 512 + 129],
                 pbuf[par][:, kc * 512 + qt * 128:kc * 512 + (qt + 1) * 128],
                 vhb[sl][:, kc, 0:129],
                 start=(kc == 0), stop=(kc == DC - 1),
                 reads=[f"pb{par}_{kc // 2}", f"vhb{sl}", f"vones{sl}"],
                 writes=[pres], accum=(kc > 0))
            if kc == DC - 1:
                qoff = qh * TPC + qb * 512 + qt * 128

                def ep(pres=pres, bb_=bb_, qoff=qoff, first=(qh + qb + qt == 0)):
                    r.op("vector", "reciprocal", recip[:, :],
                         ps_b[:, bb_ * 512 + 128:bb_ * 512 + 129],
                         reads=[pres], writes=["recip"])
                    r.op("vector", "tensor_scalar", astg[:, :],
                         ps_b[:, bb_ * 512:bb_ * 512 + 128], recip[:, 0:1], None,
                         OP.mult, reads=[pres, "recip"], writes=["astg"])
                    r.op("tensor", "transpose", ps_tr[:, 0:128], astg[:, :],
                         identb[:, :], reads=["astg", "identb"], writes=["ptr"])
                    r.op("vector", "tensor_copy", a_fm[:, qoff:qoff + 128],
                         ps_tr[:, 0:128], reads=["ptr"], writes=["a_fm"],
                         accum=(not first))
                pend_ep.append((step_serial[0] + 2, ep))

    def emit_interleaved(s_idx, a_idx):
        for j in range(16):
            step_serial[0] += 1
            flush_eps()
            if s_idx is not None:
                emit_scores_step(s_idx, j)
            if a_idx is not None:
                emit_av_step(a_idx, j)

    load_kv(0, 0)
    load_kv(1, 1)
    emit_f_prep(0, 0)
    emit_interleaved(0, None)
    for idx in range(len(iters)):
        nxt = idx + 1
        if nxt < len(iters):
            hn, fn, qbn = iters[nxt]
            if qbn == 0:
                emit_f_prep(hn, fn)
        emit_interleaved(nxt if nxt < len(iters) else None, idx)
        hh, ff, qq = iters[idx]
        if ff == GF - 1 and qq == 1 and hh + 2 < H:
            load_kv(hh + 2, (hh + 2) % 2)
    flush_eps(force=True)

    if "afm" in tapd:
        r.op("sync", "dma_start", out=tapd["afm"], in_=a_fm[:, :], dma="dtap",
             reads=["a_fm"])

    for _tb in range(TT):
        r.span(f"attnt{_tb}", 65536 + _tb * 8192, 65536 + (_tb + 1) * 8192)
        r.span(f"x2t{_tb}", 0, 65536)
        r.span(f"ffnt{_tb}", 65536 + _tb * 8192, 65536 + (_tb + 1) * 8192)

    for _tb in range(TT):
        r.span(f"asum{_tb}", r.spans["asum"][0] + _tb * 64,
               r.spans["asum"][0] + (_tb + 1) * 64)
        r.span(f"asq{_tb}", r.spans["asq"][0] + _tb * 64,
               r.spans["asq"][0] + (_tb + 1) * 64)
        r.span(f"fsum{_tb}", r.spans["asum"][0] + _tb * 64,
               r.spans["asum"][0] + (_tb + 1) * 64)
        r.span(f"fsq{_tb}", r.spans["asq"][0] + _tb * 64,
               r.spans["asq"][0] + (_tb + 1) * 64)

    def combine_stats(sum_res, sq_res, tb):
        r.op("vector", "tensor_reduce", exv[:, :], asum[:, tb, :], AX.X, OP.add,
             reads=[sum_res], writes=["exv"])
        r.op("vector", "tensor_reduce", ex2[:, :], asq[:, tb, :], AX.X, OP.add,
             reads=[sq_res], writes=["ex2"])
        r.op("vector", "tensor_scalar", bn2[:, 0:1], exv[:, :], 1.0 / D, None,
             OP.mult, reads=["exv"], writes=["bn2"])
        r.op("vector", "tensor_scalar", ex2[:, :], ex2[:, :], 1.0 / D, None,
             OP.mult, reads=["ex2"], writes=["ex2"], accum=True)
        r.op("vector", "tensor_tensor", exv[:, :], bn2[:, 0:1], bn2[:, 0:1],
             OP.mult, reads=["bn2"], writes=["exv"])
        r.op("vector", "tensor_tensor", sdv[:, :], ex2[:, :], exv[:, :],
             OP.subtract, reads=["ex2", "exv"], writes=["sdv"])
        r.op("vector", "tensor_scalar", sdv[:, :], sdv[:, :], EPS, None, OP.add,
             reads=["sdv"], writes=["sdv"], accum=True)
        r.op("scalar", "activation", sdv[:, :], sdv[:, :], AF.Sqrt,
             reads=["sdv"], writes=["sdv"])
        r.op("vector", "reciprocal", rstd[:, :], sdv[:, :], reads=["sdv"],
             writes=["rstd"])

    def emit_p5(tb):
        r.op("sync", "dma_start", out=x5[:, :], in_=xd[tb * 128:(tb + 1) * 128, :],
             dma="dx5", writes=["x5"])
        r.op("vector", "tensor_tensor", x5[:, :], x5[:, :],
             attn_sb[:, tb * D:(tb + 1) * D], OP.add,
             reads=["x5", f"attnt{tb}"], writes=["x5"], accum=True)
        layernorm_stats(x5[:, :], "x5")
        r.op("vector", "tensor_scalar", ln2buf[:, :], x5[:, :], bn2[:, 0:1],
             rstd[:, 0:1], OP.subtract, OP.mult,
             reads=["x5", "bn2", "rstd"], writes=["ln2buf"])
        for dc in range(DC):
            r.op("tensor", "transpose", ps_tr[:, 0:128],
                 ln2buf[:, dc * 128:(dc + 1) * 128], identb[:, :],
                 reads=["ln2buf", "identb"], writes=["ptr"])
            r.op("vector", "tensor_copy",
                 x2_fm[:, dc * TPC + tb * 128:dc * TPC + (tb + 1) * 128],
                 ps_tr[:, 0:128], reads=["ptr"], writes=[f"x2t{tb}"],
                 accum=(dc > 0))

    r.op("gpsimd", "dma_start", out=wobrow[0:1, :],
         in_=wob.rearrange("(a k) -> a k", a=1), dma="dwob", writes=["wobrow"])
    r.retouch("wst0")
    r.retouch("wst1")
    emit_stripe_load(wo, 0)
    for s in range(16):
        if s + 1 < 16:
            emit_stripe_load(wo, s + 1)
        sl = s % 2
        for tb in range(TT):
            if s == 15 and tb >= 2:
                emit_p5(tb - 2)
            b = next_bank()
            pres = f"pa{b}"
            po = ps_a[:, b * 512:b * 512 + 256]
            for dc in range(DC):
                r.op("tensor", "matmul", po,
                     a_fm[:, dc * TPC + tb * 128:dc * TPC + (tb + 1) * 128],
                     wst[sl][:, dc, :], start=(dc == 0), stop=False,
                     reads=[f"wst{sl}", "a_fm"], writes=[pres], accum=(dc > 0))
            r.op("tensor", "matmul", po, onesrow[0:1, :],
                 wobrow[0:1, s * 256:(s + 1) * 256], start=False, stop=True,
                 reads=["onesrow", "wobrow"], writes=[pres], accum=True)
            r.op("scalar", "activation",
                 attn_sb[:, tb * D + s * 256:tb * D + (s + 1) * 256], po, AF.Copy,
                 accum_out=asum[:, tb, s:s + 1],
                 reads=[pres], writes=[f"attnt{tb}", f"asum{tb}"], accum=(s > 0))
            r.op("scalar", "activation", sqscrap[:, :], po, AF.Square,
                 accum_out=asq[:, tb, s:s + 1],
                 reads=[pres], writes=["sqscrap", f"asq{tb}"])
            if s == 15:
                row = attn_sb[:, tb * D:(tb + 1) * D]
                combine_stats(f"asum{tb}", f"asq{tb}", tb)
                r.op("vector", "tensor_scalar", row, row, bn2[:, 0:1], rstd[:, 0:1],
                     OP.subtract, OP.mult, reads=[f"attnt{tb}", "bn2", "rstd"],
                     writes=[f"attnt{tb}"], accum=True)
                r.op("sync", "dma_start", out=lnasc[tb * 128:(tb + 1) * 128, :],
                     in_=row, dma=f"dlna{tb}", reads=[f"attnt{tb}"],
                     writes=["lnasc"], accum=(tb > 0))
    emit_p5(6)
    emit_p5(7)
    if "attn" in tapd:
        r.op("sync", "dma_start", out=tapd["attn"], in_=attn_sb[:, :], dma="dtap",
             reads=[f"attnt{t}" for t in range(TT)])
    if "x2fm" in tapd:
        r.op("sync", "dma_start", out=tapd["x2fm"], in_=x2_fm[:, :], dma="dtap",
             reads=[f"x2t{t}" for t in range(TT)])

    r.op("gpsimd", "dma_start", out=atsb[:, :, :],
         in_=at_.rearrange("(dc p) e -> p dc e", p=128), dma="dat", writes=["atsb"])
    r.op("gpsimd", "dma_start", out=ballt[:, :], in_=br_, dma="dbr", writes=["ballt"])
    r.op("gpsimd", "dma_start", out=bbrow[0:1, :],
         in_=bb.rearrange("(a k) -> a k", a=1), dma="dbb", writes=["bbrow"])
    for tb in range(TT):
        for dc in range(DC):
            r.op("tensor", "matmul", ps_b[:, 0:128],
                 x2_fm[:, dc * TPC + tb * 128:dc * TPC + (tb + 1) * 128],
                 atsb[:, dc, :], start=(dc == 0), stop=False,
                 reads=[f"x2t{tb}", "atsb"], writes=["pb0"], accum=(dc > 0))
            r.op("tensor", "matmul", ps_b[:, 128:136],
                 x2_fm[:, dc * TPC + tb * 128:dc * TPC + (tb + 1) * 128],
                 wrt[:, dc * 8:(dc + 1) * 8], start=False, stop=(dc == DC - 1),
                 reads=[f"x2t{tb}", "wrt"], writes=["pb0"], accum=True)
        r.op("scalar", "activation", hsb[:, :], ps_b[:, 0:136], AF.Copy,
             reads=["pb0"], writes=["hsb"])
        lg = hsb[:, 128:136]
        r.op("vector", "tensor_reduce", tm1[:, :], lg, AX.X, OP.max,
             reads=["hsb"], writes=["tm1"])
        r.op("vector", "tensor_scalar", teq1[:, :], lg, tm1[:, 0:1], None,
             OP.is_equal, reads=["hsb", "tm1"], writes=["teq1"])
        r.op("vector", "scalar_tensor_tensor", tlm[:, :], teq1[:, :], -1e30, lg,
             OP.mult, OP.add, reads=["teq1", "hsb"], writes=["tlm"])
        r.op("vector", "tensor_reduce", tm2[:, :], tlm[:, :], AX.X, OP.max,
             reads=["tlm"], writes=["tm2"])
        r.op("vector", "tensor_scalar", teq2[:, :], tlm[:, :], tm2[:, 0:1], None,
             OP.is_equal, reads=["tlm", "tm2"], writes=["teq2"])
        r.op("vector", "tensor_scalar", tm2[:, :], tm2[:, :], tm1[:, 0:1], None,
             OP.subtract, reads=["tm2", "tm1"], writes=["tm2"])
        r.op("scalar", "activation", trw2[:, :], tm2[:, :], AF.Sigmoid,
             reads=["tm2"], writes=["trw2"])
        r.op("vector", "tensor_scalar", trw1[:, :], trw2[:, :], -1.0, 1.0,
             OP.mult, OP.add, reads=["trw2"], writes=["trw1"])
        r.op("vector", "tensor_scalar", tt1[:, :], teq1[:, :], trw1[:, 0:1], None,
             OP.mult, reads=["teq1", "trw1"], writes=["tt1"])
        r.op("vector", "scalar_tensor_tensor", twts[:, :], teq2[:, :], trw2[:, 0:1],
             tt1[:, :], OP.mult, OP.add, reads=["teq2", "trw2", "tt1"],
             writes=["twts"])
        for e in range(NE):
            r.op("vector", "tensor_scalar", gtm[:, e * RK:(e + 1) * RK],
                 hsb[:, e * RK:(e + 1) * RK], twts[:, e:e + 1], LSCALE,
                 OP.mult, OP.mult, reads=["hsb", "twts"], writes=["gtm"],
                 accum=(e > 0))
        r.op("tensor", "transpose", ps_tr[:, 0:128], gtm[:, :], identb[:, :],
             reads=["gtm", "identb"], writes=["ptr"])
        r.op("vector", "tensor_copy", gfm[:, tb * 128:(tb + 1) * 128],
             ps_tr[:, 0:128], reads=["ptr"], writes=["gfm"], accum=(tb > 0))

    def emit_p7(tb):
        row = ffn_sb[:, tb * D:(tb + 1) * D]
        combine_stats(f"fsum{tb}", f"fsq{tb}", tb)
        r.op("vector", "tensor_scalar", row, row, bn2[:, 0:1], rstd[:, 0:1],
             OP.subtract, OP.mult, reads=[f"ffnt{tb}", "bn2", "rstd"],
             writes=[f"ffnt{tb}"], accum=True)
        r.op("scalar", "dma_start", out=x7[:, :], in_=xd[tb * 128:(tb + 1) * 128, :],
             dma="dx7", writes=["x7"])
        r.op("gpsimd", "dma_start", out=lna7[:, :],
             in_=lnasc[tb * 128:(tb + 1) * 128, :], dma="dl7",
             reads=["lnasc"], writes=["lna7"])
        r.op("vector", "tensor_tensor", x7[:, :], x7[:, :], lna7[:, :], OP.add,
             reads=["x7", "lna7"], writes=["x7"], accum=True)
        r.op("vector", "tensor_tensor", x7[:, :], x7[:, :], row, OP.add,
             reads=["x7", f"ffnt{tb}"], writes=["x7"], accum=True)
        r.op("sync", "dma_start", out=outd[tb * 128:(tb + 1) * 128, :], in_=x7[:, :],
             dma="dout", reads=["x7"], writes=["outd"], accum=(tb > 0))

    r.retouch("wst0")
    r.retouch("wst1")
    emit_stripe_load(wbt, 0)
    for s in range(16):
        if s + 1 < 16:
            emit_stripe_load(wbt, s + 1)
        sl = s % 2
        for tb in range(TT):
            b = next_bank()
            pres = f"pa{b}"
            po = ps_a[:, b * 512:b * 512 + 256]
            for dc in range(DC):
                r.op("tensor", "matmul", po,
                     x2_fm[:, dc * TPC + tb * 128:dc * TPC + (tb + 1) * 128],
                     wst[sl][:, dc, :], start=(dc == 0), stop=False,
                     reads=[f"wst{sl}", f"x2t{tb}"], writes=[pres], accum=(dc > 0))
            r.op("tensor", "matmul", po, gfm[:, tb * 128:(tb + 1) * 128],
                 ballt[:, s * 256:(s + 1) * 256], start=False, stop=False,
                 reads=["gfm", "ballt"], writes=[pres], accum=True)
            r.op("tensor", "matmul", po, onesrow[0:1, :],
                 bbrow[0:1, s * 256:(s + 1) * 256], start=False, stop=True,
                 reads=["onesrow", "bbrow"], writes=[pres], accum=True)
            r.op("scalar", "activation",
                 ffn_sb[:, tb * D + s * 256:tb * D + (s + 1) * 256], po, AF.Copy,
                 accum_out=asum[:, tb, s:s + 1],
                 reads=[pres], writes=[f"ffnt{tb}", f"fsum{tb}"], accum=(s > 0))
            r.op("scalar", "activation", sqscrap[:, :], po, AF.Square,
                 accum_out=asq[:, tb, s:s + 1],
                 reads=[pres], writes=["sqscrap", f"fsq{tb}"])
    for tb in range(TT):
        emit_p7(tb)
    if "ffn" in tapd:
        r.op("sync", "dma_start", out=tapd["ffn"], in_=ffn_sb[:, :], dma="dtap",
             reads=[f"ffnt{t}" for t in range(TT)])

    r.final_wait("dout")
    if tapd:
        r.final_wait("dtap")

    r.materialize()

    with contextlib.ExitStack() as st2:
        esems = {e: st2.enter_context(nc.semaphore(f"se_{e}")) for e in ENG}
        dsems = {n: st2.enter_context(nc.semaphore(f"sd_{n}"))
                 for n in sorted(r.dma_sems)}
        block = st2.enter_context(nc.Block())

        @block.tensor
        def _(t):
            r.replay(t, "tensor", esems, dsems)

        @block.vector
        def _(v):
            r.replay(v, "vector", esems, dsems)

        @block.scalar
        def _(sc):
            r.replay(sc, "scalar", esems, dsems)

        @block.sync
        def _(sy):
            r.replay(sy, "sync", esems, dsems)

        @block.gpsimd
        def _(gp):
            r.replay(gp, "gpsimd", esems, dsems)

    stack.close()
    return nc


def kernel(L0_attn_x, L0_attn_wq_w, L0_attn_wq_b, L0_attn_wk_w, L0_attn_wk_b,
           L0_attn_wv_w, L0_attn_wv_b, L0_attn_wo_w, L0_attn_wo_b,
           L0_ffn_W_base, L0_ffn_b_base, L0_ffn_W_router, L0_ffn_A, L0_ffn_B):
    xf = np.ascontiguousarray(
        np.asarray(L0_attn_x, dtype=np.float32).reshape(NCORES * TPC, D))
    base = {
        "wq": np.ascontiguousarray(L0_attn_wq_w, dtype=np.float32),
        "wk": np.ascontiguousarray(L0_attn_wk_w, dtype=np.float32),
        "wv": np.ascontiguousarray(L0_attn_wv_w, dtype=np.float32),
        "wo": np.ascontiguousarray(L0_attn_wo_w, dtype=np.float32),
        "wqb": np.ascontiguousarray(L0_attn_wq_b, dtype=np.float32),
        "wkb": np.ascontiguousarray(L0_attn_wk_b, dtype=np.float32),
        "wvb": np.ascontiguousarray(L0_attn_wv_b, dtype=np.float32),
        "wob": np.ascontiguousarray(L0_attn_wo_b, dtype=np.float32),
        "wbt": np.ascontiguousarray(np.asarray(L0_ffn_W_base, dtype=np.float32).T),
        "bb": np.ascontiguousarray(L0_ffn_b_base, dtype=np.float32),
        "wr": np.ascontiguousarray(L0_ffn_W_router, dtype=np.float32),
        "at": np.ascontiguousarray(
            np.asarray(L0_ffn_A, dtype=np.float32).reshape(NE * RK, D).T),
        "br": np.ascontiguousarray(
            np.asarray(L0_ffn_B, dtype=np.float32).transpose(0, 2, 1).reshape(NE * RK, D)),
    }
    in_maps = []
    for c in range(NCORES):
        m = dict(base)
        m["x"] = np.ascontiguousarray(xf[c * TPC:(c + 1) * TPC])
        in_maps.append(m)
    nc = build_nc(taps=_DEBUG_TAPS)
    res = run_bass_kernel_spmd(nc, in_maps, core_ids=list(range(NCORES)))
    out = np.concatenate([res.results[c]["out"] for c in range(NCORES)], axis=0)
    return np.ascontiguousarray(out.reshape(2, SL, D), dtype=np.float32)


# version 06
# speedup vs baseline: 1.1715x; 1.0069x over previous
import math
import contextlib
import numpy as np
import concourse.bass as bass
import concourse.mybir as mybir
from concourse.alu_op_type import AluOpType as OP
from concourse.bass_utils import run_bass_kernel_spmd

F32 = mybir.dt.float32
BF16 = mybir.dt.bfloat16
AF = mybir.ActivationFunctionType
AX = mybir.AxisListType

NCORES = 8
D = 4096
DC = 32
TPC = 1024
TT = 8
KV = 1024
H = 8
GF = 4
HD = 128
SL = 4096
NR = 4
SCALE = 1.0 / math.sqrt(HD)
EPS = 1e-5
NE = 8
RK = 16
LSCALE = 2.0

ENG = ("tensor", "vector", "scalar", "sync", "gpsimd")

_DEBUG_TAPS = []


class Rec:

    def __init__(self):
        self.ops = []
        self.wl = {}
        self.r = {}
        self.spans = {}
        self.touched = set()

    def span(self, res, lo, hi):
        self.spans[res] = (lo, hi)

    def retouch(self, res):
        self.touched.discard(res)

    def op(self, eng, meth, *args, reads=(), writes=(), accum=False, dma=None,
           cc=False, **kwargs):
        i = len(self.ops)
        deps = set()
        for res in writes:
            if res not in self.touched and res in self.spans:
                lo, hi = self.spans[res]
                for other, (lo2, hi2) in self.spans.items():
                    if other == res or hi2 <= lo or hi <= lo2:
                        continue
                    deps.update(self.wl.get(other, ()))
                    deps.update(self.r.get(other, ()))
                self.touched.add(res)
        for res in reads:
            deps.update(self.wl.get(res, ()))
        if not accum:
            for res in writes:
                deps.update(self.wl.get(res, ()))
                deps.update(self.r.get(res, ()))
        for res in writes:
            if accum:
                self.wl.setdefault(res, []).append(i)
            else:
                self.wl[res] = [i]
                self.r[res] = []
        for res in reads:
            self.r.setdefault(res, []).append(i)
        self.ops.append(dict(i=i, eng=eng, meth=meth, args=args, kwargs=kwargs,
                             deps=deps, dma=dma, cc=cc, need=False, sem=None,
                             val=None))
        return i

    def final_wait(self, dma_name):
        self.ops.append(dict(i=len(self.ops), eng="sync", meth="_final_wait",
                             args=(dma_name,), kwargs={}, deps=set(), dma=None,
                             cc=False, need=False, sem=None, val=None))

    def materialize(self):
        for o in self.ops:
            best = {}
            for d in o["deps"]:
                p = self.ops[d]
                key = ("dma", p["dma"]) if p["dma"] is not None else ("eng", p["eng"])
                if key not in best or d > best[key]:
                    best[key] = d
            o["deps"] = set(best.values())
        for o in self.ops:
            for d in o["deps"]:
                self.ops[d]["need"] = True
        engct = {e: 0 for e in ENG}
        dmact = {}
        self.dma_sems = set()
        for o in self.ops:
            if o["dma"] is not None:
                step = 1 if o["cc"] else 16
                dmact[o["dma"]] = dmact.get(o["dma"], 0) + step
                o["sem"], o["val"] = ("dma", o["dma"]), dmact[o["dma"]]
                self.dma_sems.add(o["dma"])
            elif o["need"]:
                engct[o["eng"]] += 1
                o["sem"], o["val"] = ("eng", o["eng"]), engct[o["eng"]]
        hw = {e: {} for e in ENG}
        for o in self.ops:
            if o["meth"] == "_final_wait":
                o["waits"] = [(("dma", o["args"][0]), dmact[o["args"][0]])]
                continue
            waits = []
            for d in sorted(o["deps"]):
                p = self.ops[d]
                if p["sem"] is None:
                    continue
                key = p["sem"]
                cur = hw[o["eng"]].get(key, 0)
                if p["val"] > cur:
                    waits.append((key, p["val"]))
                    hw[o["eng"]][key] = p["val"]
            o["waits"] = waits

    def replay(self, engobj, eng, esems, dsems):
        for o in self.ops:
            if o["eng"] != eng:
                continue
            for (kind, name), v in o.get("waits", ()):
                engobj.wait_ge(dsems[name] if kind == "dma" else esems[name], v)
            if o["meth"] == "_final_wait":
                continue
            inst = getattr(engobj, o["meth"])(*o["args"], **o["kwargs"])
            if o["sem"] is not None:
                if o["sem"][0] == "dma":
                    inst.then_inc(dsems[o["sem"][1]], 1 if o["cc"] else 16)
                else:
                    inst.then_inc(esems[eng], 1)


def build_nc(taps=()):
    nc = bass.Bass("TRN2", target_bir_lowering=False, debug=False)

    xd = nc.dram_tensor("x", [TPC, D], F32, kind="ExternalInput").ap()
    wq = nc.dram_tensor("wq", [D, D], F32, kind="ExternalInput").ap()
    wk = nc.dram_tensor("wk", [D, KV], F32, kind="ExternalInput").ap()
    wv = nc.dram_tensor("wv", [D, KV], F32, kind="ExternalInput").ap()
    wo = nc.dram_tensor("wo", [D, D], F32, kind="ExternalInput").ap()
    wqb = nc.dram_tensor("wqb", [D], F32, kind="ExternalInput").ap()
    wkb = nc.dram_tensor("wkb", [KV], F32, kind="ExternalInput").ap()
    wvb = nc.dram_tensor("wvb", [KV], F32, kind="ExternalInput").ap()
    wob = nc.dram_tensor("wob", [D], F32, kind="ExternalInput").ap()
    wbt = nc.dram_tensor("wbt", [D, D], F32, kind="ExternalInput").ap()
    bb = nc.dram_tensor("bb", [D], F32, kind="ExternalInput").ap()
    wr = nc.dram_tensor("wr", [NE, D], F32, kind="ExternalInput").ap()
    at_ = nc.dram_tensor("at", [D, NE * RK], F32, kind="ExternalInput").ap()
    br_ = nc.dram_tensor("br", [NE * RK, D], F32, kind="ExternalInput").ap()
    outd = nc.dram_tensor("out", [TPC, D], F32, kind="ExternalOutput").ap()

    qsc = nc.dram_tensor("qscratch", [D, TPC], BF16).ap()
    gin = nc.dram_tensor("gin", [2064, KV], BF16).ap()
    gout = nc.dram_tensor("gout", [NR * 2064, KV], BF16).ap()
    lnasc = nc.dram_tensor("lnascratch", [TPC, D], BF16).ap()

    tapd = {}
    tap_shapes = {"xfm": [128, DC * TPC], "afm": [128, DC * TPC],
                  "x2fm": [128, DC * TPC], "attn": [128, TT * D],
                  "ffn": [128, TT * D], "pbuf": [128, 16384],
                  "qkv": [2048, KV]}
    for t in taps:
        tapd[t] = nc.dram_tensor("tap_" + t, tap_shapes[t], BF16,
                                 kind="ExternalOutput").ap()

    r = Rec()

    SB0 = 16512

    def sb(name, shape, dtype, off):
        t = nc.alloc_sbuf_tensor_at(name, shape, dtype, offset=SB0 + off)
        nbytes = 1
        for s in shape[1:]:
            nbytes *= s
        nbytes *= 2 if dtype == BF16 else 4
        r.span(name, off, off + nbytes)
        return t

    x_fm = sb("x_fm", [128, DC * TPC], BF16, 0)
    for _tb in range(TT):
        r.span(f"xfmt{_tb}", 0, 65536)
    a_fm = sb("a_fm", [128, DC * TPC], BF16, 0)
    x2_fm = sb("x2_fm", [128, DC * TPC], BF16, 0)
    ln1bufs = [sb(f"ln1buf{i}", [128, D], BF16, 65536 + i * 8192) for i in range(2)]
    wrtmp = sb("wrtmp", [NE, D], BF16, 65536 + 8192)
    pbuf = [sb(f"pbuf{i}", [128, 16384], BF16, 65536 + i * 32768) for i in range(2)]
    for _p in range(2):
        for _g in range(16):
            r.span(f"pb{_p}_{_g}", 65536 + _p * 32768 + _g * 2048,
                   65536 + _p * 32768 + (_g + 1) * 2048)
    qsl = [sb(f"qsl{i}", [128, TPC], BF16, 164864 + i * 2048) for i in range(2)]
    sq8k = sb("sq8k", [128, D], BF16, 168960)
    knrow = sb("knrow", [1, 512], F32, 177152)
    attn_sb = sb("attn_sb", [128, TT * D], BF16, 65536)
    ffn_sb = sb("ffn_sb", [128, TT * D], BF16, 65536)
    xsl = [sb(f"xsl{i}", [128, D], F32, 131072 + i * 16384) for i in range(2)]
    qtg = [sb(f"qtg{i}", [128, 512], BF16, 131072 + i * 1024) for i in range(2)]
    ktg = [sb(f"ktg{i}", [128, 512], BF16, 133120 + i * 1024) for i in range(2)]
    vtg = [sb(f"vtg{i}", [128, 256], BF16, 135168 + i * 512) for i in range(2)]
    khb = [sb(f"khb{i}", [128, NR, 1024], BF16, 131072 + i * 8192) for i in range(2)]
    vhb = [sb(f"vhb{i}", [128, DC, 130], BF16, 147456 + i * 8320) for i in range(2)]
    wobrow = sb("wobrow", [1, D], BF16, 155648)
    x5 = sb("x5", [128, D], F32, 131072)
    ln2buf = sb("ln2buf", [128, D], BF16, 147456)
    atsb = sb("atsb", [128, DC, NE * RK], BF16, 131072)
    ballt = sb("ballt", [128, D], BF16, 139264)
    bbrow = sb("bbrow", [1, D], BF16, 147456)
    xh = [sb(f"xh{i}", [128, 2048], F32, 131072 + i * 8192) for i in range(2)]
    lnh = [sb(f"lnh{i}", [128, 2048], BF16, 147456 + i * 4096) for i in range(2)]
    wst = [sb(f"wst{i}", [128, DC, 256], BF16, 164864 + i * 16384) for i in range(2)]
    wst4 = wst + [sb(f"wst{2 + i}", [128, DC, 256], BF16, 98304 + i * 16384) for i in range(2)]
    M = 197632

    def msb(name, shape, dtype):
        nonlocal M
        t = sb(name, shape, dtype, M)
        nbytes = 1
        for s in shape[1:]:
            nbytes *= s
        nbytes *= 2 if dtype == BF16 else 4
        M += (nbytes + 31) // 32 * 32
        return t

    identf = msb("identf", [128, 128], F32)
    identb = msb("identb", [128, 128], BF16)
    iot0 = msb("iot0", [128, 128], F32)
    iot1 = msb("iot1", [128, 128], F32)
    wqb_sb = msb("wqb_sb", [128, 32], F32)
    wkb_sb = msb("wkb_sb", [128, 8], F32)
    braw = msb("braw", [32, 128], F32)
    onescol = msb("onescol", [128, 1], BF16)
    onesrow = msb("onesrow", [1, 128], BF16)
    wvbrow = msb("wvbrow", [1, KV], BF16)
    wrt = msb("wrt", [128, DC * NE], BF16)
    biasq = msb("biasq", [128, 1], F32)
    bqbf = msb("bqbf", [1, 1], BF16)
    mk2 = msb("mk2", [1, 1], F32)
    mq2 = msb("mq2", [1, 1], F32)
    nsc = msb("nsc", [1, 1], F32)
    bnb = msb("bnb", [128, 8, 6], F32)
    bn2 = msb("bn2", [128, 2], F32)
    sdv = msb("sdv", [128, 1], F32)
    rstd = msb("rstd", [128, 1], F32)
    recip = msb("recip", [128, 1], F32)
    astg = msb("astg", [128, 128], BF16)
    hsb = msb("hsb", [128, 136], F32)
    gtm = msb("gtm", [128, 128], BF16)
    gfm = msb("gfm", [128, TT * 128], BF16)
    tm1 = msb("tm1", [128, 1], F32)
    tm2 = msb("tm2", [128, 1], F32)
    teq1 = msb("teq1", [128, NE], F32)
    teq2 = msb("teq2", [128, NE], F32)
    tlm = msb("tlm", [128, NE], F32)
    trw2 = msb("trw2", [128, 1], F32)
    trw1 = msb("trw1", [128, 1], F32)
    tt1 = msb("tt1", [128, NE], F32)
    twts = msb("twts", [128, NE], F32)
    ksq5 = msb("ksq5", [128, 512], BF16)
    nrow = msb("nrow", [1, 512], F32)
    mklr = msb("mklr", [1, NE], F32)
    mqlr = msb("mqlr", [1, 32], F32)
    mk2f = msb("mk2f", [1, NE], F32)
    nrows4 = msb("nrows4", [1, 32], BF16)
    bqrow = msb("bqrow", [1, 32], F32)
    bqrowb = msb("bqrowb", [1, 32], BF16)
    biasqs = msb("biasqs", [128, 32], F32)
    asum = msb("asum", [128, TT, 16], F32)
    asq = msb("asq", [128, TT, 16], F32)
    sqscrap = msb("sqscrap", [128, 256], BF16)
    exv = msb("exv", [128, 1], F32)
    ex2 = msb("ex2", [128, 1], F32)
    assert SB0 + M <= 229344, M

    stack = contextlib.ExitStack()
    ps_a = stack.enter_context(nc.psum_tensor("ps_a", [128, 2048], F32))
    ps_b = stack.enter_context(nc.psum_tensor("ps_b", [128, 1024], F32))
    ps_tr = stack.enter_context(nc.psum_tensor("ps_tr", [128, 1024], BF16))
    ps_d = stack.enter_context(nc.psum_tensor("ps_d", [128, 512], F32))

    r.op("gpsimd", "iota", iot0[:, :], [[1, 128]], channel_multiplier=0,
         allow_small_or_imprecise_dtypes=True, writes=["iot0"])
    r.op("gpsimd", "iota", iot1[:, :], [[0, 128]], channel_multiplier=1,
         allow_small_or_imprecise_dtypes=True, writes=["iot1"])
    r.op("vector", "tensor_tensor", identf[:, :], iot0[:, :], iot1[:, :], OP.is_equal,
         reads=["iot0", "iot1"], writes=["identf"])
    r.op("vector", "tensor_copy", identb[:, :], identf[:, :],
         reads=["identf"], writes=["identb"])
    r.op("gpsimd", "memset", onescol[:, :], 1.0, writes=["onescol"])
    r.op("gpsimd", "memset", onesrow[:, :], 1.0, writes=["onesrow"])
    for i in range(2):
        r.span(f"vones{i}", *r.spans[f"vhb{i}"])
    r.op("sync", "dma_start", out=braw[:, :], in_=wqb.rearrange("(a p) -> a p", p=128),
         dma="dbias", writes=["braw"])
    r.op("tensor", "transpose", ps_d[:, 0:32], braw[:, :], identf[0:32, 0:32],
         reads=["braw", "identf"], writes=["pd"])
    r.op("vector", "tensor_copy", wqb_sb[:, :], ps_d[:, 0:32], reads=["pd"],
         writes=["wqb_sb"])
    r.op("sync", "dma_start", out=braw[0:8, :], in_=wkb.rearrange("(a p) -> a p", p=128),
         dma="dbias", reads=["pd"], writes=["braw"])
    r.op("tensor", "transpose", ps_d[:, 32:40], braw[0:8, :], identf[0:8, 0:8],
         reads=["braw", "identf"], writes=["pd"])
    r.op("vector", "tensor_copy", wkb_sb[:, :], ps_d[:, 32:40], reads=["pd"],
         writes=["wkb_sb"])
    r.op("gpsimd", "dma_start", out=wvbrow[0:1, :],
         in_=wvb.rearrange("(a k) -> a k", a=1), dma="dwvb", writes=["wvbrow"])
    r.op("gpsimd", "dma_start", out=wrtmp[:, :], in_=wr, dma="dwrt", writes=["wrtmp"])
    for dc in range(DC):
        r.op("tensor", "transpose", ps_tr[:, 0:8], wrtmp[:, dc * 128:(dc + 1) * 128],
             identb[0:8, 0:8], reads=["wrtmp", "identb"], writes=["ptr"])
        r.op("vector", "tensor_copy", wrt[:, dc * 8:(dc + 1) * 8], ps_tr[:, 0:8],
             reads=["ptr"], writes=["wrt"], accum=(dc > 0))

    def layernorm_stats(src_ap, src_res):
        for i in range(8):
            r.op("vector", "bn_stats", bnb[:, i, :], src_ap[:, i * 512:(i + 1) * 512],
                 reads=[src_res], writes=["bnb"], accum=(i > 0))
        r.op("vector", "bn_aggr", bn2[:, :], bnb[:, :, :], reads=["bnb"], writes=["bn2"])
        r.op("vector", "tensor_scalar", sdv[:, :], bn2[:, 1:2], EPS, None, OP.add,
             reads=["bn2"], writes=["sdv"])
        r.op("scalar", "activation", sdv[:, :], sdv[:, :], AF.Sqrt,
             reads=["sdv"], writes=["sdv"])
        r.op("vector", "reciprocal", rstd[:, :], sdv[:, :], reads=["sdv"], writes=["rstd"])

    def emit_stripe_load(wsrc, s):
        sl = s % 2
        src = wsrc.rearrange("(dc p) n -> p dc n", p=128)[:, :, s * 256:(s + 1) * 256]
        r.op("gpsimd", "dma_start", out=wst[sl][:, :, :], in_=src,
             dma=f"dw{sl}", writes=[f"wst{sl}"])
        return sl

    pctr = [0]

    def next_bank():
        b = pctr[0] % 4
        pctr[0] += 1
        return b

    for tb in range(TT):
        sl = tb % 2
        r.op("sync", "dma_start", out=xsl[sl][:, :], in_=xd[tb * 128:(tb + 1) * 128, :],
             dma=f"dx{sl}", writes=[f"xsl{sl}"])
        layernorm_stats(xsl[sl][:, :], f"xsl{sl}")
        r.op("vector", "tensor_scalar", ln1bufs[sl][:, :], xsl[sl][:, :], bn2[:, 0:1],
             rstd[:, 0:1], OP.subtract, OP.mult,
             reads=[f"xsl{sl}", "bn2", "rstd"], writes=[f"ln1buf{sl}"])
        for dc in range(DC):
            r.op("tensor", "transpose", ps_tr[:, 0:128],
                 ln1bufs[sl][:, dc * 128:(dc + 1) * 128], identb[:, :],
                 reads=[f"ln1buf{sl}", "identb"], writes=["ptr"])
            dst = x_fm[:, dc * TPC + tb * 128:dc * TPC + (tb + 1) * 128]
            if dc % 2 == 0:
                r.op("vector", "tensor_copy", dst, ps_tr[:, 0:128],
                     reads=["ptr"], writes=[f"xfmt{tb}"], accum=(dc > 0))
            else:
                r.op("scalar", "copy", dst, ps_tr[:, 0:128],
                     reads=["ptr"], writes=[f"xfmt{tb}"], accum=True)
    if "xfm" in tapd:
        r.op("sync", "dma_start", out=tapd["xfm"], in_=x_fm[:, :], dma="dtap",
             reads=[f"xfmt{t}" for t in range(TT)])

    stripe_plan = [(wk, s) for s in range(4)] + [(wv, s) for s in range(4)] + \
                  [(wq, s) for s in range(16)]

    def p1_load(gidx):
        if gidx >= len(stripe_plan):
            return
        wsrc, s = stripe_plan[gidx]
        slot = gidx % 4
        src_ap = wsrc.rearrange("(dc p) n -> p dc n", p=128)[:, :, s * 256:(s + 1) * 256]
        r.op("gpsimd", "dma_start", out=wst4[slot][:, :, :], in_=src_ap,
             dma=f"dw{slot}", writes=[f"wst{slot}"])

    for g in range(3):
        p1_load(g)

    for s in range(4):
        gidx = s
        p1_load(gidx + 3)
        sl = gidx % 4
        for sub in range(2):
            dkc = s * 2 + sub
            for tb in range(2):
                b = next_bank()
                pres = f"pa{b}"
                po = ps_a[:, b * 512:(b + 1) * 512]
                for dc in range(DC):
                    r.op("tensor", "matmul", po,
                         wst4[sl][:, dc, sub * 128:(sub + 1) * 128],
                         x_fm[:, dc * TPC + tb * 512:dc * TPC + (tb + 1) * 512],
                         start=(dc == 0), stop=(dc == DC - 1),
                         reads=[f"wst{sl}", f"xfmt{tb * 4}", f"xfmt{tb * 4 + 1}",
                                f"xfmt{tb * 4 + 2}", f"xfmt{tb * 4 + 3}"],
                         writes=[pres], accum=(dc > 0))
                st = (dkc * 2 + tb) % 2
                r.op("vector", "tensor_scalar", ktg[st][:, :], po,
                     wkb_sb[:, dkc:dkc + 1], None, OP.add,
                     reads=[pres, "wkb_sb"], writes=[f"ktg{st}"])
                r.op("sync", "dma_start",
                     out=gin[dkc * 128:(dkc + 1) * 128, tb * 512:(tb + 1) * 512],
                     in_=ktg[st][:, :], dma=f"dkst{st}", reads=[f"ktg{st}"],
                     writes=["gin_k"], accum=(dkc + tb > 0))
                r.op("scalar", "activation", ksq5[:, :], ktg[st][:, :], AF.Square,
                     reads=[f"ktg{st}"], writes=["ksq5"])
                r.op("tensor", "matmul", ps_d[0:1, 0:512], onescol[:, 0:1],
                     ksq5[:, :], start=True, stop=True,
                     reads=["ksq5", "onescol"], writes=["pd"])
                r.op("vector", "tensor_copy", nrow[0:1, :], ps_d[0:1, 0:512],
                     reads=["pd"], writes=["nrow"])
                r.op("vector", "tensor_reduce", nsc[0:1, 0:1], nrow[0:1, :], AX.X,
                     OP.max, reads=["nrow"], writes=["nsc"])
                if tb == 0:
                    r.op("vector", "tensor_copy", mklr[0:1, dkc:dkc + 1],
                         nsc[0:1, 0:1], reads=["nsc"], writes=["mklr"],
                         accum=(dkc > 0))
                else:
                    r.op("vector", "tensor_tensor", mklr[0:1, dkc:dkc + 1],
                         mklr[0:1, dkc:dkc + 1], nsc[0:1, 0:1], OP.max,
                         reads=["nsc", "mklr"], writes=["mklr"], accum=True)

    vgd = gin[1024:2048, :].rearrange("(b s16) t -> b s16 t", s16=16)
    vgd = vgd.rearrange("(h tc) s16 t -> h tc s16 t", tc=8)
    for s in range(4):
        gidx = 4 + s
        p1_load(gidx + 3)
        sl = gidx % 4
        for tb in range(TT):
            b = next_bank()
            pres = f"pa{b}"
            po = ps_a[:, b * 512:b * 512 + 256]
            for dc in range(DC):
                r.op("tensor", "matmul", po,
                     x_fm[:, dc * TPC + tb * 128:dc * TPC + (tb + 1) * 128],
                     wst4[sl][:, dc, :], start=(dc == 0), stop=False,
                     reads=[f"wst{sl}", f"xfmt{tb}"], writes=[pres], accum=(dc > 0))
            r.op("tensor", "matmul", po, onesrow[0:1, :],
                 wvbrow[0:1, s * 256:(s + 1) * 256], start=False, stop=True,
                 reads=["onesrow", "wvbrow"], writes=[pres], accum=True)
            st = tb % 2
            r.op("scalar", "activation", vtg[st][:, :], po, AF.Copy,
                 reads=[pres], writes=[f"vtg{st}"])
            dst = vgd[s * 2:(s + 1) * 2, tb, :, :].rearrange(
                "h s16 (t8 dv) -> (s16 t8) h dv", dv=128)
            r.op("sync", "dma_start", out=dst,
                 in_=vtg[st][:, :].rearrange("p (h dv) -> p h dv", dv=128),
                 dma=f"dvst{st}", reads=[f"vtg{st}"], writes=["gin_v"],
                 accum=(s + tb > 0))

    if "qkv" in tapd:
        r.op("sync", "dma_start", out=tapd["qkv"], in_=gin[0:2048, :], dma="dtap",
             reads=["gin_k", "gin_v"])

    r.op("gpsimd", "memset", ksq5[0:16, :], 0.0, writes=["ksq5"])
    r.op("gpsimd", "dma_start", out=gin[2048:2064, 0:512], in_=ksq5[0:16, :],
         dma="dnrm", reads=["ksq5"], writes=["gin_n"])
    r.op("gpsimd", "dma_start", out=gin[2048:2064, 512:1024], in_=ksq5[0:16, :],
         dma="dnrm", reads=["ksq5"], writes=["gin_n"], accum=True)
    r.op("gpsimd", "dma_start", out=gin[2048:2049, 0:NE], in_=mklr[0:1, :],
         dma="dnrm", reads=["mklr"], writes=["gin_n"])

    r.op("gpsimd", "collective_compute", "AllGather", OP.bypass,
         replica_groups=[[0, 1, 2, 3], [4, 5, 6, 7]],
         ins=[gin.opt()], outs=[gout.opt()],
         reads=["gin_k", "gin_v", "gin_n"], writes=["gout"], dma="dcc", cc=True)

    for s in range(16):
        gidx = 8 + s
        p1_load(gidx + 3)
        sl = gidx % 4
        for sub in range(2):
            dqc = s * 2 + sub
            for tb in range(2):
                b = next_bank()
                pres = f"pa{b}"
                po = ps_a[:, b * 512:(b + 1) * 512]
                for dc in range(DC):
                    r.op("tensor", "matmul", po,
                         wst4[sl][:, dc, sub * 128:(sub + 1) * 128],
                         x_fm[:, dc * TPC + tb * 512:dc * TPC + (tb + 1) * 512],
                         start=(dc == 0), stop=(dc == DC - 1),
                         reads=[f"wst{sl}", f"xfmt{tb * 4}", f"xfmt{tb * 4 + 1}",
                                f"xfmt{tb * 4 + 2}", f"xfmt{tb * 4 + 3}"],
                         writes=[pres], accum=(dc > 0))
                st = (dqc * 2 + tb) % 2
                r.op("vector", "tensor_scalar", qtg[st][:, :], po,
                     wqb_sb[:, dqc:dqc + 1], None, OP.add,
                     reads=[pres, "wqb_sb"], writes=[f"qtg{st}"])
                r.op("sync", "dma_start",
                     out=qsc[dqc * 128:(dqc + 1) * 128, tb * 512:(tb + 1) * 512],
                     in_=qtg[st][:, :], dma=f"dqst{st}", reads=[f"qtg{st}"],
                     writes=[f"qsc{dqc}"], accum=(tb > 0))
                r.op("scalar", "activation", ksq5[:, :], qtg[st][:, :], AF.Square,
                     reads=[f"qtg{st}"], writes=["ksq5"])
                r.op("tensor", "matmul", ps_d[0:1, 0:512], onescol[:, 0:1],
                     ksq5[:, :], start=True, stop=True,
                     reads=["ksq5", "onescol"], writes=["pd"])
                r.op("vector", "tensor_copy", nrow[0:1, :], ps_d[0:1, 0:512],
                     reads=["pd"], writes=["nrow"])
                r.op("vector", "tensor_reduce", nsc[0:1, 0:1], nrow[0:1, :], AX.X,
                     OP.max, reads=["nrow"], writes=["nsc"])
                if tb == 0:
                    r.op("vector", "tensor_copy", mqlr[0:1, dqc:dqc + 1],
                         nsc[0:1, 0:1], reads=["nsc"], writes=["mqlr"],
                         accum=(dqc > 0))
                else:
                    r.op("vector", "tensor_tensor", mqlr[0:1, dqc:dqc + 1],
                         mqlr[0:1, dqc:dqc + 1], nsc[0:1, 0:1], OP.max,
                         reads=["nsc", "mqlr"], writes=["mqlr"], accum=True)

    gor = gout.rearrange("(rr blk) t -> rr blk t", blk=2064)
    govr = gor[:, 1024:2048, :].rearrange("rr (b s16) t -> rr b s16 t", s16=16)
    govr = govr.rearrange("rr (h tc) s16 t -> rr h tc s16 t", tc=8)

    for i in range(2):
        r.op("gpsimd", "memset", vhb[i][:, :, 128:129], 1.0, writes=[f"vones{i}"])

    def load_kv(h, sl):
        both = [f"khb{sl}", f"vhb{sl}"]
        r.op("sync", "dma_start", out=khb[sl][:, :, :],
             in_=gor[:, h * 128:(h + 1) * 128, :].rearrange("rr p t -> p rr t"),
             dma=f"dkh{sl}", reads=["gout"], writes=both)
        for rr in range(NR):
            src_v = govr[rr, h, :, :, :].rearrange("tc s (t8 dv) -> (s t8) tc dv",
                                                   dv=128)
            r.op("sync", "dma_start",
                 out=vhb[sl][:, rr * 8:(rr + 1) * 8, 0:128],
                 in_=src_v, dma=f"dkh{sl}", reads=["gout"],
                 writes=both, accum=True)

    iters = [(h, f, qb) for h in range(H) for f in range(GF) for qb in range(2)]

    r.op("sync", "dma_start", out=nrows4[0:1, :].rearrange("a (rr e) -> a rr e", e=NE),
         in_=gout.rearrange("(rr blk) t -> rr blk t", blk=2064)[:, 2048:2049, 0:NE]
         .rearrange("rr a e -> a rr e"),
         dma="dnr4", reads=["gout"], writes=["nrows4"])
    r.op("vector", "tensor_copy", mk2f[0:1, :], nrows4[0:1, 0:NE],
         reads=["nrows4"], writes=["mk2f"])
    for rr in range(1, NR):
        r.op("vector", "tensor_tensor", mk2f[0:1, :], mk2f[0:1, :],
             nrows4[0:1, rr * NE:(rr + 1) * NE], OP.max,
             reads=["nrows4", "mk2f"], writes=["mk2f"], accum=True)
    for h in range(H):
        r.op("vector", "tensor_scalar", bqrow[0:1, h * GF:(h + 1) * GF],
             mqlr[0:1, h * GF:(h + 1) * GF], mk2f[0:1, h:h + 1], None, OP.add,
             reads=["mqlr", "mk2f"], writes=["bqrow"], accum=(h > 0))
    r.op("vector", "tensor_scalar", bqrow[0:1, :], bqrow[0:1, :],
         -0.5 * SCALE, None, OP.mult, reads=["bqrow"], writes=["bqrow"], accum=True)
    r.op("vector", "tensor_copy", bqrowb[0:1, :], bqrow[0:1, :],
         reads=["bqrow"], writes=["bqrowb"])
    r.op("tensor", "matmul", ps_d[:, 0:32], onesrow[0:1, :], bqrowb[0:1, :],
         start=True, stop=True, reads=["onesrow", "bqrowb"], writes=["pd"])
    r.op("vector", "tensor_copy", biasqs[:, :], ps_d[:, 0:32],
         reads=["pd"], writes=["biasqs"])

    def emit_f_prep(h, f):
        qh = h * GF + f
        qs = qh % 2
        r.op("sync", "dma_start", out=qsl[qs][:, :],
             in_=qsc[qh * 128:(qh + 1) * 128, :], dma=f"dql{qs}",
             reads=[f"qsc{qh}"], writes=[f"qsl{qs}"])

    def emit_scores_step(idx, j):
        h, f, qb = iters[idx]
        sl = h % 2
        qh = h * GF + f
        qs = qh % 2
        par = idx % 2
        khflat = khb[sl][:, :, :].rearrange("p rr t -> p (rr t)")
        for kc in (2 * j, 2 * j + 1):
            bk = kc % 4
            r.op("tensor", "matmul", ps_a[:, bk * 512:(bk + 1) * 512],
                 khflat[:, kc * 128:(kc + 1) * 128],
                 qsl[qs][:, qb * 512:(qb + 1) * 512],
                 start=True, stop=True,
                 reads=[f"khb{sl}", f"qsl{qs}"], writes=[f"pa{bk}"])
        g0 = 2 * j
        bk0 = g0 % 4
        r.op("scalar", "activation",
             pbuf[par][:, g0 * 512:(g0 + 2) * 512],
             ps_a[:, bk0 * 512:(bk0 + 2) * 512], AF.Exp,
             bias=biasqs[:, qh:qh + 1], scale=SCALE,
             reads=[f"pa{bk0}", f"pa{bk0 + 1}", "biasqs"],
             writes=[f"pb{par}_{j}"])

    pend_ep = []
    step_serial = [0]

    def flush_eps(force=False):
        while pend_ep and (force or pend_ep[0][0] <= step_serial[0]):
            pend_ep.pop(0)[1]()

    def emit_av_step(idx, j):
        h, f, qb = iters[idx]
        sl = h % 2
        qh = h * GF + f
        par = idx % 2
        for m in range(8 * j, 8 * j + 8):
            qt = m // DC
            kc = m % DC
            bb_ = qt % 2
            pres = f"pb{bb_}"
            r.op("tensor", "matmul", ps_b[:, bb_ * 512:bb_ * 512 + 129],
                 pbuf[par][:, kc * 512 + qt * 128:kc * 512 + (qt + 1) * 128],
                 vhb[sl][:, kc, 0:129],
                 start=(kc == 0), stop=(kc == DC - 1),
                 reads=[f"pb{par}_{kc // 2}", f"vhb{sl}", f"vones{sl}"],
                 writes=[pres], accum=(kc > 0))
            if kc == DC - 1:
                qoff = qh * TPC + qb * 512 + qt * 128

                def ep(pres=pres, bb_=bb_, qoff=qoff, first=(qh + qb + qt == 0)):
                    r.op("vector", "reciprocal", recip[:, :],
                         ps_b[:, bb_ * 512 + 128:bb_ * 512 + 129],
                         reads=[pres], writes=["recip"])
                    r.op("vector", "tensor_scalar", astg[:, :],
                         ps_b[:, bb_ * 512:bb_ * 512 + 128], recip[:, 0:1], None,
                         OP.mult, reads=[pres, "recip"], writes=["astg"])
                    r.op("tensor", "transpose", ps_tr[:, 0:128], astg[:, :],
                         identb[:, :], reads=["astg", "identb"], writes=["ptr"])
                    r.op("vector", "tensor_copy", a_fm[:, qoff:qoff + 128],
                         ps_tr[:, 0:128], reads=["ptr"], writes=["a_fm"],
                         accum=(not first))
                pend_ep.append((step_serial[0] + 2, ep))

    def emit_interleaved(s_idx, a_idx):
        for j in range(16):
            step_serial[0] += 1
            flush_eps()
            if s_idx is not None:
                emit_scores_step(s_idx, j)
            if a_idx is not None:
                emit_av_step(a_idx, j)

    load_kv(0, 0)
    load_kv(1, 1)
    emit_f_prep(0, 0)
    emit_interleaved(0, None)
    for idx in range(len(iters)):
        nxt = idx + 1
        if nxt < len(iters):
            hn, fn, qbn = iters[nxt]
            if qbn == 0:
                emit_f_prep(hn, fn)
        emit_interleaved(nxt if nxt < len(iters) else None, idx)
        hh, ff, qq = iters[idx]
        if ff == GF - 1 and qq == 1 and hh + 2 < H:
            load_kv(hh + 2, (hh + 2) % 2)
    flush_eps(force=True)

    if "afm" in tapd:
        r.op("sync", "dma_start", out=tapd["afm"], in_=a_fm[:, :], dma="dtap",
             reads=["a_fm"])

    for _tb in range(TT):
        r.span(f"attnt{_tb}", 65536 + _tb * 8192, 65536 + (_tb + 1) * 8192)
        r.span(f"x2t{_tb}", 0, 65536)
        r.span(f"ffnt{_tb}", 65536 + _tb * 8192, 65536 + (_tb + 1) * 8192)

    for _tb in range(TT):
        r.span(f"asum{_tb}", r.spans["asum"][0] + _tb * 64,
               r.spans["asum"][0] + (_tb + 1) * 64)
        r.span(f"asq{_tb}", r.spans["asq"][0] + _tb * 64,
               r.spans["asq"][0] + (_tb + 1) * 64)
        r.span(f"fsum{_tb}", r.spans["asum"][0] + _tb * 64,
               r.spans["asum"][0] + (_tb + 1) * 64)
        r.span(f"fsq{_tb}", r.spans["asq"][0] + _tb * 64,
               r.spans["asq"][0] + (_tb + 1) * 64)

    def combine_stats(sum_res, sq_res, tb):
        r.op("vector", "tensor_reduce", exv[:, :], asum[:, tb, :], AX.X, OP.add,
             reads=[sum_res], writes=["exv"])
        r.op("vector", "tensor_reduce", ex2[:, :], asq[:, tb, :], AX.X, OP.add,
             reads=[sq_res], writes=["ex2"])
        r.op("vector", "tensor_scalar", bn2[:, 0:1], exv[:, :], 1.0 / D, None,
             OP.mult, reads=["exv"], writes=["bn2"])
        r.op("vector", "tensor_scalar", ex2[:, :], ex2[:, :], 1.0 / D, None,
             OP.mult, reads=["ex2"], writes=["ex2"], accum=True)
        r.op("vector", "tensor_tensor", exv[:, :], bn2[:, 0:1], bn2[:, 0:1],
             OP.mult, reads=["bn2"], writes=["exv"])
        r.op("vector", "tensor_tensor", sdv[:, :], ex2[:, :], exv[:, :],
             OP.subtract, reads=["ex2", "exv"], writes=["sdv"])
        r.op("vector", "tensor_scalar", sdv[:, :], sdv[:, :], EPS, None, OP.add,
             reads=["sdv"], writes=["sdv"], accum=True)
        r.op("scalar", "activation", sdv[:, :], sdv[:, :], AF.Sqrt,
             reads=["sdv"], writes=["sdv"])
        r.op("vector", "reciprocal", rstd[:, :], sdv[:, :], reads=["sdv"],
             writes=["rstd"])

    def emit_p5(tb):
        r.op("sync", "dma_start", out=x5[:, :], in_=xd[tb * 128:(tb + 1) * 128, :],
             dma="dx5", writes=["x5"])
        r.op("vector", "tensor_tensor", x5[:, :], x5[:, :],
             attn_sb[:, tb * D:(tb + 1) * D], OP.add,
             reads=["x5", f"attnt{tb}"], writes=["x5"], accum=True)
        r.op("scalar", "activation", ln2buf[:, :], x5[:, :], AF.Copy,
             accum_out=exv[:, 0:1], reads=["x5"], writes=["ln2buf", "exv"])
        r.op("scalar", "activation", ln2buf[:, :], x5[:, :], AF.Square,
             accum_out=ex2[:, 0:1], reads=["x5"], writes=["ln2buf", "ex2"])
        r.op("vector", "tensor_scalar", bn2[:, 0:1], exv[:, :], 1.0 / D, None,
             OP.mult, reads=["exv"], writes=["bn2"])
        r.op("vector", "tensor_scalar", ex2[:, :], ex2[:, :], 1.0 / D, None,
             OP.mult, reads=["ex2"], writes=["ex2"], accum=True)
        r.op("vector", "tensor_tensor", exv[:, :], bn2[:, 0:1], bn2[:, 0:1],
             OP.mult, reads=["bn2"], writes=["exv"])
        r.op("vector", "tensor_tensor", sdv[:, :], ex2[:, :], exv[:, :],
             OP.subtract, reads=["ex2", "exv"], writes=["sdv"])
        r.op("vector", "tensor_scalar", sdv[:, :], sdv[:, :], EPS, None, OP.add,
             reads=["sdv"], writes=["sdv"], accum=True)
        r.op("scalar", "activation", sdv[:, :], sdv[:, :], AF.Sqrt,
             reads=["sdv"], writes=["sdv"])
        r.op("vector", "reciprocal", rstd[:, :], sdv[:, :], reads=["sdv"],
             writes=["rstd"])
        r.op("vector", "tensor_scalar", ln2buf[:, :], x5[:, :], bn2[:, 0:1],
             rstd[:, 0:1], OP.subtract, OP.mult,
             reads=["x5", "bn2", "rstd"], writes=["ln2buf"])
        for dc in range(DC):
            r.op("tensor", "transpose", ps_tr[:, 0:128],
                 ln2buf[:, dc * 128:(dc + 1) * 128], identb[:, :],
                 reads=["ln2buf", "identb"], writes=["ptr"])
            dst = x2_fm[:, dc * TPC + tb * 128:dc * TPC + (tb + 1) * 128]
            if dc % 2 == 0:
                r.op("vector", "tensor_copy", dst, ps_tr[:, 0:128],
                     reads=["ptr"], writes=[f"x2t{tb}"], accum=(dc > 0))
            else:
                r.op("scalar", "copy", dst, ps_tr[:, 0:128],
                     reads=["ptr"], writes=[f"x2t{tb}"], accum=True)

    r.op("gpsimd", "dma_start", out=wobrow[0:1, :],
         in_=wob.rearrange("(a k) -> a k", a=1), dma="dwob", writes=["wobrow"])
    r.retouch("wst0")
    r.retouch("wst1")
    emit_stripe_load(wo, 0)
    for s in range(16):
        if s + 1 < 16:
            emit_stripe_load(wo, s + 1)
        sl = s % 2
        for tb in range(TT):
            if s == 15 and tb >= 2:
                emit_p5(tb - 2)
            b = next_bank()
            pres = f"pa{b}"
            po = ps_a[:, b * 512:b * 512 + 256]
            for dc in range(DC):
                r.op("tensor", "matmul", po,
                     a_fm[:, dc * TPC + tb * 128:dc * TPC + (tb + 1) * 128],
                     wst[sl][:, dc, :], start=(dc == 0), stop=False,
                     reads=[f"wst{sl}", "a_fm"], writes=[pres], accum=(dc > 0))
            r.op("tensor", "matmul", po, onesrow[0:1, :],
                 wobrow[0:1, s * 256:(s + 1) * 256], start=False, stop=True,
                 reads=["onesrow", "wobrow"], writes=[pres], accum=True)
            r.op("scalar", "activation",
                 attn_sb[:, tb * D + s * 256:tb * D + (s + 1) * 256], po, AF.Copy,
                 accum_out=asum[:, tb, s:s + 1],
                 reads=[pres], writes=[f"attnt{tb}", f"asum{tb}"], accum=(s > 0))
            r.op("scalar", "activation", sqscrap[:, :], po, AF.Square,
                 accum_out=asq[:, tb, s:s + 1],
                 reads=[pres], writes=["sqscrap", f"asq{tb}"])
            if s == 15:
                row = attn_sb[:, tb * D:(tb + 1) * D]
                combine_stats(f"asum{tb}", f"asq{tb}", tb)
                r.op("vector", "tensor_scalar", row, row, bn2[:, 0:1], rstd[:, 0:1],
                     OP.subtract, OP.mult, reads=[f"attnt{tb}", "bn2", "rstd"],
                     writes=[f"attnt{tb}"], accum=True)
                r.op("sync", "dma_start", out=lnasc[tb * 128:(tb + 1) * 128, :],
                     in_=row, dma=f"dlna{tb}", reads=[f"attnt{tb}"],
                     writes=["lnasc"], accum=(tb > 0))
    emit_p5(6)
    emit_p5(7)
    if "attn" in tapd:
        r.op("sync", "dma_start", out=tapd["attn"], in_=attn_sb[:, :], dma="dtap",
             reads=[f"attnt{t}" for t in range(TT)])
    if "x2fm" in tapd:
        r.op("sync", "dma_start", out=tapd["x2fm"], in_=x2_fm[:, :], dma="dtap",
             reads=[f"x2t{t}" for t in range(TT)])

    r.op("gpsimd", "dma_start", out=atsb[:, :, :],
         in_=at_.rearrange("(dc p) e -> p dc e", p=128), dma="dat", writes=["atsb"])
    r.op("gpsimd", "dma_start", out=ballt[:, :], in_=br_, dma="dbr", writes=["ballt"])
    r.op("gpsimd", "dma_start", out=bbrow[0:1, :],
         in_=bb.rearrange("(a k) -> a k", a=1), dma="dbb", writes=["bbrow"])
    for tb in range(TT):
        for dc in range(DC):
            r.op("tensor", "matmul", ps_b[:, 0:128],
                 x2_fm[:, dc * TPC + tb * 128:dc * TPC + (tb + 1) * 128],
                 atsb[:, dc, :], start=(dc == 0), stop=False,
                 reads=[f"x2t{tb}", "atsb"], writes=["pb0"], accum=(dc > 0))
            r.op("tensor", "matmul", ps_b[:, 128:136],
                 x2_fm[:, dc * TPC + tb * 128:dc * TPC + (tb + 1) * 128],
                 wrt[:, dc * 8:(dc + 1) * 8], start=False, stop=(dc == DC - 1),
                 reads=[f"x2t{tb}", "wrt"], writes=["pb0"], accum=True)
        r.op("scalar", "activation", hsb[:, :], ps_b[:, 0:136], AF.Copy,
             reads=["pb0"], writes=["hsb"])
        lg = hsb[:, 128:136]
        r.op("vector", "tensor_reduce", tm1[:, :], lg, AX.X, OP.max,
             reads=["hsb"], writes=["tm1"])
        r.op("vector", "tensor_scalar", teq1[:, :], lg, tm1[:, 0:1], None,
             OP.is_equal, reads=["hsb", "tm1"], writes=["teq1"])
        r.op("vector", "scalar_tensor_tensor", tlm[:, :], teq1[:, :], -1e30, lg,
             OP.mult, OP.add, reads=["teq1", "hsb"], writes=["tlm"])
        r.op("vector", "tensor_reduce", tm2[:, :], tlm[:, :], AX.X, OP.max,
             reads=["tlm"], writes=["tm2"])
        r.op("vector", "tensor_scalar", teq2[:, :], tlm[:, :], tm2[:, 0:1], None,
             OP.is_equal, reads=["tlm", "tm2"], writes=["teq2"])
        r.op("vector", "tensor_scalar", tm2[:, :], tm2[:, :], tm1[:, 0:1], None,
             OP.subtract, reads=["tm2", "tm1"], writes=["tm2"])
        r.op("scalar", "activation", trw2[:, :], tm2[:, :], AF.Sigmoid,
             reads=["tm2"], writes=["trw2"])
        r.op("vector", "tensor_scalar", trw1[:, :], trw2[:, :], -1.0, 1.0,
             OP.mult, OP.add, reads=["trw2"], writes=["trw1"])
        r.op("vector", "tensor_scalar", tt1[:, :], teq1[:, :], trw1[:, 0:1], None,
             OP.mult, reads=["teq1", "trw1"], writes=["tt1"])
        r.op("vector", "scalar_tensor_tensor", twts[:, :], teq2[:, :], trw2[:, 0:1],
             tt1[:, :], OP.mult, OP.add, reads=["teq2", "trw2", "tt1"],
             writes=["twts"])
        for e in range(NE):
            r.op("vector", "tensor_scalar", gtm[:, e * RK:(e + 1) * RK],
                 hsb[:, e * RK:(e + 1) * RK], twts[:, e:e + 1], LSCALE,
                 OP.mult, OP.mult, reads=["hsb", "twts"], writes=["gtm"],
                 accum=(e > 0))
        r.op("tensor", "transpose", ps_tr[:, 0:128], gtm[:, :], identb[:, :],
             reads=["gtm", "identb"], writes=["ptr"])
        r.op("vector", "tensor_copy", gfm[:, tb * 128:(tb + 1) * 128],
             ps_tr[:, 0:128], reads=["ptr"], writes=["gfm"], accum=(tb > 0))

    def emit_p7(tb):
        row = ffn_sb[:, tb * D:(tb + 1) * D]
        combine_stats(f"fsum{tb}", f"fsq{tb}", tb)
        r.op("vector", "tensor_scalar", row, row, bn2[:, 0:1], rstd[:, 0:1],
             OP.subtract, OP.mult, reads=[f"ffnt{tb}", "bn2", "rstd"],
             writes=[f"ffnt{tb}"], accum=True)
        for hf in range(2):
            par = (tb * 2 + hf) % 2
            cs = hf * 2048
            r.op("scalar", "dma_start", out=xh[par][:, :],
                 in_=xd[tb * 128:(tb + 1) * 128, cs:cs + 2048],
                 dma=f"dx7{par}", writes=[f"xh{par}"])
            r.op("gpsimd", "dma_start", out=lnh[par][:, :],
                 in_=lnasc[tb * 128:(tb + 1) * 128, cs:cs + 2048], dma=f"dl7{par}",
                 reads=["lnasc"], writes=[f"lnh{par}"])
            r.op("vector", "tensor_tensor", xh[par][:, :], xh[par][:, :],
                 lnh[par][:, :], OP.add,
                 reads=[f"xh{par}", f"lnh{par}"], writes=[f"xh{par}"], accum=True)
            r.op("vector", "tensor_tensor", xh[par][:, :], xh[par][:, :],
                 row[:, cs:cs + 2048], OP.add,
                 reads=[f"xh{par}", f"ffnt{tb}"], writes=[f"xh{par}"], accum=True)
            r.op("sync", "dma_start",
                 out=outd[tb * 128:(tb + 1) * 128, cs:cs + 2048],
                 in_=xh[par][:, :], dma=f"dout{par}", reads=[f"xh{par}"],
                 writes=["outd"], accum=(tb + hf > 0))

    r.retouch("wst0")
    r.retouch("wst1")
    emit_stripe_load(wbt, 0)
    for s in range(16):
        if s + 1 < 16:
            emit_stripe_load(wbt, s + 1)
        sl = s % 2
        for tb in range(TT):
            b = next_bank()
            pres = f"pa{b}"
            po = ps_a[:, b * 512:b * 512 + 256]
            for dc in range(DC):
                r.op("tensor", "matmul", po,
                     x2_fm[:, dc * TPC + tb * 128:dc * TPC + (tb + 1) * 128],
                     wst[sl][:, dc, :], start=(dc == 0), stop=False,
                     reads=[f"wst{sl}", f"x2t{tb}"], writes=[pres], accum=(dc > 0))
            r.op("tensor", "matmul", po, gfm[:, tb * 128:(tb + 1) * 128],
                 ballt[:, s * 256:(s + 1) * 256], start=False, stop=False,
                 reads=["gfm", "ballt"], writes=[pres], accum=True)
            r.op("tensor", "matmul", po, onesrow[0:1, :],
                 bbrow[0:1, s * 256:(s + 1) * 256], start=False, stop=True,
                 reads=["onesrow", "bbrow"], writes=[pres], accum=True)
            r.op("scalar", "activation",
                 ffn_sb[:, tb * D + s * 256:tb * D + (s + 1) * 256], po, AF.Copy,
                 accum_out=asum[:, tb, s:s + 1],
                 reads=[pres], writes=[f"ffnt{tb}", f"fsum{tb}"], accum=(s > 0))
            r.op("scalar", "activation", sqscrap[:, :], po, AF.Square,
                 accum_out=asq[:, tb, s:s + 1],
                 reads=[pres], writes=["sqscrap", f"fsq{tb}"])
    for tb in range(TT):
        emit_p7(tb)
    if "ffn" in tapd:
        r.op("sync", "dma_start", out=tapd["ffn"], in_=ffn_sb[:, :], dma="dtap",
             reads=[f"ffnt{t}" for t in range(TT)])

    r.final_wait("dout0")
    r.final_wait("dout1")
    if tapd:
        r.final_wait("dtap")

    r.materialize()

    with contextlib.ExitStack() as st2:
        esems = {e: st2.enter_context(nc.semaphore(f"se_{e}")) for e in ENG}
        dsems = {n: st2.enter_context(nc.semaphore(f"sd_{n}"))
                 for n in sorted(r.dma_sems)}
        block = st2.enter_context(nc.Block())

        @block.tensor
        def _(t):
            r.replay(t, "tensor", esems, dsems)

        @block.vector
        def _(v):
            r.replay(v, "vector", esems, dsems)

        @block.scalar
        def _(sc):
            r.replay(sc, "scalar", esems, dsems)

        @block.sync
        def _(sy):
            r.replay(sy, "sync", esems, dsems)

        @block.gpsimd
        def _(gp):
            r.replay(gp, "gpsimd", esems, dsems)

    stack.close()
    return nc


def kernel(L0_attn_x, L0_attn_wq_w, L0_attn_wq_b, L0_attn_wk_w, L0_attn_wk_b,
           L0_attn_wv_w, L0_attn_wv_b, L0_attn_wo_w, L0_attn_wo_b,
           L0_ffn_W_base, L0_ffn_b_base, L0_ffn_W_router, L0_ffn_A, L0_ffn_B):
    xf = np.ascontiguousarray(
        np.asarray(L0_attn_x, dtype=np.float32).reshape(NCORES * TPC, D))
    base = {
        "wq": np.ascontiguousarray(L0_attn_wq_w, dtype=np.float32),
        "wk": np.ascontiguousarray(L0_attn_wk_w, dtype=np.float32),
        "wv": np.ascontiguousarray(L0_attn_wv_w, dtype=np.float32),
        "wo": np.ascontiguousarray(L0_attn_wo_w, dtype=np.float32),
        "wqb": np.ascontiguousarray(L0_attn_wq_b, dtype=np.float32),
        "wkb": np.ascontiguousarray(L0_attn_wk_b, dtype=np.float32),
        "wvb": np.ascontiguousarray(L0_attn_wv_b, dtype=np.float32),
        "wob": np.ascontiguousarray(L0_attn_wo_b, dtype=np.float32),
        "wbt": np.ascontiguousarray(np.asarray(L0_ffn_W_base, dtype=np.float32).T),
        "bb": np.ascontiguousarray(L0_ffn_b_base, dtype=np.float32),
        "wr": np.ascontiguousarray(L0_ffn_W_router, dtype=np.float32),
        "at": np.ascontiguousarray(
            np.asarray(L0_ffn_A, dtype=np.float32).reshape(NE * RK, D).T),
        "br": np.ascontiguousarray(
            np.asarray(L0_ffn_B, dtype=np.float32).transpose(0, 2, 1).reshape(NE * RK, D)),
    }
    in_maps = []
    for c in range(NCORES):
        m = dict(base)
        m["x"] = np.ascontiguousarray(xf[c * TPC:(c + 1) * TPC])
        in_maps.append(m)
    nc = build_nc(taps=_DEBUG_TAPS)
    res = run_bass_kernel_spmd(nc, in_maps, core_ids=list(range(NCORES)))
    out = np.concatenate([res.results[c]["out"] for c in range(NCORES)], axis=0)
    return np.ascontiguousarray(out.reshape(2, SL, D), dtype=np.float32)


# version 07
# speedup vs baseline: 1.1929x; 1.0182x over previous
import math
import contextlib
import numpy as np
import concourse.bass as bass
import concourse.mybir as mybir
from concourse.alu_op_type import AluOpType as OP
from concourse.bass_utils import run_bass_kernel_spmd

F32 = mybir.dt.float32
BF16 = mybir.dt.bfloat16
AF = mybir.ActivationFunctionType
AX = mybir.AxisListType

NCORES = 8
D = 4096
DC = 32
TPC = 1024
TT = 8
KV = 1024
H = 8
GF = 4
HD = 128
SL = 4096
NR = 4
SCALE = 1.0 / math.sqrt(HD)
EPS = 1e-5
NE = 8
RK = 16
LSCALE = 2.0

ENG = ("tensor", "vector", "scalar", "sync", "gpsimd")

_DEBUG_TAPS = []


class Rec:

    def __init__(self):
        self.ops = []
        self.wl = {}
        self.r = {}
        self.spans = {}
        self.touched = set()

    def span(self, res, lo, hi):
        self.spans[res] = (lo, hi)

    def retouch(self, res):
        self.touched.discard(res)

    def op(self, eng, meth, *args, reads=(), writes=(), accum=False, dma=None,
           cc=False, **kwargs):
        i = len(self.ops)
        deps = set()
        for res in writes:
            if res not in self.touched and res in self.spans:
                lo, hi = self.spans[res]
                for other, (lo2, hi2) in self.spans.items():
                    if other == res or hi2 <= lo or hi <= lo2:
                        continue
                    deps.update(self.wl.get(other, ()))
                    deps.update(self.r.get(other, ()))
                self.touched.add(res)
        for res in reads:
            deps.update(self.wl.get(res, ()))
        if not accum:
            for res in writes:
                deps.update(self.wl.get(res, ()))
                deps.update(self.r.get(res, ()))
        for res in writes:
            if accum:
                self.wl.setdefault(res, []).append(i)
            else:
                self.wl[res] = [i]
                self.r[res] = []
        for res in reads:
            self.r.setdefault(res, []).append(i)
        self.ops.append(dict(i=i, eng=eng, meth=meth, args=args, kwargs=kwargs,
                             deps=deps, dma=dma, cc=cc, need=False, sem=None,
                             val=None))
        return i

    def final_wait(self, dma_name):
        self.ops.append(dict(i=len(self.ops), eng="sync", meth="_final_wait",
                             args=(dma_name,), kwargs={}, deps=set(), dma=None,
                             cc=False, need=False, sem=None, val=None))

    def materialize(self):
        for o in self.ops:
            best = {}
            for d in o["deps"]:
                p = self.ops[d]
                key = ("dma", p["dma"]) if p["dma"] is not None else ("eng", p["eng"])
                if key not in best or d > best[key]:
                    best[key] = d
            o["deps"] = set(best.values())
        for o in self.ops:
            for d in o["deps"]:
                self.ops[d]["need"] = True
        engct = {e: 0 for e in ENG}
        dmact = {}
        self.dma_sems = set()
        for o in self.ops:
            if o["dma"] is not None:
                step = 1 if o["cc"] else 16
                dmact[o["dma"]] = dmact.get(o["dma"], 0) + step
                o["sem"], o["val"] = ("dma", o["dma"]), dmact[o["dma"]]
                self.dma_sems.add(o["dma"])
            elif o["need"]:
                engct[o["eng"]] += 1
                o["sem"], o["val"] = ("eng", o["eng"]), engct[o["eng"]]
        hw = {e: {} for e in ENG}
        for o in self.ops:
            if o["meth"] == "_final_wait":
                o["waits"] = [(("dma", o["args"][0]), dmact[o["args"][0]])]
                continue
            waits = []
            for d in sorted(o["deps"]):
                p = self.ops[d]
                if p["sem"] is None:
                    continue
                key = p["sem"]
                cur = hw[o["eng"]].get(key, 0)
                if p["val"] > cur:
                    waits.append((key, p["val"]))
                    hw[o["eng"]][key] = p["val"]
            o["waits"] = waits

    def replay(self, engobj, eng, esems, dsems):
        for o in self.ops:
            if o["eng"] != eng:
                continue
            for (kind, name), v in o.get("waits", ()):
                engobj.wait_ge(dsems[name] if kind == "dma" else esems[name], v)
            if o["meth"] == "_final_wait":
                continue
            inst = getattr(engobj, o["meth"])(*o["args"], **o["kwargs"])
            if o["sem"] is not None:
                if o["sem"][0] == "dma":
                    inst.then_inc(dsems[o["sem"][1]], 1 if o["cc"] else 16)
                else:
                    inst.then_inc(esems[eng], 1)


def build_nc(taps=()):
    nc = bass.Bass("TRN2", target_bir_lowering=False, debug=False)

    xd = nc.dram_tensor("x", [TPC, D], F32, kind="ExternalInput").ap()
    wq = nc.dram_tensor("wq", [D, D], F32, kind="ExternalInput").ap()
    wk = nc.dram_tensor("wk", [D, KV], F32, kind="ExternalInput").ap()
    wv = nc.dram_tensor("wv", [D, KV], F32, kind="ExternalInput").ap()
    wo = nc.dram_tensor("wo", [D, D], F32, kind="ExternalInput").ap()
    wqb = nc.dram_tensor("wqb", [D], F32, kind="ExternalInput").ap()
    wkb = nc.dram_tensor("wkb", [KV], F32, kind="ExternalInput").ap()
    wvb = nc.dram_tensor("wvb", [KV], F32, kind="ExternalInput").ap()
    wob = nc.dram_tensor("wob", [D], F32, kind="ExternalInput").ap()
    wbt = nc.dram_tensor("wbt", [D, D], F32, kind="ExternalInput").ap()
    bb = nc.dram_tensor("bb", [D], F32, kind="ExternalInput").ap()
    wr = nc.dram_tensor("wr", [NE, D], F32, kind="ExternalInput").ap()
    at_ = nc.dram_tensor("at", [D, NE * RK], F32, kind="ExternalInput").ap()
    br_ = nc.dram_tensor("br", [NE * RK, D], F32, kind="ExternalInput").ap()
    outd = nc.dram_tensor("out", [TPC, D], F32, kind="ExternalOutput").ap()

    qsc = nc.dram_tensor("qscratch", [D, TPC], BF16).ap()
    gin = nc.dram_tensor("gin", [2064, KV], BF16).ap()
    gout = nc.dram_tensor("gout", [NR * 2064, KV], BF16).ap()
    lnasc = nc.dram_tensor("lnascratch", [TPC, D], BF16).ap()

    tapd = {}
    tap_shapes = {"xfm": [128, DC * TPC], "afm": [128, DC * TPC],
                  "x2fm": [128, DC * TPC], "attn": [128, TT * D],
                  "ffn": [128, TT * D], "pbuf": [128, 16384],
                  "qkv": [2048, KV]}
    for t in taps:
        tapd[t] = nc.dram_tensor("tap_" + t, tap_shapes[t], BF16,
                                 kind="ExternalOutput").ap()

    r = Rec()

    SB0 = 16512

    def sb(name, shape, dtype, off):
        t = nc.alloc_sbuf_tensor_at(name, shape, dtype, offset=SB0 + off)
        nbytes = 1
        for s in shape[1:]:
            nbytes *= s
        nbytes *= 2 if dtype == BF16 else 4
        r.span(name, off, off + nbytes)
        return t

    x_fm = sb("x_fm", [128, DC * TPC], BF16, 0)
    for _tb in range(TT):
        r.span(f"xfmt{_tb}", 0, 65536)
    a_fm = sb("a_fm", [128, DC * TPC], BF16, 0)
    x2_fm = sb("x2_fm", [128, DC * TPC], BF16, 0)
    ln1bufs = [sb(f"ln1buf{i}", [128, D], BF16, 65536 + i * 8192) for i in range(2)]
    wrtmp = sb("wrtmp", [NE, D], BF16, 65536 + 8192)
    pbuf = [sb(f"pbuf{i}", [128, 16384], BF16, 65536 + i * 32768) for i in range(2)]
    for _p in range(2):
        for _g in range(16):
            r.span(f"pb{_p}_{_g}", 65536 + _p * 32768 + _g * 2048,
                   65536 + _p * 32768 + (_g + 1) * 2048)
    qsl = [sb(f"qsl{i}", [128, TPC], BF16, 164864 + i * 2048) for i in range(2)]
    sq8k = sb("sq8k", [128, D], BF16, 168960)
    knrow = sb("knrow", [1, 512], F32, 177152)
    attn_sb = sb("attn_sb", [128, TT * D], BF16, 65536)
    ffn_sb = sb("ffn_sb", [128, TT * D], BF16, 65536)
    xsl = [sb(f"xsl{i}", [128, D], F32, 131072 + i * 16384) for i in range(2)]
    qtg = [sb(f"qtg{i}", [128, 512], BF16, 131072 + i * 1024) for i in range(2)]
    ktg = [sb(f"ktg{i}", [128, 512], BF16, 133120 + i * 1024) for i in range(2)]
    vtg = [sb(f"vtg{i}", [128, 256], BF16, 135168 + i * 512) for i in range(2)]
    khb = [sb(f"khb{i}", [128, NR, 1024], BF16, 131072 + i * 8192) for i in range(2)]
    vhb = [sb(f"vhb{i}", [128, DC, 130], BF16, 147456 + i * 8320) for i in range(2)]
    wobrow = sb("wobrow", [1, D], BF16, 155648)
    x5 = sb("x5", [128, D], F32, 131072)
    ln2buf = sb("ln2buf", [128, D], BF16, 147456)
    atsb = sb("atsb", [128, DC, NE * RK], BF16, 131072)
    ballt = sb("ballt", [128, D], BF16, 139264)
    bbrow = sb("bbrow", [1, D], BF16, 147456)
    xh = [sb(f"xh{i}", [128, 2048], F32, 131072 + i * 8192) for i in range(2)]
    lnh = [sb(f"lnh{i}", [128, 2048], BF16, 147456 + i * 4096) for i in range(2)]
    wst = [sb(f"wst{i}", [128, DC, 256], BF16, 164864 + i * 16384) for i in range(2)]
    wst4 = wst + [sb(f"wst{2 + i}", [128, DC, 256], BF16, 98304 + i * 16384) for i in range(2)]
    M = 197632

    def msb(name, shape, dtype):
        nonlocal M
        t = sb(name, shape, dtype, M)
        nbytes = 1
        for s in shape[1:]:
            nbytes *= s
        nbytes *= 2 if dtype == BF16 else 4
        M += (nbytes + 31) // 32 * 32
        return t

    identf = msb("identf", [128, 128], F32)
    identb = msb("identb", [128, 128], BF16)
    iot0 = msb("iot0", [128, 128], F32)
    iot1 = msb("iot1", [128, 128], F32)
    wqb_sb = msb("wqb_sb", [128, 32], F32)
    wkb_sb = msb("wkb_sb", [128, 8], F32)
    braw = msb("braw", [32, 128], F32)
    onescol = msb("onescol", [128, 1], BF16)
    onesrow = msb("onesrow", [1, 128], BF16)
    wvbrow = msb("wvbrow", [1, KV], BF16)
    wrt = msb("wrt", [128, DC * NE], BF16)
    biasq = msb("biasq", [128, 1], F32)
    bqbf = msb("bqbf", [1, 1], BF16)
    mk2 = msb("mk2", [1, 1], F32)
    mq2 = msb("mq2", [1, 1], F32)
    nsc = msb("nsc", [1, 1], F32)
    bnb = msb("bnb", [128, 8, 6], F32)
    bn2 = msb("bn2", [128, 2], F32)
    sdv = msb("sdv", [128, 1], F32)
    rstd = msb("rstd", [128, 1], F32)
    recip = msb("recip", [128, 1], F32)
    astg = msb("astg", [128, 128], BF16)
    hsb = msb("hsb", [128, 136], F32)
    gtm = msb("gtm", [128, 128], BF16)
    gfm = msb("gfm", [128, TT * 128], BF16)
    tm1 = msb("tm1", [128, 1], F32)
    tm2 = msb("tm2", [128, 1], F32)
    teq1 = msb("teq1", [128, NE], F32)
    teq2 = msb("teq2", [128, NE], F32)
    tlm = msb("tlm", [128, NE], F32)
    trw2 = msb("trw2", [128, 1], F32)
    trw1 = msb("trw1", [128, 1], F32)
    tt1 = msb("tt1", [128, NE], F32)
    twts = msb("twts", [128, NE], F32)
    ksq5 = msb("ksq5", [128, 512], BF16)
    nrow = msb("nrow", [1, 512], F32)
    mklr = msb("mklr", [1, NE], F32)
    mqlr = msb("mqlr", [1, 32], F32)
    mk2f = msb("mk2f", [1, NE], F32)
    nrows4 = msb("nrows4", [1, 32], BF16)
    bqrow = msb("bqrow", [1, 32], F32)
    bqrowb = msb("bqrowb", [1, 32], BF16)
    biasqs = msb("biasqs", [128, 32], F32)
    asum = msb("asum", [128, TT, 16], F32)
    asq = msb("asq", [128, TT, 16], F32)
    sqscrap = msb("sqscrap", [128, 256], BF16)
    exv = msb("exv", [128, 1], F32)
    ex2 = msb("ex2", [128, 1], F32)
    assert SB0 + M <= 229344, M

    stack = contextlib.ExitStack()
    ps_a = stack.enter_context(nc.psum_tensor("ps_a", [128, 2048], F32))
    ps_b = stack.enter_context(nc.psum_tensor("ps_b", [128, 1024], F32))
    ps_tr = stack.enter_context(nc.psum_tensor("ps_tr", [128, 1024], BF16))
    ps_d = stack.enter_context(nc.psum_tensor("ps_d", [128, 512], F32))

    r.op("gpsimd", "iota", iot0[:, :], [[1, 128]], channel_multiplier=0,
         allow_small_or_imprecise_dtypes=True, writes=["iot0"])
    r.op("gpsimd", "iota", iot1[:, :], [[0, 128]], channel_multiplier=1,
         allow_small_or_imprecise_dtypes=True, writes=["iot1"])
    r.op("vector", "tensor_tensor", identf[:, :], iot0[:, :], iot1[:, :], OP.is_equal,
         reads=["iot0", "iot1"], writes=["identf"])
    r.op("vector", "tensor_copy", identb[:, :], identf[:, :],
         reads=["identf"], writes=["identb"])
    r.op("gpsimd", "memset", onescol[:, :], 1.0, writes=["onescol"])
    r.op("gpsimd", "memset", onesrow[:, :], 1.0, writes=["onesrow"])
    for i in range(2):
        r.span(f"vones{i}", *r.spans[f"vhb{i}"])
    r.op("sync", "dma_start", out=braw[:, :], in_=wqb.rearrange("(a p) -> a p", p=128),
         dma="dbias", writes=["braw"])
    r.op("tensor", "transpose", ps_d[:, 0:32], braw[:, :], identf[0:32, 0:32],
         reads=["braw", "identf"], writes=["pd"])
    r.op("vector", "tensor_copy", wqb_sb[:, :], ps_d[:, 0:32], reads=["pd"],
         writes=["wqb_sb"])
    r.op("sync", "dma_start", out=braw[0:8, :], in_=wkb.rearrange("(a p) -> a p", p=128),
         dma="dbias", reads=["pd"], writes=["braw"])
    r.op("tensor", "transpose", ps_d[:, 32:40], braw[0:8, :], identf[0:8, 0:8],
         reads=["braw", "identf"], writes=["pd"])
    r.op("vector", "tensor_copy", wkb_sb[:, :], ps_d[:, 32:40], reads=["pd"],
         writes=["wkb_sb"])
    r.op("gpsimd", "dma_start", out=wvbrow[0:1, :],
         in_=wvb.rearrange("(a k) -> a k", a=1), dma="dwvb", writes=["wvbrow"])
    r.op("gpsimd", "dma_start", out=wrtmp[:, :], in_=wr, dma="dwrt", writes=["wrtmp"])
    for dc in range(DC):
        r.op("tensor", "transpose", ps_tr[:, 0:8], wrtmp[:, dc * 128:(dc + 1) * 128],
             identb[0:8, 0:8], reads=["wrtmp", "identb"], writes=["ptr"])
        r.op("vector", "tensor_copy", wrt[:, dc * 8:(dc + 1) * 8], ps_tr[:, 0:8],
             reads=["ptr"], writes=["wrt"], accum=(dc > 0))

    def layernorm_stats(src_ap, src_res):
        for i in range(8):
            r.op("vector", "bn_stats", bnb[:, i, :], src_ap[:, i * 512:(i + 1) * 512],
                 reads=[src_res], writes=["bnb"], accum=(i > 0))
        r.op("vector", "bn_aggr", bn2[:, :], bnb[:, :, :], reads=["bnb"], writes=["bn2"])
        r.op("vector", "tensor_scalar", sdv[:, :], bn2[:, 1:2], EPS, None, OP.add,
             reads=["bn2"], writes=["sdv"])
        r.op("scalar", "activation", sdv[:, :], sdv[:, :], AF.Sqrt,
             reads=["sdv"], writes=["sdv"])
        r.op("vector", "reciprocal", rstd[:, :], sdv[:, :], reads=["sdv"], writes=["rstd"])

    def emit_stripe_load(wsrc, s):
        sl = s % 2
        src = wsrc.rearrange("(dc p) n -> p dc n", p=128)[:, :, s * 256:(s + 1) * 256]
        r.op("gpsimd", "dma_start", out=wst[sl][:, :, :], in_=src,
             dma=f"dw{sl}", writes=[f"wst{sl}"])
        return sl

    pctr = [0]

    def next_bank():
        b = pctr[0] % 4
        pctr[0] += 1
        return b

    for tb in range(TT):
        sl = tb % 2
        r.op("sync", "dma_start", out=xsl[sl][:, :], in_=xd[tb * 128:(tb + 1) * 128, :],
             dma=f"dx{sl}", writes=[f"xsl{sl}"])
        layernorm_stats(xsl[sl][:, :], f"xsl{sl}")
        r.op("vector", "tensor_scalar", ln1bufs[sl][:, :], xsl[sl][:, :], bn2[:, 0:1],
             rstd[:, 0:1], OP.subtract, OP.mult,
             reads=[f"xsl{sl}", "bn2", "rstd"], writes=[f"ln1buf{sl}"])
        for dc in range(DC):
            r.op("tensor", "transpose", ps_tr[:, 0:128],
                 ln1bufs[sl][:, dc * 128:(dc + 1) * 128], identb[:, :],
                 reads=[f"ln1buf{sl}", "identb"], writes=["ptr"])
            r.op("vector", "tensor_copy",
                 x_fm[:, dc * TPC + tb * 128:dc * TPC + (tb + 1) * 128],
                 ps_tr[:, 0:128], reads=["ptr"], writes=[f"xfmt{tb}"],
                 accum=(dc > 0))
    if "xfm" in tapd:
        r.op("sync", "dma_start", out=tapd["xfm"], in_=x_fm[:, :], dma="dtap",
             reads=[f"xfmt{t}" for t in range(TT)])

    stripe_plan = [(wk, s) for s in range(4)] + [(wv, s) for s in range(4)] + \
                  [(wq, s) for s in range(16)]

    def p1_load(gidx):
        if gidx >= len(stripe_plan):
            return
        wsrc, s = stripe_plan[gidx]
        slot = gidx % 4
        src_ap = wsrc.rearrange("(dc p) n -> p dc n", p=128)[:, :, s * 256:(s + 1) * 256]
        r.op("gpsimd", "dma_start", out=wst4[slot][:, :, :], in_=src_ap,
             dma=f"dw{slot}", writes=[f"wst{slot}"])

    for g in range(3):
        p1_load(g)

    for s in range(4):
        gidx = s
        p1_load(gidx + 3)
        sl = gidx % 4
        for sub in range(2):
            dkc = s * 2 + sub
            for tb in range(2):
                b = next_bank()
                pres = f"pa{b}"
                po = ps_a[:, b * 512:(b + 1) * 512]
                for dc in range(DC):
                    r.op("tensor", "matmul", po,
                         wst4[sl][:, dc, sub * 128:(sub + 1) * 128],
                         x_fm[:, dc * TPC + tb * 512:dc * TPC + (tb + 1) * 512],
                         start=(dc == 0), stop=(dc == DC - 1),
                         reads=[f"wst{sl}", f"xfmt{tb * 4}", f"xfmt{tb * 4 + 1}",
                                f"xfmt{tb * 4 + 2}", f"xfmt{tb * 4 + 3}"],
                         writes=[pres], accum=(dc > 0))
                st = (dkc * 2 + tb) % 2
                r.op("vector", "tensor_scalar", ktg[st][:, :], po,
                     wkb_sb[:, dkc:dkc + 1], None, OP.add,
                     reads=[pres, "wkb_sb"], writes=[f"ktg{st}"])
                r.op("sync", "dma_start",
                     out=gin[dkc * 128:(dkc + 1) * 128, tb * 512:(tb + 1) * 512],
                     in_=ktg[st][:, :], dma=f"dkst{st}", reads=[f"ktg{st}"],
                     writes=["gin_k"], accum=(dkc + tb > 0))
                r.op("scalar", "activation", ksq5[:, :], ktg[st][:, :], AF.Square,
                     reads=[f"ktg{st}"], writes=["ksq5"])
                r.op("tensor", "matmul", ps_d[0:1, 0:512], onescol[:, 0:1],
                     ksq5[:, :], start=True, stop=True,
                     reads=["ksq5", "onescol"], writes=["pd"])
                r.op("vector", "tensor_copy", nrow[0:1, :], ps_d[0:1, 0:512],
                     reads=["pd"], writes=["nrow"])
                r.op("vector", "tensor_reduce", nsc[0:1, 0:1], nrow[0:1, :], AX.X,
                     OP.max, reads=["nrow"], writes=["nsc"])
                if tb == 0:
                    r.op("vector", "tensor_copy", mklr[0:1, dkc:dkc + 1],
                         nsc[0:1, 0:1], reads=["nsc"], writes=["mklr"],
                         accum=(dkc > 0))
                else:
                    r.op("vector", "tensor_tensor", mklr[0:1, dkc:dkc + 1],
                         mklr[0:1, dkc:dkc + 1], nsc[0:1, 0:1], OP.max,
                         reads=["nsc", "mklr"], writes=["mklr"], accum=True)

    vgd = gin[1024:2048, :].rearrange("(b s16) t -> b s16 t", s16=16)
    vgd = vgd.rearrange("(h tc) s16 t -> h tc s16 t", tc=8)
    for s in range(4):
        gidx = 4 + s
        p1_load(gidx + 3)
        sl = gidx % 4
        for tb in range(TT):
            b = next_bank()
            pres = f"pa{b}"
            po = ps_a[:, b * 512:b * 512 + 256]
            for dc in range(DC):
                r.op("tensor", "matmul", po,
                     x_fm[:, dc * TPC + tb * 128:dc * TPC + (tb + 1) * 128],
                     wst4[sl][:, dc, :], start=(dc == 0), stop=False,
                     reads=[f"wst{sl}", f"xfmt{tb}"], writes=[pres], accum=(dc > 0))
            r.op("tensor", "matmul", po, onesrow[0:1, :],
                 wvbrow[0:1, s * 256:(s + 1) * 256], start=False, stop=True,
                 reads=["onesrow", "wvbrow"], writes=[pres], accum=True)
            st = tb % 2
            r.op("scalar", "activation", vtg[st][:, :], po, AF.Copy,
                 reads=[pres], writes=[f"vtg{st}"])
            dst = vgd[s * 2:(s + 1) * 2, tb, :, :].rearrange(
                "h s16 (t8 dv) -> (s16 t8) h dv", dv=128)
            r.op("sync", "dma_start", out=dst,
                 in_=vtg[st][:, :].rearrange("p (h dv) -> p h dv", dv=128),
                 dma=f"dvst{st}", reads=[f"vtg{st}"], writes=["gin_v"],
                 accum=(s + tb > 0))

    if "qkv" in tapd:
        r.op("sync", "dma_start", out=tapd["qkv"], in_=gin[0:2048, :], dma="dtap",
             reads=["gin_k", "gin_v"])

    r.op("gpsimd", "memset", ksq5[0:16, :], 0.0, writes=["ksq5"])
    r.op("gpsimd", "dma_start", out=gin[2048:2064, 0:512], in_=ksq5[0:16, :],
         dma="dnrm", reads=["ksq5"], writes=["gin_n"])
    r.op("gpsimd", "dma_start", out=gin[2048:2064, 512:1024], in_=ksq5[0:16, :],
         dma="dnrm", reads=["ksq5"], writes=["gin_n"], accum=True)
    r.op("gpsimd", "dma_start", out=gin[2048:2049, 0:NE], in_=mklr[0:1, :],
         dma="dnrm", reads=["mklr"], writes=["gin_n"])

    r.op("gpsimd", "collective_compute", "AllGather", OP.bypass,
         replica_groups=[[0, 1, 2, 3], [4, 5, 6, 7]],
         ins=[gin.opt()], outs=[gout.opt()],
         reads=["gin_k", "gin_v", "gin_n"], writes=["gout"], dma="dcc", cc=True)

    for s in range(16):
        gidx = 8 + s
        p1_load(gidx + 3)
        sl = gidx % 4
        for sub in range(2):
            dqc = s * 2 + sub
            for tb in range(2):
                b = next_bank()
                pres = f"pa{b}"
                po = ps_a[:, b * 512:(b + 1) * 512]
                for dc in range(DC):
                    r.op("tensor", "matmul", po,
                         wst4[sl][:, dc, sub * 128:(sub + 1) * 128],
                         x_fm[:, dc * TPC + tb * 512:dc * TPC + (tb + 1) * 512],
                         start=(dc == 0), stop=(dc == DC - 1),
                         reads=[f"wst{sl}", f"xfmt{tb * 4}", f"xfmt{tb * 4 + 1}",
                                f"xfmt{tb * 4 + 2}", f"xfmt{tb * 4 + 3}"],
                         writes=[pres], accum=(dc > 0))
                st = (dqc * 2 + tb) % 2
                r.op("vector", "tensor_scalar", qtg[st][:, :], po,
                     wqb_sb[:, dqc:dqc + 1], None, OP.add,
                     reads=[pres, "wqb_sb"], writes=[f"qtg{st}"])
                r.op("sync", "dma_start",
                     out=qsc[dqc * 128:(dqc + 1) * 128, tb * 512:(tb + 1) * 512],
                     in_=qtg[st][:, :], dma=f"dqst{st}", reads=[f"qtg{st}"],
                     writes=[f"qsc{dqc}"], accum=(tb > 0))
                r.op("scalar", "activation", ksq5[:, :], qtg[st][:, :], AF.Square,
                     reads=[f"qtg{st}"], writes=["ksq5"])
                r.op("tensor", "matmul", ps_d[0:1, 0:512], onescol[:, 0:1],
                     ksq5[:, :], start=True, stop=True,
                     reads=["ksq5", "onescol"], writes=["pd"])
                r.op("vector", "tensor_copy", nrow[0:1, :], ps_d[0:1, 0:512],
                     reads=["pd"], writes=["nrow"])
                r.op("vector", "tensor_reduce", nsc[0:1, 0:1], nrow[0:1, :], AX.X,
                     OP.max, reads=["nrow"], writes=["nsc"])
                if tb == 0:
                    r.op("vector", "tensor_copy", mqlr[0:1, dqc:dqc + 1],
                         nsc[0:1, 0:1], reads=["nsc"], writes=["mqlr"],
                         accum=(dqc > 0))
                else:
                    r.op("vector", "tensor_tensor", mqlr[0:1, dqc:dqc + 1],
                         mqlr[0:1, dqc:dqc + 1], nsc[0:1, 0:1], OP.max,
                         reads=["nsc", "mqlr"], writes=["mqlr"], accum=True)

    gor = gout.rearrange("(rr blk) t -> rr blk t", blk=2064)
    govr = gor[:, 1024:2048, :].rearrange("rr (b s16) t -> rr b s16 t", s16=16)
    govr = govr.rearrange("rr (h tc) s16 t -> rr h tc s16 t", tc=8)

    for i in range(2):
        r.op("gpsimd", "memset", vhb[i][:, :, 128:129], 1.0, writes=[f"vones{i}"])

    def load_kv(h, sl):
        both = [f"khb{sl}", f"vhb{sl}"]
        r.op("sync", "dma_start", out=khb[sl][:, :, :],
             in_=gor[:, h * 128:(h + 1) * 128, :].rearrange("rr p t -> p rr t"),
             dma=f"dkh{sl}", reads=["gout"], writes=both)
        for rr in range(NR):
            src_v = govr[rr, h, :, :, :].rearrange("tc s (t8 dv) -> (s t8) tc dv",
                                                   dv=128)
            r.op("sync", "dma_start",
                 out=vhb[sl][:, rr * 8:(rr + 1) * 8, 0:128],
                 in_=src_v, dma=f"dkh{sl}", reads=["gout"],
                 writes=both, accum=True)

    iters = [(h, f, qb) for h in range(H) for f in range(GF) for qb in range(2)]

    r.op("sync", "dma_start", out=nrows4[0:1, :].rearrange("a (rr e) -> a rr e", e=NE),
         in_=gout.rearrange("(rr blk) t -> rr blk t", blk=2064)[:, 2048:2049, 0:NE]
         .rearrange("rr a e -> a rr e"),
         dma="dnr4", reads=["gout"], writes=["nrows4"])
    r.op("vector", "tensor_copy", mk2f[0:1, :], nrows4[0:1, 0:NE],
         reads=["nrows4"], writes=["mk2f"])
    for rr in range(1, NR):
        r.op("vector", "tensor_tensor", mk2f[0:1, :], mk2f[0:1, :],
             nrows4[0:1, rr * NE:(rr + 1) * NE], OP.max,
             reads=["nrows4", "mk2f"], writes=["mk2f"], accum=True)
    for h in range(H):
        r.op("vector", "tensor_scalar", bqrow[0:1, h * GF:(h + 1) * GF],
             mqlr[0:1, h * GF:(h + 1) * GF], mk2f[0:1, h:h + 1], None, OP.add,
             reads=["mqlr", "mk2f"], writes=["bqrow"], accum=(h > 0))
    r.op("vector", "tensor_scalar", bqrow[0:1, :], bqrow[0:1, :],
         -0.5 * SCALE, None, OP.mult, reads=["bqrow"], writes=["bqrow"], accum=True)
    r.op("vector", "tensor_copy", bqrowb[0:1, :], bqrow[0:1, :],
         reads=["bqrow"], writes=["bqrowb"])
    r.op("tensor", "matmul", ps_d[:, 0:32], onesrow[0:1, :], bqrowb[0:1, :],
         start=True, stop=True, reads=["onesrow", "bqrowb"], writes=["pd"])
    r.op("vector", "tensor_copy", biasqs[:, :], ps_d[:, 0:32],
         reads=["pd"], writes=["biasqs"])

    def emit_f_prep(h, f):
        qh = h * GF + f
        qs = qh % 2
        r.op("sync", "dma_start", out=qsl[qs][:, :],
             in_=qsc[qh * 128:(qh + 1) * 128, :], dma=f"dql{qs}",
             reads=[f"qsc{qh}"], writes=[f"qsl{qs}"])

    def emit_scores_step(idx, j):
        h, f, qb = iters[idx]
        sl = h % 2
        qh = h * GF + f
        qs = qh % 2
        par = idx % 2
        khflat = khb[sl][:, :, :].rearrange("p rr t -> p (rr t)")
        for kc in (2 * j, 2 * j + 1):
            bk = kc % 4
            r.op("tensor", "matmul", ps_a[:, bk * 512:(bk + 1) * 512],
                 khflat[:, kc * 128:(kc + 1) * 128],
                 qsl[qs][:, qb * 512:(qb + 1) * 512],
                 start=True, stop=True,
                 reads=[f"khb{sl}", f"qsl{qs}"], writes=[f"pa{bk}"])
        g0 = 2 * j
        bk0 = g0 % 4
        r.op("scalar", "activation",
             pbuf[par][:, g0 * 512:(g0 + 2) * 512],
             ps_a[:, bk0 * 512:(bk0 + 2) * 512], AF.Exp,
             bias=biasqs[:, qh:qh + 1], scale=SCALE,
             reads=[f"pa{bk0}", f"pa{bk0 + 1}", "biasqs"],
             writes=[f"pb{par}_{j}"])

    pend_ep = []
    step_serial = [0]

    def flush_eps(force=False):
        while pend_ep and (force or pend_ep[0][0] <= step_serial[0]):
            pend_ep.pop(0)[1]()

    def emit_av_step(idx, j):
        h, f, qb = iters[idx]
        sl = h % 2
        qh = h * GF + f
        par = idx % 2
        for m in range(8 * j, 8 * j + 8):
            qt = m // DC
            kc = m % DC
            bb_ = qt % 2
            pres = f"pb{bb_}"
            r.op("tensor", "matmul", ps_b[:, bb_ * 512:bb_ * 512 + 129],
                 pbuf[par][:, kc * 512 + qt * 128:kc * 512 + (qt + 1) * 128],
                 vhb[sl][:, kc, 0:129],
                 start=(kc == 0), stop=(kc == DC - 1),
                 reads=[f"pb{par}_{kc // 2}", f"vhb{sl}", f"vones{sl}"],
                 writes=[pres], accum=(kc > 0))
            if kc == DC - 1:
                qoff = qh * TPC + qb * 512 + qt * 128

                def ep(pres=pres, bb_=bb_, qoff=qoff, first=(qh + qb + qt == 0)):
                    r.op("vector", "reciprocal", recip[:, :],
                         ps_b[:, bb_ * 512 + 128:bb_ * 512 + 129],
                         reads=[pres], writes=["recip"])
                    r.op("vector", "tensor_scalar", astg[:, :],
                         ps_b[:, bb_ * 512:bb_ * 512 + 128], recip[:, 0:1], None,
                         OP.mult, reads=[pres, "recip"], writes=["astg"])
                    r.op("tensor", "transpose", ps_tr[:, 0:128], astg[:, :],
                         identb[:, :], reads=["astg", "identb"], writes=["ptr"])
                    r.op("vector", "tensor_copy", a_fm[:, qoff:qoff + 128],
                         ps_tr[:, 0:128], reads=["ptr"], writes=["a_fm"],
                         accum=(not first))
                pend_ep.append((step_serial[0] + 2, ep))

    def emit_interleaved(s_idx, a_idx):
        for j in range(16):
            step_serial[0] += 1
            flush_eps()
            if s_idx is not None:
                emit_scores_step(s_idx, j)
            if a_idx is not None:
                emit_av_step(a_idx, j)

    load_kv(0, 0)
    load_kv(1, 1)
    emit_f_prep(0, 0)
    emit_interleaved(0, None)
    for idx in range(len(iters)):
        nxt = idx + 1
        if nxt < len(iters):
            hn, fn, qbn = iters[nxt]
            if qbn == 0:
                emit_f_prep(hn, fn)
        emit_interleaved(nxt if nxt < len(iters) else None, idx)
        hh, ff, qq = iters[idx]
        if ff == GF - 1 and qq == 1 and hh + 2 < H:
            load_kv(hh + 2, (hh + 2) % 2)
    flush_eps(force=True)

    if "afm" in tapd:
        r.op("sync", "dma_start", out=tapd["afm"], in_=a_fm[:, :], dma="dtap",
             reads=["a_fm"])

    for _tb in range(TT):
        r.span(f"attnt{_tb}", 65536 + _tb * 8192, 65536 + (_tb + 1) * 8192)
        r.span(f"x2t{_tb}", 0, 65536)
        r.span(f"ffnt{_tb}", 65536 + _tb * 8192, 65536 + (_tb + 1) * 8192)

    for _tb in range(TT):
        r.span(f"asum{_tb}", r.spans["asum"][0] + _tb * 64,
               r.spans["asum"][0] + (_tb + 1) * 64)
        r.span(f"asq{_tb}", r.spans["asq"][0] + _tb * 64,
               r.spans["asq"][0] + (_tb + 1) * 64)
        r.span(f"fsum{_tb}", r.spans["asum"][0] + _tb * 64,
               r.spans["asum"][0] + (_tb + 1) * 64)
        r.span(f"fsq{_tb}", r.spans["asq"][0] + _tb * 64,
               r.spans["asq"][0] + (_tb + 1) * 64)

    def combine_stats(sum_res, sq_res, tb):
        r.op("vector", "tensor_reduce", exv[:, :], asum[:, tb, :], AX.X, OP.add,
             reads=[sum_res], writes=["exv"])
        r.op("vector", "tensor_reduce", ex2[:, :], asq[:, tb, :], AX.X, OP.add,
             reads=[sq_res], writes=["ex2"])
        r.op("vector", "tensor_scalar", bn2[:, 0:1], exv[:, :], 1.0 / D, None,
             OP.mult, reads=["exv"], writes=["bn2"])
        r.op("vector", "tensor_scalar", ex2[:, :], ex2[:, :], 1.0 / D, None,
             OP.mult, reads=["ex2"], writes=["ex2"], accum=True)
        r.op("vector", "tensor_tensor", exv[:, :], bn2[:, 0:1], bn2[:, 0:1],
             OP.mult, reads=["bn2"], writes=["exv"])
        r.op("vector", "tensor_tensor", sdv[:, :], ex2[:, :], exv[:, :],
             OP.subtract, reads=["ex2", "exv"], writes=["sdv"])
        r.op("vector", "tensor_scalar", sdv[:, :], sdv[:, :], EPS, None, OP.add,
             reads=["sdv"], writes=["sdv"], accum=True)
        r.op("scalar", "activation", sdv[:, :], sdv[:, :], AF.Sqrt,
             reads=["sdv"], writes=["sdv"])
        r.op("vector", "reciprocal", rstd[:, :], sdv[:, :], reads=["sdv"],
             writes=["rstd"])

    def emit_p5(tb):
        r.op("sync", "dma_start", out=x5[:, :], in_=xd[tb * 128:(tb + 1) * 128, :],
             dma="dx5", writes=["x5"])
        r.op("vector", "tensor_tensor", x5[:, :], x5[:, :],
             attn_sb[:, tb * D:(tb + 1) * D], OP.add,
             reads=["x5", f"attnt{tb}"], writes=["x5"], accum=True)
        layernorm_stats(x5[:, :], "x5")
        r.op("vector", "tensor_scalar", ln2buf[:, :], x5[:, :], bn2[:, 0:1],
             rstd[:, 0:1], OP.subtract, OP.mult,
             reads=["x5", "bn2", "rstd"], writes=["ln2buf"])
        for dc in range(DC):
            r.op("tensor", "transpose", ps_tr[:, 0:128],
                 ln2buf[:, dc * 128:(dc + 1) * 128], identb[:, :],
                 reads=["ln2buf", "identb"], writes=["ptr"])
            r.op("vector", "tensor_copy",
                 x2_fm[:, dc * TPC + tb * 128:dc * TPC + (tb + 1) * 128],
                 ps_tr[:, 0:128], reads=["ptr"], writes=[f"x2t{tb}"],
                 accum=(dc > 0))

    r.op("gpsimd", "dma_start", out=wobrow[0:1, :],
         in_=wob.rearrange("(a k) -> a k", a=1), dma="dwob", writes=["wobrow"])
    r.retouch("wst0")
    r.retouch("wst1")
    emit_stripe_load(wo, 0)
    for s in range(16):
        if s + 1 < 16:
            emit_stripe_load(wo, s + 1)
        sl = s % 2
        for tb in range(TT):
            if s == 15 and tb >= 2:
                emit_p5(tb - 2)
            b = next_bank()
            pres = f"pa{b}"
            po = ps_a[:, b * 512:b * 512 + 256]
            for dc in range(DC):
                r.op("tensor", "matmul", po,
                     a_fm[:, dc * TPC + tb * 128:dc * TPC + (tb + 1) * 128],
                     wst[sl][:, dc, :], start=(dc == 0), stop=False,
                     reads=[f"wst{sl}", "a_fm"], writes=[pres], accum=(dc > 0))
            r.op("tensor", "matmul", po, onesrow[0:1, :],
                 wobrow[0:1, s * 256:(s + 1) * 256], start=False, stop=True,
                 reads=["onesrow", "wobrow"], writes=[pres], accum=True)
            r.op("scalar", "activation",
                 attn_sb[:, tb * D + s * 256:tb * D + (s + 1) * 256], po, AF.Copy,
                 accum_out=asum[:, tb, s:s + 1],
                 reads=[pres], writes=[f"attnt{tb}", f"asum{tb}"], accum=(s > 0))
            r.op("scalar", "activation", sqscrap[:, :], po, AF.Square,
                 accum_out=asq[:, tb, s:s + 1],
                 reads=[pres], writes=["sqscrap", f"asq{tb}"])
            if s == 15:
                row = attn_sb[:, tb * D:(tb + 1) * D]
                combine_stats(f"asum{tb}", f"asq{tb}", tb)
                r.op("vector", "tensor_scalar", row, row, bn2[:, 0:1], rstd[:, 0:1],
                     OP.subtract, OP.mult, reads=[f"attnt{tb}", "bn2", "rstd"],
                     writes=[f"attnt{tb}"], accum=True)
                r.op("sync", "dma_start", out=lnasc[tb * 128:(tb + 1) * 128, :],
                     in_=row, dma=f"dlna{tb}", reads=[f"attnt{tb}"],
                     writes=["lnasc"], accum=(tb > 0))
    emit_p5(6)
    emit_p5(7)
    if "attn" in tapd:
        r.op("sync", "dma_start", out=tapd["attn"], in_=attn_sb[:, :], dma="dtap",
             reads=[f"attnt{t}" for t in range(TT)])
    if "x2fm" in tapd:
        r.op("sync", "dma_start", out=tapd["x2fm"], in_=x2_fm[:, :], dma="dtap",
             reads=[f"x2t{t}" for t in range(TT)])

    r.op("gpsimd", "dma_start", out=atsb[:, :, :],
         in_=at_.rearrange("(dc p) e -> p dc e", p=128), dma="dat", writes=["atsb"])
    r.op("gpsimd", "dma_start", out=ballt[:, :], in_=br_, dma="dbr", writes=["ballt"])
    r.op("gpsimd", "dma_start", out=bbrow[0:1, :],
         in_=bb.rearrange("(a k) -> a k", a=1), dma="dbb", writes=["bbrow"])
    for tb in range(TT):
        for dc in range(DC):
            r.op("tensor", "matmul", ps_b[:, 0:128],
                 x2_fm[:, dc * TPC + tb * 128:dc * TPC + (tb + 1) * 128],
                 atsb[:, dc, :], start=(dc == 0), stop=False,
                 reads=[f"x2t{tb}", "atsb"], writes=["pb0"], accum=(dc > 0))
            r.op("tensor", "matmul", ps_b[:, 128:136],
                 x2_fm[:, dc * TPC + tb * 128:dc * TPC + (tb + 1) * 128],
                 wrt[:, dc * 8:(dc + 1) * 8], start=False, stop=(dc == DC - 1),
                 reads=[f"x2t{tb}", "wrt"], writes=["pb0"], accum=True)
        r.op("scalar", "activation", hsb[:, :], ps_b[:, 0:136], AF.Copy,
             reads=["pb0"], writes=["hsb"])
        lg = hsb[:, 128:136]
        r.op("vector", "tensor_reduce", tm1[:, :], lg, AX.X, OP.max,
             reads=["hsb"], writes=["tm1"])
        r.op("vector", "tensor_scalar", teq1[:, :], lg, tm1[:, 0:1], None,
             OP.is_equal, reads=["hsb", "tm1"], writes=["teq1"])
        r.op("vector", "scalar_tensor_tensor", tlm[:, :], teq1[:, :], -1e30, lg,
             OP.mult, OP.add, reads=["teq1", "hsb"], writes=["tlm"])
        r.op("vector", "tensor_reduce", tm2[:, :], tlm[:, :], AX.X, OP.max,
             reads=["tlm"], writes=["tm2"])
        r.op("vector", "tensor_scalar", teq2[:, :], tlm[:, :], tm2[:, 0:1], None,
             OP.is_equal, reads=["tlm", "tm2"], writes=["teq2"])
        r.op("vector", "tensor_scalar", tm2[:, :], tm2[:, :], tm1[:, 0:1], None,
             OP.subtract, reads=["tm2", "tm1"], writes=["tm2"])
        r.op("scalar", "activation", trw2[:, :], tm2[:, :], AF.Sigmoid,
             reads=["tm2"], writes=["trw2"])
        r.op("vector", "tensor_scalar", trw1[:, :], trw2[:, :], -1.0, 1.0,
             OP.mult, OP.add, reads=["trw2"], writes=["trw1"])
        r.op("vector", "tensor_scalar", tt1[:, :], teq1[:, :], trw1[:, 0:1], None,
             OP.mult, reads=["teq1", "trw1"], writes=["tt1"])
        r.op("vector", "scalar_tensor_tensor", twts[:, :], teq2[:, :], trw2[:, 0:1],
             tt1[:, :], OP.mult, OP.add, reads=["teq2", "trw2", "tt1"],
             writes=["twts"])
        for e in range(NE):
            r.op("vector", "tensor_scalar", gtm[:, e * RK:(e + 1) * RK],
                 hsb[:, e * RK:(e + 1) * RK], twts[:, e:e + 1], LSCALE,
                 OP.mult, OP.mult, reads=["hsb", "twts"], writes=["gtm"],
                 accum=(e > 0))
        r.op("tensor", "transpose", ps_tr[:, 0:128], gtm[:, :], identb[:, :],
             reads=["gtm", "identb"], writes=["ptr"])
        r.op("vector", "tensor_copy", gfm[:, tb * 128:(tb + 1) * 128],
             ps_tr[:, 0:128], reads=["ptr"], writes=["gfm"], accum=(tb > 0))

    def emit_p7(tb):
        row = ffn_sb[:, tb * D:(tb + 1) * D]
        combine_stats(f"fsum{tb}", f"fsq{tb}", tb)
        r.op("vector", "tensor_scalar", row, row, bn2[:, 0:1], rstd[:, 0:1],
             OP.subtract, OP.mult, reads=[f"ffnt{tb}", "bn2", "rstd"],
             writes=[f"ffnt{tb}"], accum=True)
        for hf in range(2):
            par = (tb * 2 + hf) % 2
            cs = hf * 2048
            r.op("scalar", "dma_start", out=xh[par][:, :],
                 in_=xd[tb * 128:(tb + 1) * 128, cs:cs + 2048],
                 dma=f"dx7{par}", writes=[f"xh{par}"])
            r.op("gpsimd", "dma_start", out=lnh[par][:, :],
                 in_=lnasc[tb * 128:(tb + 1) * 128, cs:cs + 2048], dma=f"dl7{par}",
                 reads=["lnasc"], writes=[f"lnh{par}"])
            r.op("vector", "tensor_tensor", xh[par][:, :], xh[par][:, :],
                 lnh[par][:, :], OP.add,
                 reads=[f"xh{par}", f"lnh{par}"], writes=[f"xh{par}"], accum=True)
            r.op("vector", "tensor_tensor", xh[par][:, :], xh[par][:, :],
                 row[:, cs:cs + 2048], OP.add,
                 reads=[f"xh{par}", f"ffnt{tb}"], writes=[f"xh{par}"], accum=True)
            r.op("sync", "dma_start",
                 out=outd[tb * 128:(tb + 1) * 128, cs:cs + 2048],
                 in_=xh[par][:, :], dma=f"dout{par}", reads=[f"xh{par}"],
                 writes=["outd"], accum=(tb + hf > 0))

    r.retouch("wst0")
    r.retouch("wst1")
    emit_stripe_load(wbt, 0)
    for s in range(16):
        if s + 1 < 16:
            emit_stripe_load(wbt, s + 1)
        sl = s % 2
        for tb in range(TT):
            b = next_bank()
            pres = f"pa{b}"
            po = ps_a[:, b * 512:b * 512 + 256]
            for dc in range(DC):
                r.op("tensor", "matmul", po,
                     x2_fm[:, dc * TPC + tb * 128:dc * TPC + (tb + 1) * 128],
                     wst[sl][:, dc, :], start=(dc == 0), stop=False,
                     reads=[f"wst{sl}", f"x2t{tb}"], writes=[pres], accum=(dc > 0))
            r.op("tensor", "matmul", po, gfm[:, tb * 128:(tb + 1) * 128],
                 ballt[:, s * 256:(s + 1) * 256], start=False, stop=False,
                 reads=["gfm", "ballt"], writes=[pres], accum=True)
            r.op("tensor", "matmul", po, onesrow[0:1, :],
                 bbrow[0:1, s * 256:(s + 1) * 256], start=False, stop=True,
                 reads=["onesrow", "bbrow"], writes=[pres], accum=True)
            r.op("scalar", "activation",
                 ffn_sb[:, tb * D + s * 256:tb * D + (s + 1) * 256], po, AF.Copy,
                 accum_out=asum[:, tb, s:s + 1],
                 reads=[pres], writes=[f"ffnt{tb}", f"fsum{tb}"], accum=(s > 0))
            r.op("scalar", "activation", sqscrap[:, :], po, AF.Square,
                 accum_out=asq[:, tb, s:s + 1],
                 reads=[pres], writes=["sqscrap", f"fsq{tb}"])
    for tb in range(TT):
        emit_p7(tb)
    if "ffn" in tapd:
        r.op("sync", "dma_start", out=tapd["ffn"], in_=ffn_sb[:, :], dma="dtap",
             reads=[f"ffnt{t}" for t in range(TT)])

    r.final_wait("dout0")
    r.final_wait("dout1")
    if tapd:
        r.final_wait("dtap")

    r.materialize()

    with contextlib.ExitStack() as st2:
        esems = {e: st2.enter_context(nc.semaphore(f"se_{e}")) for e in ENG}
        dsems = {n: st2.enter_context(nc.semaphore(f"sd_{n}"))
                 for n in sorted(r.dma_sems)}
        block = st2.enter_context(nc.Block())

        @block.tensor
        def _(t):
            r.replay(t, "tensor", esems, dsems)

        @block.vector
        def _(v):
            r.replay(v, "vector", esems, dsems)

        @block.scalar
        def _(sc):
            r.replay(sc, "scalar", esems, dsems)

        @block.sync
        def _(sy):
            r.replay(sy, "sync", esems, dsems)

        @block.gpsimd
        def _(gp):
            r.replay(gp, "gpsimd", esems, dsems)

    stack.close()
    return nc


def kernel(L0_attn_x, L0_attn_wq_w, L0_attn_wq_b, L0_attn_wk_w, L0_attn_wk_b,
           L0_attn_wv_w, L0_attn_wv_b, L0_attn_wo_w, L0_attn_wo_b,
           L0_ffn_W_base, L0_ffn_b_base, L0_ffn_W_router, L0_ffn_A, L0_ffn_B):
    xf = np.ascontiguousarray(
        np.asarray(L0_attn_x, dtype=np.float32).reshape(NCORES * TPC, D))
    base = {
        "wq": np.ascontiguousarray(L0_attn_wq_w, dtype=np.float32),
        "wk": np.ascontiguousarray(L0_attn_wk_w, dtype=np.float32),
        "wv": np.ascontiguousarray(L0_attn_wv_w, dtype=np.float32),
        "wo": np.ascontiguousarray(L0_attn_wo_w, dtype=np.float32),
        "wqb": np.ascontiguousarray(L0_attn_wq_b, dtype=np.float32),
        "wkb": np.ascontiguousarray(L0_attn_wk_b, dtype=np.float32),
        "wvb": np.ascontiguousarray(L0_attn_wv_b, dtype=np.float32),
        "wob": np.ascontiguousarray(L0_attn_wo_b, dtype=np.float32),
        "wbt": np.ascontiguousarray(np.asarray(L0_ffn_W_base, dtype=np.float32).T),
        "bb": np.ascontiguousarray(L0_ffn_b_base, dtype=np.float32),
        "wr": np.ascontiguousarray(L0_ffn_W_router, dtype=np.float32),
        "at": np.ascontiguousarray(
            np.asarray(L0_ffn_A, dtype=np.float32).reshape(NE * RK, D).T),
        "br": np.ascontiguousarray(
            np.asarray(L0_ffn_B, dtype=np.float32).transpose(0, 2, 1).reshape(NE * RK, D)),
    }
    in_maps = []
    for c in range(NCORES):
        m = dict(base)
        m["x"] = np.ascontiguousarray(xf[c * TPC:(c + 1) * TPC])
        in_maps.append(m)
    nc = build_nc(taps=_DEBUG_TAPS)
    res = run_bass_kernel_spmd(nc, in_maps, core_ids=list(range(NCORES)))
    out = np.concatenate([res.results[c]["out"] for c in range(NCORES)], axis=0)
    return np.ascontiguousarray(out.reshape(2, SL, D), dtype=np.float32)


# version 10 (final)
# speedup vs baseline: 1.2360x; 1.0362x over previous
import math
import contextlib
import numpy as np
import concourse.bass as bass
import concourse.mybir as mybir
from concourse.alu_op_type import AluOpType as OP
from concourse.bass_utils import run_bass_kernel_spmd

F32 = mybir.dt.float32
BF16 = mybir.dt.bfloat16
AF = mybir.ActivationFunctionType
AX = mybir.AxisListType

NCORES = 8
D = 4096
DC = 32
TPC = 1024
TT = 8
KV = 1024
H = 8
GF = 4
HD = 128
SL = 4096
NR = 4
SCALE = 1.0 / math.sqrt(HD)
EPS = 1e-5
NE = 8
RK = 16
LSCALE = 2.0

ENG = ("tensor", "vector", "scalar", "sync", "gpsimd")

_DEBUG_TAPS = []


class Rec:

    def __init__(self):
        self.ops = []
        self.wl = {}
        self.r = {}
        self.spans = {}
        self.touched = set()

    def span(self, res, lo, hi):
        self.spans[res] = (lo, hi)

    def retouch(self, res):
        self.touched.discard(res)

    def op(self, eng, meth, *args, reads=(), writes=(), accum=False, dma=None,
           cc=False, **kwargs):
        i = len(self.ops)
        deps = set()
        for res in writes:
            if res not in self.touched and res in self.spans:
                lo, hi = self.spans[res]
                for other, (lo2, hi2) in self.spans.items():
                    if other == res or hi2 <= lo or hi <= lo2:
                        continue
                    deps.update(self.wl.get(other, ()))
                    deps.update(self.r.get(other, ()))
                self.touched.add(res)
        for res in reads:
            deps.update(self.wl.get(res, ()))
        if not accum:
            for res in writes:
                deps.update(self.wl.get(res, ()))
                deps.update(self.r.get(res, ()))
        for res in writes:
            if accum:
                self.wl.setdefault(res, []).append(i)
            else:
                self.wl[res] = [i]
                self.r[res] = []
        for res in reads:
            self.r.setdefault(res, []).append(i)
        self.ops.append(dict(i=i, eng=eng, meth=meth, args=args, kwargs=kwargs,
                             deps=deps, dma=dma, cc=cc, need=False, sem=None,
                             val=None))
        return i

    def final_wait(self, dma_name):
        self.ops.append(dict(i=len(self.ops), eng="sync", meth="_final_wait",
                             args=(dma_name,), kwargs={}, deps=set(), dma=None,
                             cc=False, need=False, sem=None, val=None))

    def materialize(self):
        for o in self.ops:
            best = {}
            for d in o["deps"]:
                p = self.ops[d]
                key = ("dma", p["dma"]) if p["dma"] is not None else ("eng", p["eng"])
                if key not in best or d > best[key]:
                    best[key] = d
            o["deps"] = set(best.values())
        for o in self.ops:
            for d in o["deps"]:
                self.ops[d]["need"] = True
        engct = {e: 0 for e in ENG}
        dmact = {}
        self.dma_sems = set()
        for o in self.ops:
            if o["dma"] is not None:
                step = 1 if o["cc"] else 16
                dmact[o["dma"]] = dmact.get(o["dma"], 0) + step
                o["sem"], o["val"] = ("dma", o["dma"]), dmact[o["dma"]]
                self.dma_sems.add(o["dma"])
            elif o["need"]:
                engct[o["eng"]] += 1
                o["sem"], o["val"] = ("eng", o["eng"]), engct[o["eng"]]
        hw = {e: {} for e in ENG}
        for o in self.ops:
            if o["meth"] == "_final_wait":
                o["waits"] = [(("dma", o["args"][0]), dmact[o["args"][0]])]
                continue
            waits = []
            for d in sorted(o["deps"]):
                p = self.ops[d]
                if p["sem"] is None:
                    continue
                key = p["sem"]
                cur = hw[o["eng"]].get(key, 0)
                if p["val"] > cur:
                    waits.append((key, p["val"]))
                    hw[o["eng"]][key] = p["val"]
            o["waits"] = waits

    def replay(self, engobj, eng, esems, dsems):
        for o in self.ops:
            if o["eng"] != eng:
                continue
            for (kind, name), v in o.get("waits", ()):
                engobj.wait_ge(dsems[name] if kind == "dma" else esems[name], v)
            if o["meth"] == "_final_wait":
                continue
            inst = getattr(engobj, o["meth"])(*o["args"], **o["kwargs"])
            if o["sem"] is not None:
                if o["sem"][0] == "dma":
                    inst.then_inc(dsems[o["sem"][1]], 1 if o["cc"] else 16)
                else:
                    inst.then_inc(esems[eng], 1)


def build_nc(taps=()):
    nc = bass.Bass("TRN2", target_bir_lowering=False, debug=False)

    xd = nc.dram_tensor("x", [TPC, D], F32, kind="ExternalInput").ap()
    wq = nc.dram_tensor("wq", [D, D], F32, kind="ExternalInput").ap()
    wk = nc.dram_tensor("wk", [D, KV], F32, kind="ExternalInput").ap()
    wv = nc.dram_tensor("wv", [D, KV], F32, kind="ExternalInput").ap()
    wo = nc.dram_tensor("wo", [D, D], F32, kind="ExternalInput").ap()
    wqb = nc.dram_tensor("wqb", [D], F32, kind="ExternalInput").ap()
    wkb = nc.dram_tensor("wkb", [KV], F32, kind="ExternalInput").ap()
    wvb = nc.dram_tensor("wvb", [KV], F32, kind="ExternalInput").ap()
    wob = nc.dram_tensor("wob", [D], F32, kind="ExternalInput").ap()
    wbt = nc.dram_tensor("wbt", [D, D], F32, kind="ExternalInput").ap()
    bb = nc.dram_tensor("bb", [D], F32, kind="ExternalInput").ap()
    wr = nc.dram_tensor("wr", [NE, D], F32, kind="ExternalInput").ap()
    at_ = nc.dram_tensor("at", [D, NE * RK], F32, kind="ExternalInput").ap()
    br_ = nc.dram_tensor("br", [NE * RK, D], F32, kind="ExternalInput").ap()
    outd = nc.dram_tensor("out", [TPC, D], F32, kind="ExternalOutput").ap()

    qsc = nc.dram_tensor("qscratch", [D, TPC], BF16).ap()
    gin = nc.dram_tensor("gin", [2064, KV], BF16).ap()
    gout = nc.dram_tensor("gout", [NR * 2064, KV], BF16).ap()
    x5sc = nc.dram_tensor("x5scratch", [TPC, D], BF16).ap()

    tapd = {}
    tap_shapes = {"xfm": [128, DC * TPC], "afm": [128, DC * TPC],
                  "x2fm": [128, DC * TPC], "attn": [128, TT * D],
                  "ffn": [128, TT * D], "pbuf": [128, 16384],
                  "qkv": [2048, KV]}
    for t in taps:
        tapd[t] = nc.dram_tensor("tap_" + t, tap_shapes[t], BF16,
                                 kind="ExternalOutput").ap()

    r = Rec()

    SB0 = 16512

    def sb(name, shape, dtype, off):
        t = nc.alloc_sbuf_tensor_at(name, shape, dtype, offset=SB0 + off)
        nbytes = 1
        for s in shape[1:]:
            nbytes *= s
        nbytes *= 2 if dtype == BF16 else 4
        r.span(name, off, off + nbytes)
        return t

    x_fm = sb("x_fm", [128, DC * TPC], BF16, 0)
    for _tb in range(TT):
        r.span(f"xfmt{_tb}", 0, 65536)
    a_fm = sb("a_fm", [128, DC * TPC], BF16, 0)
    x2_fm = sb("x2_fm", [128, DC * TPC], BF16, 0)
    ln1bufs = [sb(f"ln1buf{i}", [128, D], BF16, 65536 + i * 8192) for i in range(2)]
    wrtmp = sb("wrtmp", [NE, D], BF16, 65536 + 8192)
    pbuf = [sb(f"pbuf{i}", [128, 16384], BF16, 65536 + i * 32768) for i in range(2)]
    for _p in range(2):
        for _g in range(16):
            r.span(f"pb{_p}_{_g}", 65536 + _p * 32768 + _g * 2048,
                   65536 + _p * 32768 + (_g + 1) * 2048)
    qsl = [sb(f"qsl{i}", [128, TPC], BF16, 164864 + i * 2048) for i in range(2)]
    braw = sb("braw", [32, 128], F32, 81920)
    ksq5 = sb("ksq5", [128, 512], BF16, 82432)
    nrow = sb("nrow", [1, 512], F32, 83456)
    mklr = sb("mklr", [1, NE], F32, 85504)
    mqlr = sb("mqlr", [1, 32], F32, 85536)
    mk2f = sb("mk2f", [1, NE], F32, 85664)
    nrows4 = sb("nrows4", [1, 32], BF16, 85696)
    attn_sb = sb("attn_sb", [128, TT * D], BF16, 65536)
    ffn_sb = sb("ffn_sb", [128, TT * D], BF16, 65536)
    xsl = [sb(f"xsl{i}", [128, D], F32, 131072 + i * 16384) for i in range(2)]
    qtg = [sb(f"qtg{i}", [128, 512], BF16, 131072 + i * 1024) for i in range(2)]
    ktg = [sb(f"ktg{i}", [128, 512], BF16, 133120 + i * 1024) for i in range(2)]
    vtg = [sb(f"vtg{i}", [128, 256], BF16, 135168 + i * 512) for i in range(2)]
    khb = [sb(f"khb{i}", [128, NR, 1024], BF16, 131072 + i * 8192) for i in range(2)]
    vhb = [sb(f"vhb{i}", [128, DC, 130], BF16, 147456 + i * 8320) for i in range(2)]
    wobrow = sb("wobrow", [1, D], BF16, 131072)
    atsb = sb("atsb", [128, DC, NE * RK], BF16, 131072)
    x5sl = [sb(f"x5sl{i}", [128, D], BF16, 139264 + i * 8192) for i in range(2)]
    ln2buf = sb("ln2buf", [128, D], BF16, 155648)
    xq = [sb(f"xq{i}", [128, 1024], F32, 155648 + i * 4096) for i in range(2)]
    wst = [sb(f"wst{i}", [128, DC, 256], BF16, 164864 + i * 16384) for i in range(2)]
    wst4 = wst + [sb(f"wst{2 + i}", [128, DC, 256], BF16, 98304 + i * 16384) for i in range(2)]
    M = 197632

    def msb(name, shape, dtype):
        nonlocal M
        t = sb(name, shape, dtype, M)
        nbytes = 1
        for s in shape[1:]:
            nbytes *= s
        nbytes *= 2 if dtype == BF16 else 4
        M += (nbytes + 31) // 32 * 32
        return t

    identf = msb("identf", [128, 128], F32)
    identb = msb("identb", [128, 128], BF16)
    iot0 = msb("iot0", [128, 128], F32)
    iot1 = msb("iot1", [128, 128], F32)
    wqb_sb = msb("wqb_sb", [128, 32], F32)
    wkb_sb = msb("wkb_sb", [128, 8], F32)
    onescol = msb("onescol", [128, 1], BF16)
    onesrow = msb("onesrow", [1, 128], BF16)
    wvbrow = msb("wvbrow", [1, KV], BF16)
    wrt = msb("wrt", [128, DC * NE], BF16)
    nsc = msb("nsc", [1, 1], F32)
    bnb = [msb(f"bnb{p}", [128, 8, 6], F32) for p in range(2)]
    bn2 = [msb(f"bn2{p}", [128, 2], F32) for p in range(2)]
    sdv = [msb(f"sdv{p}", [128, 1], F32) for p in range(2)]
    rstd = [msb(f"rstd{p}", [128, 1], F32) for p in range(2)]
    recip = [msb(f"recip{p}", [128, 1], F32) for p in range(2)]
    astg = [msb(f"astg{p}", [128, 128], BF16) for p in range(2)]
    ball_sl = [msb(f"ball{p}", [128, 256], BF16) for p in range(2)]
    bb_sl = [msb(f"bbs{p}", [1, 256], BF16) for p in range(2)]
    hsb = msb("hsb", [128, 136], F32)
    gtm = msb("gtm", [128, 128], BF16)
    gfm = msb("gfm", [128, TT * 128], BF16)
    tm1 = msb("tm1", [128, 1], F32)
    tm2 = msb("tm2", [128, 1], F32)
    teq1 = msb("teq1", [128, NE], F32)
    teq2 = msb("teq2", [128, NE], F32)
    tlm = msb("tlm", [128, NE], F32)
    trw2 = msb("trw2", [128, 1], F32)
    trw1 = msb("trw1", [128, 1], F32)
    tt1 = msb("tt1", [128, NE], F32)
    twts = msb("twts", [128, NE], F32)
    exv = [msb(f"exv{p}", [128, 1], F32) for p in range(2)]
    ex2 = [msb(f"ex2{p}", [128, 1], F32) for p in range(2)]
    bqrow = msb("bqrow", [1, 32], F32)
    bqrowb = msb("bqrowb", [1, 32], BF16)
    biasqs = msb("biasqs", [128, 32], F32)
    asum = msb("asum", [128, TT, 16], F32)
    asq = msb("asq", [128, TT, 16], F32)
    sqscrap = msb("sqscrap", [128, 256], BF16)
    assert SB0 + M <= 229344, M

    stack = contextlib.ExitStack()
    ps_a = stack.enter_context(nc.psum_tensor("ps_a", [128, 2048], F32))
    ps_b = stack.enter_context(nc.psum_tensor("ps_b", [128, 1024], F32))
    ps_tr = stack.enter_context(nc.psum_tensor("ps_tr", [128, 1024], BF16))
    ps_d = stack.enter_context(nc.psum_tensor("ps_d", [128, 512], F32))

    r.op("gpsimd", "iota", iot0[:, :], [[1, 128]], channel_multiplier=0,
         allow_small_or_imprecise_dtypes=True, writes=["iot0"])
    r.op("gpsimd", "iota", iot1[:, :], [[0, 128]], channel_multiplier=1,
         allow_small_or_imprecise_dtypes=True, writes=["iot1"])
    r.op("vector", "tensor_tensor", identf[:, :], iot0[:, :], iot1[:, :], OP.is_equal,
         reads=["iot0", "iot1"], writes=["identf"])
    r.op("vector", "tensor_copy", identb[:, :], identf[:, :],
         reads=["identf"], writes=["identb"])
    r.op("gpsimd", "memset", onescol[:, :], 1.0, writes=["onescol"])
    r.op("gpsimd", "memset", onesrow[:, :], 1.0, writes=["onesrow"])
    for i in range(2):
        r.span(f"vones{i}", *r.spans[f"vhb{i}"])
    r.op("sync", "dma_start", out=braw[:, :], in_=wqb.rearrange("(a p) -> a p", p=128),
         dma="dbias", writes=["braw"])
    r.op("tensor", "transpose", ps_d[:, 0:32], braw[:, :], identf[0:32, 0:32],
         reads=["braw", "identf"], writes=["pd"])
    r.op("vector", "tensor_copy", wqb_sb[:, :], ps_d[:, 0:32], reads=["pd"],
         writes=["wqb_sb"])
    r.op("sync", "dma_start", out=braw[0:8, :], in_=wkb.rearrange("(a p) -> a p", p=128),
         dma="dbias", reads=["pd"], writes=["braw"])
    r.op("tensor", "transpose", ps_d[:, 32:40], braw[0:8, :], identf[0:8, 0:8],
         reads=["braw", "identf"], writes=["pd"])
    r.op("vector", "tensor_copy", wkb_sb[:, :], ps_d[:, 32:40], reads=["pd"],
         writes=["wkb_sb"])
    r.op("gpsimd", "dma_start", out=wvbrow[0:1, :],
         in_=wvb.rearrange("(a k) -> a k", a=1), dma="dwvb", writes=["wvbrow"])
    r.op("gpsimd", "dma_start", out=wrtmp[:, :], in_=wr, dma="dwrt", writes=["wrtmp"])
    for dc in range(DC):
        r.op("tensor", "transpose", ps_tr[:, 0:8], wrtmp[:, dc * 128:(dc + 1) * 128],
             identb[0:8, 0:8], reads=["wrtmp", "identb"], writes=["ptr0"])
        r.op("vector", "tensor_copy", wrt[:, dc * 8:(dc + 1) * 8], ps_tr[:, 0:8],
             reads=["ptr0"], writes=["wrt"], accum=(dc > 0))

    def layernorm_stats(src_ap, src_res, p):
        for i in range(8):
            r.op("vector", "bn_stats", bnb[p][:, i, :], src_ap[:, i * 512:(i + 1) * 512],
                 reads=[src_res], writes=[f"bnb{p}"], accum=(i > 0))
        r.op("vector", "bn_aggr", bn2[p][:, :], bnb[p][:, :, :], reads=[f"bnb{p}"],
             writes=[f"bn2{p}"])
        r.op("vector", "tensor_scalar", sdv[p][:, :], bn2[p][:, 1:2], EPS, None, OP.add,
             reads=[f"bn2{p}"], writes=[f"sdv{p}"])
        r.op("scalar", "activation", sdv[p][:, :], sdv[p][:, :], AF.Sqrt,
             reads=[f"sdv{p}"], writes=[f"sdv{p}"])
        r.op("vector", "reciprocal", rstd[p][:, :], sdv[p][:, :], reads=[f"sdv{p}"],
             writes=[f"rstd{p}"])

    def emit_stripe_load(wsrc, s, sl=None):
        if sl is None:
            sl = s % 2
        src = wsrc.rearrange("(dc p) n -> p dc n", p=128)[:, :, s * 256:(s + 1) * 256]
        r.op("gpsimd", "dma_start", out=wst[sl][:, :, :], in_=src,
             dma=f"dw{sl}", writes=[f"wst{sl}"])
        return sl

    pctr = [0]

    def next_bank():
        b = pctr[0] % 4
        pctr[0] += 1
        return b

    def emit_fm_transposes(src_ap, src_res, dst_fm, dst_tag, tb):
        for dc in range(DC):
            r.op("tensor", "transpose", ps_tr[:, 0:128],
                 src_ap[:, dc * 128:(dc + 1) * 128], identb[:, :],
                 reads=[src_res, "identb"], writes=["ptr0"])
            r.op("vector", "tensor_copy",
                 dst_fm[:, dc * TPC + tb * 128:dc * TPC + (tb + 1) * 128],
                 ps_tr[:, 0:128], reads=["ptr0"],
                 writes=[dst_tag], accum=(dc > 0))

    for tb in range(TT):
        sl = tb % 2
        r.op("sync", "dma_start", out=xsl[sl][:, :], in_=xd[tb * 128:(tb + 1) * 128, :],
             dma=f"dx{sl}", writes=[f"xsl{sl}"])
        layernorm_stats(xsl[sl][:, :], f"xsl{sl}", sl)
        r.op("vector", "tensor_scalar", ln1bufs[sl][:, :], xsl[sl][:, :],
             bn2[sl][:, 0:1], rstd[sl][:, 0:1], OP.subtract, OP.mult,
             reads=[f"xsl{sl}", f"bn2{sl}", f"rstd{sl}"], writes=[f"ln1buf{sl}"])
        emit_fm_transposes(ln1bufs[sl][:, :], f"ln1buf{sl}", x_fm, f"xfmt{tb}", tb)
    if "xfm" in tapd:
        r.op("sync", "dma_start", out=tapd["xfm"], in_=x_fm[:, :], dma="dtap",
             reads=[f"xfmt{t}" for t in range(TT)])

    stripe_plan = [(wk, s) for s in range(4)] + [(wv, s) for s in range(4)] + \
                  [(wq, s) for s in range(16)]

    def p1_load(gidx):
        if gidx >= len(stripe_plan):
            return
        wsrc, s = stripe_plan[gidx]
        slot = gidx % 4
        src_ap = wsrc.rearrange("(dc p) n -> p dc n", p=128)[:, :, s * 256:(s + 1) * 256]
        r.op("gpsimd", "dma_start", out=wst4[slot][:, :, :], in_=src_ap,
             dma=f"dw{slot}", writes=[f"wst{slot}"])

    for g in range(3):
        p1_load(g)

    for s in range(4):
        gidx = s
        p1_load(gidx + 3)
        sl = gidx % 4
        for sub in range(2):
            dkc = s * 2 + sub
            for tb in range(2):
                b = next_bank()
                pres = f"pa{b}"
                po = ps_a[:, b * 512:(b + 1) * 512]
                for dc in range(DC):
                    r.op("tensor", "matmul", po,
                         wst4[sl][:, dc, sub * 128:(sub + 1) * 128],
                         x_fm[:, dc * TPC + tb * 512:dc * TPC + (tb + 1) * 512],
                         start=(dc == 0), stop=(dc == DC - 1),
                         reads=[f"wst{sl}", f"xfmt{tb * 4}", f"xfmt{tb * 4 + 1}",
                                f"xfmt{tb * 4 + 2}", f"xfmt{tb * 4 + 3}"],
                         writes=[pres], accum=(dc > 0))
                st = (dkc * 2 + tb) % 2
                r.op("vector", "tensor_scalar", ktg[st][:, :], po,
                     wkb_sb[:, dkc:dkc + 1], None, OP.add,
                     reads=[pres, "wkb_sb"], writes=[f"ktg{st}"])
                r.op("sync", "dma_start",
                     out=gin[dkc * 128:(dkc + 1) * 128, tb * 512:(tb + 1) * 512],
                     in_=ktg[st][:, :], dma=f"dkst{st}", reads=[f"ktg{st}"],
                     writes=["gin_k"], accum=(dkc + tb > 0))
                r.op("scalar", "activation", ksq5[:, :], ktg[st][:, :], AF.Square,
                     reads=[f"ktg{st}"], writes=["ksq5"])
                r.op("tensor", "matmul", ps_d[0:1, 0:512], onescol[:, 0:1],
                     ksq5[:, :], start=True, stop=True,
                     reads=["ksq5", "onescol"], writes=["pd"])
                r.op("vector", "tensor_copy", nrow[0:1, :], ps_d[0:1, 0:512],
                     reads=["pd"], writes=["nrow"])
                r.op("vector", "tensor_reduce", nsc[0:1, 0:1], nrow[0:1, :], AX.X,
                     OP.max, reads=["nrow"], writes=["nsc"])
                if tb == 0:
                    r.op("vector", "tensor_copy", mklr[0:1, dkc:dkc + 1],
                         nsc[0:1, 0:1], reads=["nsc"], writes=["mklr"],
                         accum=(dkc > 0))
                else:
                    r.op("vector", "tensor_tensor", mklr[0:1, dkc:dkc + 1],
                         mklr[0:1, dkc:dkc + 1], nsc[0:1, 0:1], OP.max,
                         reads=["nsc", "mklr"], writes=["mklr"], accum=True)

    vgd = gin[1024:2048, :].rearrange("(b s16) t -> b s16 t", s16=16)
    vgd = vgd.rearrange("(h tc) s16 t -> h tc s16 t", tc=8)
    for s in range(4):
        gidx = 4 + s
        p1_load(gidx + 3)
        sl = gidx % 4
        for tb in range(TT):
            b = next_bank()
            pres = f"pa{b}"
            po = ps_a[:, b * 512:b * 512 + 256]
            for dc in range(DC):
                r.op("tensor", "matmul", po,
                     x_fm[:, dc * TPC + tb * 128:dc * TPC + (tb + 1) * 128],
                     wst4[sl][:, dc, :], start=(dc == 0), stop=False,
                     reads=[f"wst{sl}", f"xfmt{tb}"], writes=[pres], accum=(dc > 0))
            r.op("tensor", "matmul", po, onesrow[0:1, :],
                 wvbrow[0:1, s * 256:(s + 1) * 256], start=False, stop=True,
                 reads=["onesrow", "wvbrow"], writes=[pres], accum=True)
            st = tb % 2
            r.op("scalar", "activation", vtg[st][:, :], po, AF.Copy,
                 reads=[pres], writes=[f"vtg{st}"])
            dst = vgd[s * 2:(s + 1) * 2, tb, :, :].rearrange(
                "h s16 (t8 dv) -> (s16 t8) h dv", dv=128)
            r.op("sync", "dma_start", out=dst,
                 in_=vtg[st][:, :].rearrange("p (h dv) -> p h dv", dv=128),
                 dma=f"dvst{st}", reads=[f"vtg{st}"], writes=["gin_v"],
                 accum=(s + tb > 0))

    if "qkv" in tapd:
        r.op("sync", "dma_start", out=tapd["qkv"], in_=gin[0:2048, :], dma="dtap",
             reads=["gin_k", "gin_v"])

    r.op("gpsimd", "memset", ksq5[0:16, :], 0.0, writes=["ksq5"])
    r.op("gpsimd", "dma_start", out=gin[2048:2064, 0:512], in_=ksq5[0:16, :],
         dma="dnrm", reads=["ksq5"], writes=["gin_n"])
    r.op("gpsimd", "dma_start", out=gin[2048:2064, 512:1024], in_=ksq5[0:16, :],
         dma="dnrm", reads=["ksq5"], writes=["gin_n"], accum=True)
    r.op("gpsimd", "dma_start", out=gin[2048:2049, 0:NE], in_=mklr[0:1, :],
         dma="dnrm", reads=["mklr"], writes=["gin_n"])

    r.op("gpsimd", "collective_compute", "AllGather", OP.bypass,
         replica_groups=[[0, 1, 2, 3], [4, 5, 6, 7]],
         ins=[gin.opt()], outs=[gout.opt()],
         reads=["gin_k", "gin_v", "gin_n"], writes=["gout"], dma="dcc", cc=True)

    for s in range(16):
        gidx = 8 + s
        p1_load(gidx + 3)
        sl = gidx % 4
        for sub in range(2):
            dqc = s * 2 + sub
            for tb in range(2):
                b = next_bank()
                pres = f"pa{b}"
                po = ps_a[:, b * 512:(b + 1) * 512]
                for dc in range(DC):
                    r.op("tensor", "matmul", po,
                         wst4[sl][:, dc, sub * 128:(sub + 1) * 128],
                         x_fm[:, dc * TPC + tb * 512:dc * TPC + (tb + 1) * 512],
                         start=(dc == 0), stop=(dc == DC - 1),
                         reads=[f"wst{sl}", f"xfmt{tb * 4}", f"xfmt{tb * 4 + 1}",
                                f"xfmt{tb * 4 + 2}", f"xfmt{tb * 4 + 3}"],
                         writes=[pres], accum=(dc > 0))
                st = (dqc * 2 + tb) % 2
                r.op("vector", "tensor_scalar", qtg[st][:, :], po,
                     wqb_sb[:, dqc:dqc + 1], None, OP.add,
                     reads=[pres, "wqb_sb"], writes=[f"qtg{st}"])
                r.op("sync", "dma_start",
                     out=qsc[dqc * 128:(dqc + 1) * 128, tb * 512:(tb + 1) * 512],
                     in_=qtg[st][:, :], dma=f"dqst{st}", reads=[f"qtg{st}"],
                     writes=[f"qsc{dqc}"], accum=(tb > 0))
                r.op("scalar", "activation", ksq5[:, :], qtg[st][:, :], AF.Square,
                     reads=[f"qtg{st}"], writes=["ksq5"])
                r.op("tensor", "matmul", ps_d[0:1, 0:512], onescol[:, 0:1],
                     ksq5[:, :], start=True, stop=True,
                     reads=["ksq5", "onescol"], writes=["pd"])
                r.op("vector", "tensor_copy", nrow[0:1, :], ps_d[0:1, 0:512],
                     reads=["pd"], writes=["nrow"])
                r.op("vector", "tensor_reduce", nsc[0:1, 0:1], nrow[0:1, :], AX.X,
                     OP.max, reads=["nrow"], writes=["nsc"])
                if tb == 0:
                    r.op("vector", "tensor_copy", mqlr[0:1, dqc:dqc + 1],
                         nsc[0:1, 0:1], reads=["nsc"], writes=["mqlr"],
                         accum=(dqc > 0))
                else:
                    r.op("vector", "tensor_tensor", mqlr[0:1, dqc:dqc + 1],
                         mqlr[0:1, dqc:dqc + 1], nsc[0:1, 0:1], OP.max,
                         reads=["nsc", "mqlr"], writes=["mqlr"], accum=True)

    gor = gout.rearrange("(rr blk) t -> rr blk t", blk=2064)
    govr = gor[:, 1024:2048, :].rearrange("rr (b s16) t -> rr b s16 t", s16=16)
    govr = govr.rearrange("rr (h tc) s16 t -> rr h tc s16 t", tc=8)

    for i in range(2):
        r.op("gpsimd", "memset", vhb[i][:, :, 128:129], 1.0, writes=[f"vones{i}"])

    def load_kv(h, sl):
        both = [f"khb{sl}", f"vhb{sl}"]
        r.op("sync", "dma_start", out=khb[sl][:, :, :],
             in_=gor[:, h * 128:(h + 1) * 128, :].rearrange("rr p t -> p rr t"),
             dma=f"dkh{sl}", reads=["gout"], writes=both)
        for rr in range(NR):
            src_v = govr[rr, h, :, :, :].rearrange("tc s (t8 dv) -> (s t8) tc dv",
                                                   dv=128)
            r.op("sync", "dma_start",
                 out=vhb[sl][:, rr * 8:(rr + 1) * 8, 0:128],
                 in_=src_v, dma=f"dkh{sl}", reads=["gout"],
                 writes=both, accum=True)

    iters = [(h, f, qb) for h in range(H) for f in range(GF) for qb in range(2)]

    r.op("sync", "dma_start", out=nrows4[0:1, :].rearrange("a (rr e) -> a rr e", e=NE),
         in_=gout.rearrange("(rr blk) t -> rr blk t", blk=2064)[:, 2048:2049, 0:NE]
         .rearrange("rr a e -> a rr e"),
         dma="dnr4", reads=["gout"], writes=["nrows4"])
    r.op("vector", "tensor_copy", mk2f[0:1, :], nrows4[0:1, 0:NE],
         reads=["nrows4"], writes=["mk2f"])
    for rr in range(1, NR):
        r.op("vector", "tensor_tensor", mk2f[0:1, :], mk2f[0:1, :],
             nrows4[0:1, rr * NE:(rr + 1) * NE], OP.max,
             reads=["nrows4", "mk2f"], writes=["mk2f"], accum=True)
    for h in range(H):
        r.op("vector", "tensor_scalar", bqrow[0:1, h * GF:(h + 1) * GF],
             mqlr[0:1, h * GF:(h + 1) * GF], mk2f[0:1, h:h + 1], None, OP.add,
             reads=["mqlr", "mk2f"], writes=["bqrow"], accum=(h > 0))
    r.op("vector", "tensor_scalar", bqrow[0:1, :], bqrow[0:1, :],
         -0.5 * SCALE, None, OP.mult, reads=["bqrow"], writes=["bqrow"], accum=True)
    r.op("vector", "tensor_copy", bqrowb[0:1, :], bqrow[0:1, :],
         reads=["bqrow"], writes=["bqrowb"])
    r.op("tensor", "matmul", ps_d[:, 0:32], onesrow[0:1, :], bqrowb[0:1, :],
         start=True, stop=True, reads=["onesrow", "bqrowb"], writes=["pd"])
    r.op("vector", "tensor_copy", biasqs[:, :], ps_d[:, 0:32],
         reads=["pd"], writes=["biasqs"])

    def emit_f_prep(h, f):
        qh = h * GF + f
        qs = qh % 2
        r.op("sync", "dma_start", out=qsl[qs][:, :],
             in_=qsc[qh * 128:(qh + 1) * 128, :], dma=f"dql{qs}",
             reads=[f"qsc{qh}"], writes=[f"qsl{qs}"])

    def emit_scores_step(idx, j):
        h, f, qb = iters[idx]
        sl = h % 2
        qh = h * GF + f
        qs = qh % 2
        par = idx % 2
        khflat = khb[sl][:, :, :].rearrange("p rr t -> p (rr t)")
        for kc in (2 * j, 2 * j + 1):
            bk = kc % 4
            r.op("tensor", "matmul", ps_a[:, bk * 512:(bk + 1) * 512],
                 khflat[:, kc * 128:(kc + 1) * 128],
                 qsl[qs][:, qb * 512:(qb + 1) * 512],
                 start=True, stop=True,
                 reads=[f"khb{sl}", f"qsl{qs}"], writes=[f"pa{bk}"])
        g0 = 2 * j
        bk0 = g0 % 4
        r.op("scalar", "activation",
             pbuf[par][:, g0 * 512:(g0 + 2) * 512],
             ps_a[:, bk0 * 512:(bk0 + 2) * 512], AF.Exp,
             bias=biasqs[:, qh:qh + 1], scale=SCALE,
             reads=[f"pa{bk0}", f"pa{bk0 + 1}", "biasqs"],
             writes=[f"pb{par}_{j}"])

    pend_ep = []
    step_serial = [0]

    def flush_eps(force=False):
        while pend_ep and (force or pend_ep[0][0] <= step_serial[0]):
            pend_ep.pop(0)[1]()

    def emit_av_step(idx, j):
        h, f, qb = iters[idx]
        sl = h % 2
        qh = h * GF + f
        par = idx % 2
        for m in range(8 * j, 8 * j + 8):
            qt = m // DC
            kc = m % DC
            bb_ = qt % 2
            pres = f"pb{bb_}"
            r.op("tensor", "matmul", ps_b[:, bb_ * 512:bb_ * 512 + 129],
                 pbuf[par][:, kc * 512 + qt * 128:kc * 512 + (qt + 1) * 128],
                 vhb[sl][:, kc, 0:129],
                 start=(kc == 0), stop=(kc == DC - 1),
                 reads=[f"pb{par}_{kc // 2}", f"vhb{sl}", f"vones{sl}"],
                 writes=[pres], accum=(kc > 0))
            if kc == DC - 1:
                qoff = qh * TPC + qb * 512 + qt * 128

                def ep(pres=pres, bb_=bb_, qoff=qoff, qt=qt,
                       first=(qh + qb + qt == 0)):
                    r.op("vector", "reciprocal", recip[bb_][:, :],
                         ps_b[:, bb_ * 512 + 128:bb_ * 512 + 129],
                         reads=[pres], writes=[f"recip{bb_}"])
                    r.op("vector", "tensor_scalar", astg[bb_][:, :],
                         ps_b[:, bb_ * 512:bb_ * 512 + 128], recip[bb_][:, 0:1], None,
                         OP.mult, reads=[pres, f"recip{bb_}"], writes=[f"astg{bb_}"])
                    r.op("tensor", "transpose", ps_tr[:, 0:128],
                         astg[bb_][:, :], identb[:, :],
                         reads=[f"astg{bb_}", "identb"], writes=["ptr0"])
                    r.op("vector", "tensor_copy", a_fm[:, qoff:qoff + 128],
                         ps_tr[:, 0:128], reads=["ptr0"],
                         writes=["a_fm"], accum=(not first))
                pend_ep.append((step_serial[0] + 2, ep))

    def emit_interleaved(s_idx, a_idx):
        for j in range(16):
            step_serial[0] += 1
            flush_eps()
            if s_idx is not None:
                emit_scores_step(s_idx, j)
            if a_idx is not None:
                emit_av_step(a_idx, j)

    r.retouch("wst1")
    emit_stripe_load(wo, 0, sl=1)

    load_kv(0, 0)
    load_kv(1, 1)
    emit_f_prep(0, 0)
    emit_interleaved(0, None)
    for idx in range(len(iters)):
        nxt = idx + 1
        if nxt < len(iters):
            hn, fn, qbn = iters[nxt]
            if qbn == 0:
                emit_f_prep(hn, fn)
        emit_interleaved(nxt if nxt < len(iters) else None, idx)
        hh, ff, qq = iters[idx]
        if ff == GF - 1 and qq == 1 and hh + 2 < H:
            load_kv(hh + 2, (hh + 2) % 2)
    flush_eps(force=True)

    if "afm" in tapd:
        r.op("sync", "dma_start", out=tapd["afm"], in_=a_fm[:, :], dma="dtap",
             reads=["a_fm"])

    for _tb in range(TT):
        r.span(f"attnt{_tb}", 65536 + _tb * 8192, 65536 + (_tb + 1) * 8192)
        r.span(f"x2t{_tb}", 0, 65536)
        r.span(f"ffnt{_tb}", 65536 + _tb * 8192, 65536 + (_tb + 1) * 8192)

    for _tb in range(TT):
        r.span(f"asum{_tb}", r.spans["asum"][0] + _tb * 64,
               r.spans["asum"][0] + (_tb + 1) * 64)
        r.span(f"asq{_tb}", r.spans["asq"][0] + _tb * 64,
               r.spans["asq"][0] + (_tb + 1) * 64)
        r.span(f"fsum{_tb}", r.spans["asum"][0] + _tb * 64,
               r.spans["asum"][0] + (_tb + 1) * 64)
        r.span(f"fsq{_tb}", r.spans["asq"][0] + _tb * 64,
               r.spans["asq"][0] + (_tb + 1) * 64)

    def combine_stats(sum_res, sq_res, tb, p):
        r.op("vector", "tensor_reduce", exv[p][:, :], asum[:, tb, :], AX.X, OP.add,
             reads=[sum_res], writes=[f"exv{p}"])
        r.op("vector", "tensor_reduce", ex2[p][:, :], asq[:, tb, :], AX.X, OP.add,
             reads=[sq_res], writes=[f"ex2{p}"])
        r.op("vector", "tensor_scalar", bn2[p][:, 0:1], exv[p][:, :], 1.0 / D, None,
             OP.mult, reads=[f"exv{p}"], writes=[f"bn2{p}"])
        r.op("vector", "tensor_scalar", ex2[p][:, :], ex2[p][:, :], 1.0 / D, None,
             OP.mult, reads=[f"ex2{p}"], writes=[f"ex2{p}"], accum=True)
        r.op("vector", "tensor_tensor", exv[p][:, :], bn2[p][:, 0:1], bn2[p][:, 0:1],
             OP.mult, reads=[f"bn2{p}"], writes=[f"exv{p}"])
        finish_stats(p)

    def finish_stats(p):
        r.op("vector", "tensor_tensor", sdv[p][:, :], ex2[p][:, :], exv[p][:, :],
             OP.subtract, reads=[f"ex2{p}", f"exv{p}"], writes=[f"sdv{p}"])
        r.op("vector", "tensor_scalar", sdv[p][:, :], sdv[p][:, :], EPS, None,
             OP.add, reads=[f"sdv{p}"], writes=[f"sdv{p}"], accum=True)
        r.op("scalar", "activation", sdv[p][:, :], sdv[p][:, :], AF.Sqrt,
             reads=[f"sdv{p}"], writes=[f"sdv{p}"])
        r.op("vector", "reciprocal", rstd[p][:, :], sdv[p][:, :],
             reads=[f"sdv{p}"], writes=[f"rstd{p}"])

    r.op("gpsimd", "dma_start", out=wobrow[0:1, :],
         in_=wob.rearrange("(a k) -> a k", a=1), dma="dwob", writes=["wobrow"])
    r.retouch("wst0")
    for s in range(16):
        if s + 1 < 16:
            emit_stripe_load(wo, s + 1, sl=s % 2)
        sl = (s + 1) % 2
        if s == 13 or s == 14:
            t0 = s - 13
            r.op("gpsimd", "dma_start", out=x5sl[t0][:, :],
                 in_=xd[t0 * 128:(t0 + 1) * 128, :], dma=f"dx5{t0}",
                 writes=[f"x5sl{t0}"])
        for tb in range(TT):
            b = next_bank()
            pres = f"pa{b}"
            po = ps_a[:, b * 512:b * 512 + 256]
            for dc in range(DC):
                r.op("tensor", "matmul", po,
                     a_fm[:, dc * TPC + tb * 128:dc * TPC + (tb + 1) * 128],
                     wst[sl][:, dc, :], start=(dc == 0), stop=False,
                     reads=[f"wst{sl}", "a_fm"], writes=[pres], accum=(dc > 0))
            r.op("tensor", "matmul", po, onesrow[0:1, :],
                 wobrow[0:1, s * 256:(s + 1) * 256], start=False, stop=True,
                 reads=["onesrow", "wobrow"], writes=[pres], accum=True)
            r.op("scalar", "activation",
                 attn_sb[:, tb * D + s * 256:tb * D + (s + 1) * 256], po, AF.Copy,
                 accum_out=asum[:, tb, s:s + 1],
                 reads=[pres], writes=[f"attnt{tb}", f"asum{tb}"], accum=(s > 0))
            r.op("scalar", "activation", sqscrap[:, :], po, AF.Square,
                 accum_out=asq[:, tb, s:s + 1],
                 reads=[pres], writes=["sqscrap", f"asq{tb}"])
            if s == 15:
                row = attn_sb[:, tb * D:(tb + 1) * D]
                combine_stats(f"asum{tb}", f"asq{tb}", tb, tb % 2)
                r.op("vector", "tensor_scalar", row, row, bn2[tb % 2][:, 0:1],
                     rstd[tb % 2][:, 0:1], OP.subtract, OP.mult,
                     reads=[f"attnt{tb}", f"bn2{tb % 2}", f"rstd{tb % 2}"],
                     writes=[f"attnt{tb}"], accum=True)

    def emit_router(tb):
        for dc in range(DC):
            r.op("tensor", "matmul", ps_b[:, 0:128],
                 x2_fm[:, dc * TPC + tb * 128:dc * TPC + (tb + 1) * 128],
                 atsb[:, dc, :], start=(dc == 0), stop=False,
                 reads=[f"x2t{tb}", "atsb"], writes=["pb0"], accum=(dc > 0))
            r.op("tensor", "matmul", ps_b[:, 128:136],
                 x2_fm[:, dc * TPC + tb * 128:dc * TPC + (tb + 1) * 128],
                 wrt[:, dc * 8:(dc + 1) * 8], start=False, stop=(dc == DC - 1),
                 reads=[f"x2t{tb}", "wrt"], writes=["pb0"], accum=True)
        r.op("scalar", "activation", hsb[:, :], ps_b[:, 0:136], AF.Copy,
             reads=["pb0"], writes=["hsb"])
        lg = hsb[:, 128:136]
        r.op("vector", "tensor_reduce", tm1[:, :], lg, AX.X, OP.max,
             reads=["hsb"], writes=["tm1"])
        r.op("vector", "tensor_scalar", teq1[:, :], lg, tm1[:, 0:1], None,
             OP.is_equal, reads=["hsb", "tm1"], writes=["teq1"])
        r.op("vector", "scalar_tensor_tensor", tlm[:, :], teq1[:, :], -1e30, lg,
             OP.mult, OP.add, reads=["teq1", "hsb"], writes=["tlm"])
        r.op("vector", "tensor_reduce", tm2[:, :], tlm[:, :], AX.X, OP.max,
             reads=["tlm"], writes=["tm2"])
        r.op("vector", "tensor_scalar", teq2[:, :], tlm[:, :], tm2[:, 0:1], None,
             OP.is_equal, reads=["tlm", "tm2"], writes=["teq2"])
        r.op("vector", "tensor_scalar", tm2[:, :], tm2[:, :], tm1[:, 0:1], None,
             OP.subtract, reads=["tm2", "tm1"], writes=["tm2"])
        r.op("scalar", "activation", trw2[:, :], tm2[:, :], AF.Sigmoid,
             reads=["tm2"], writes=["trw2"])
        r.op("vector", "tensor_scalar", trw1[:, :], trw2[:, :], -1.0, 1.0,
             OP.mult, OP.add, reads=["trw2"], writes=["trw1"])
        r.op("vector", "tensor_scalar", tt1[:, :], teq1[:, :], trw1[:, 0:1], None,
             OP.mult, reads=["teq1", "trw1"], writes=["tt1"])
        r.op("vector", "scalar_tensor_tensor", twts[:, :], teq2[:, :], trw2[:, 0:1],
             tt1[:, :], OP.mult, OP.add, reads=["teq2", "trw2", "tt1"],
             writes=["twts"])
        for e in range(NE):
            r.op("vector", "tensor_scalar", gtm[:, e * RK:(e + 1) * RK],
                 hsb[:, e * RK:(e + 1) * RK], twts[:, e:e + 1], LSCALE,
                 OP.mult, OP.mult, reads=["hsb", "twts"], writes=["gtm"],
                 accum=(e > 0))
        r.op("tensor", "transpose", ps_tr[:, 0:128], gtm[:, :], identb[:, :],
             reads=["gtm", "identb"], writes=["ptr0"])
        r.op("vector", "tensor_copy", gfm[:, tb * 128:(tb + 1) * 128],
             ps_tr[:, 0:128], reads=["ptr0"],
             writes=["gfm"], accum=(tb > 0))

    def emit_bias_slices(s):
        sl = s % 2
        r.op("gpsimd", "dma_start", out=ball_sl[sl][:, :],
             in_=br_[:, s * 256:(s + 1) * 256], dma=f"dbal{sl}",
             writes=[f"ball{sl}"])
        r.op("gpsimd", "dma_start", out=bb_sl[sl][0:1, :],
             in_=bb.rearrange("(a k) -> a k", a=1)[:, s * 256:(s + 1) * 256],
             dma=f"dbal{sl}", writes=[f"bbs{sl}"])

    r.op("gpsimd", "dma_start", out=atsb[:, :, :],
         in_=at_.rearrange("(dc p) e -> p dc e", p=128), dma="dat", writes=["atsb"])
    r.retouch("wst0")
    r.retouch("wst1")
    emit_stripe_load(wbt, 0)
    emit_bias_slices(0)

    for tb in range(TT):
        p = tb % 2
        row = attn_sb[:, tb * D:(tb + 1) * D]
        if tb >= 2:
            r.op("gpsimd", "dma_start", out=x5sl[p][:, :],
                 in_=xd[tb * 128:(tb + 1) * 128, :], dma=f"dx5{p}",
                 writes=[f"x5sl{p}"])
        r.op("vector", "tensor_tensor", x5sl[p][:, :], x5sl[p][:, :], row,
             OP.add, reads=[f"x5sl{p}", f"attnt{tb}"], writes=[f"x5sl{p}"],
             accum=True)
        r.op("sync", "dma_start", out=x5sc[tb * 128:(tb + 1) * 128, :],
             in_=x5sl[p][:, :], dma=f"dxs{p}", reads=[f"x5sl{p}"],
             writes=[f"x5sc{tb}"])
        r.op("vector", "tensor_reduce", exv[p][:, :], x5sl[p][:, :], AX.X,
             OP.add, reads=[f"x5sl{p}"], writes=[f"exv{p}"])
        r.op("scalar", "activation", row, x5sl[p][:, :], AF.Square,
             reads=[f"x5sl{p}"], writes=[f"attnt{tb}"])
        r.op("vector", "tensor_reduce", ex2[p][:, :], row, AX.X, OP.add,
             reads=[f"attnt{tb}"], writes=[f"ex2{p}"])
        r.op("vector", "tensor_scalar", bn2[p][:, 0:1], exv[p][:, :],
             1.0 / D, None, OP.mult, reads=[f"exv{p}"], writes=[f"bn2{p}"])
        r.op("vector", "tensor_scalar", ex2[p][:, :], ex2[p][:, :],
             1.0 / D, None, OP.mult, reads=[f"ex2{p}"], writes=[f"ex2{p}"],
             accum=True)
        r.op("vector", "tensor_tensor", exv[p][:, :], bn2[p][:, 0:1],
             bn2[p][:, 0:1], OP.mult, reads=[f"bn2{p}"], writes=[f"exv{p}"])
        finish_stats(p)
        r.op("vector", "tensor_scalar", ln2buf[:, :], x5sl[p][:, :],
             bn2[p][:, 0:1], rstd[p][:, 0:1], OP.subtract, OP.mult,
             reads=[f"x5sl{p}", f"bn2{p}", f"rstd{p}"], writes=["ln2buf"])
        emit_fm_transposes(ln2buf[:, :], "ln2buf", x2_fm, f"x2t{tb}", tb)
        emit_router(tb)

    if "attn" in tapd:
        r.op("sync", "dma_start", out=tapd["attn"], in_=attn_sb[:, :], dma="dtap",
             reads=[f"attnt{t}" for t in range(TT)])
    if "x2fm" in tapd:
        r.op("sync", "dma_start", out=tapd["x2fm"], in_=x2_fm[:, :], dma="dtap",
             reads=[f"x2t{t}" for t in range(TT)])

    for s in range(16):
        if s + 1 < 16:
            emit_stripe_load(wbt, s + 1)
            emit_bias_slices(s + 1)
        sl = s % 2
        if s == 13 or s == 14:
            t0 = s - 13
            r.op("gpsimd", "dma_start", out=x5sl[t0][:, :],
                 in_=x5sc[t0 * 128:(t0 + 1) * 128, :], dma="dx5r",
                 reads=[f"x5sc{t0}"], writes=[f"x5sl{t0}"])
        for tb in range(TT):
            b = next_bank()
            pres = f"pa{b}"
            po = ps_a[:, b * 512:b * 512 + 256]
            for dc in range(DC):
                r.op("tensor", "matmul", po,
                     x2_fm[:, dc * TPC + tb * 128:dc * TPC + (tb + 1) * 128],
                     wst[sl][:, dc, :], start=(dc == 0), stop=False,
                     reads=[f"wst{sl}", f"x2t{tb}"], writes=[pres], accum=(dc > 0))
            r.op("tensor", "matmul", po, gfm[:, tb * 128:(tb + 1) * 128],
                 ball_sl[sl][:, :], start=False, stop=False,
                 reads=["gfm", f"ball{sl}"], writes=[pres], accum=True)
            r.op("tensor", "matmul", po, onesrow[0:1, :],
                 bb_sl[sl][0:1, :], start=False, stop=True,
                 reads=["onesrow", f"bbs{sl}"], writes=[pres], accum=True)
            r.op("scalar", "activation",
                 ffn_sb[:, tb * D + s * 256:tb * D + (s + 1) * 256], po, AF.Copy,
                 accum_out=asum[:, tb, s:s + 1],
                 reads=[pres], writes=[f"ffnt{tb}", f"fsum{tb}"], accum=(s > 0))
            r.op("scalar", "activation", sqscrap[:, :], po, AF.Square,
                 accum_out=asq[:, tb, s:s + 1],
                 reads=[pres], writes=["sqscrap", f"fsq{tb}"])

    for tb in range(TT):
        p = tb % 2
        row = ffn_sb[:, tb * D:(tb + 1) * D]
        combine_stats(f"fsum{tb}", f"fsq{tb}", tb, p)
        r.op("vector", "tensor_scalar", row, row, bn2[p][:, 0:1],
             rstd[p][:, 0:1], OP.subtract, OP.mult,
             reads=[f"ffnt{tb}", f"bn2{p}", f"rstd{p}"],
             writes=[f"ffnt{tb}"], accum=True)
        for q4 in range(4):
            qp = (tb * 4 + q4) % 2
            r.op("vector", "tensor_tensor", xq[qp][:, :],
                 x5sl[p][:, q4 * 1024:(q4 + 1) * 1024],
                 row[:, q4 * 1024:(q4 + 1) * 1024], OP.add,
                 reads=[f"x5sl{p}", f"ffnt{tb}"], writes=[f"xq{qp}"])
            r.op("sync", "dma_start",
                 out=outd[tb * 128:(tb + 1) * 128, q4 * 1024:(q4 + 1) * 1024],
                 in_=xq[qp][:, :], dma=f"dout{qp}", reads=[f"xq{qp}"],
                 writes=["outd"], accum=(tb + q4 > 0))
        if tb + 2 < TT:
            r.op("gpsimd", "dma_start", out=x5sl[p][:, :],
                 in_=x5sc[(tb + 2) * 128:(tb + 3) * 128, :], dma="dx5r",
                 reads=[f"x5sc{tb + 2}"], writes=[f"x5sl{p}"])
    if "ffn" in tapd:
        r.op("sync", "dma_start", out=tapd["ffn"], in_=ffn_sb[:, :], dma="dtap",
             reads=[f"ffnt{t}" for t in range(TT)])

    r.final_wait("dout0")
    r.final_wait("dout1")
    if tapd:
        r.final_wait("dtap")

    r.materialize()

    with contextlib.ExitStack() as st2:
        esems = {e: st2.enter_context(nc.semaphore(f"se_{e}")) for e in ENG}
        dsems = {n: st2.enter_context(nc.semaphore(f"sd_{n}"))
                 for n in sorted(r.dma_sems)}
        block = st2.enter_context(nc.Block())

        @block.tensor
        def _(t):
            r.replay(t, "tensor", esems, dsems)

        @block.vector
        def _(v):
            r.replay(v, "vector", esems, dsems)

        @block.scalar
        def _(sc):
            r.replay(sc, "scalar", esems, dsems)

        @block.sync
        def _(sy):
            r.replay(sy, "sync", esems, dsems)

        @block.gpsimd
        def _(gp):
            r.replay(gp, "gpsimd", esems, dsems)

    stack.close()
    return nc


def kernel(L0_attn_x, L0_attn_wq_w, L0_attn_wq_b, L0_attn_wk_w, L0_attn_wk_b,
           L0_attn_wv_w, L0_attn_wv_b, L0_attn_wo_w, L0_attn_wo_b,
           L0_ffn_W_base, L0_ffn_b_base, L0_ffn_W_router, L0_ffn_A, L0_ffn_B):
    xf = np.ascontiguousarray(
        np.asarray(L0_attn_x, dtype=np.float32).reshape(NCORES * TPC, D))
    base = {
        "wq": np.ascontiguousarray(L0_attn_wq_w, dtype=np.float32),
        "wk": np.ascontiguousarray(L0_attn_wk_w, dtype=np.float32),
        "wv": np.ascontiguousarray(L0_attn_wv_w, dtype=np.float32),
        "wo": np.ascontiguousarray(L0_attn_wo_w, dtype=np.float32),
        "wqb": np.ascontiguousarray(L0_attn_wq_b, dtype=np.float32),
        "wkb": np.ascontiguousarray(L0_attn_wk_b, dtype=np.float32),
        "wvb": np.ascontiguousarray(L0_attn_wv_b, dtype=np.float32),
        "wob": np.ascontiguousarray(L0_attn_wo_b, dtype=np.float32),
        "wbt": np.ascontiguousarray(np.asarray(L0_ffn_W_base, dtype=np.float32).T),
        "bb": np.ascontiguousarray(L0_ffn_b_base, dtype=np.float32),
        "wr": np.ascontiguousarray(L0_ffn_W_router, dtype=np.float32),
        "at": np.ascontiguousarray(
            np.asarray(L0_ffn_A, dtype=np.float32).reshape(NE * RK, D).T),
        "br": np.ascontiguousarray(
            np.asarray(L0_ffn_B, dtype=np.float32).transpose(0, 2, 1).reshape(NE * RK, D)),
    }
    in_maps = []
    for c in range(NCORES):
        m = dict(base)
        m["x"] = np.ascontiguousarray(xf[c * TPC:(c + 1) * TPC])
        in_maps.append(m)
    nc = build_nc(taps=_DEBUG_TAPS)
    res = run_bass_kernel_spmd(nc, in_maps, core_ids=list(range(NCORES)))
    out = np.concatenate([res.results[c]["out"] for c in range(NCORES)], axis=0)
    return np.ascontiguousarray(out.reshape(2, SL, D), dtype=np.float32)
